# version 03
# baseline (speedup 1.0000x reference)
import math
from contextlib import ExitStack

import numpy as np
import concourse.bass as bass
import concourse.mybir as mybir
from concourse.bass_utils import run_bass_kernel_spmd

F32 = mybir.dt.float32
BF16 = mybir.dt.bfloat16
U8 = mybir.dt.uint8
U32 = mybir.dt.uint32
AF = mybir.ActivationFunctionType
ALU = mybir.AluOpType
AX = mybir.AxisListType

T = 4096
D = 1024
NDC = D // 128
TC = 512
NTC = T // TC

ENGINES = ("sync", "scalar", "vector", "tensor", "gpsimd")


class Prog:

    def __init__(self, name="prog"):
        self.nc = bass.Bass("TRN2", target_bir_lowering=False)
        self.ctx = ExitStack()
        self.plan = {e: [] for e in ENGINES}
        self.sems = {}
        self.cnt = {}
        self.seen = {e: {} for e in ENGINES}
        for s in ("epe", "eact", "edve", "epool", "esp"):
            self.sems[s] = self.ctx.enter_context(self.nc.semaphore(s))
            self.cnt[s] = 0
        for i in range(8):
            s = "pb%d" % i
            self.sems[s] = self.ctx.enter_context(self.nc.semaphore(s))
            self.cnt[s] = 0
        self._bank_free = list(range(8))
        self._bank_of = {}
        self.engine_sem = {"sync": "esp", "scalar": "eact", "vector": "edve",
                          "tensor": "epe", "gpsimd": "epool"}
        self.NSLOT = 8
        self.queue_prefix = {"sync": "qs", "scalar": "qa",
                            "gpsimd": "qg", "vector": "qv"}
        self.qidx = {}
        for q, pfx in self.queue_prefix.items():
            self.qidx[q] = 0
            for i in range(self.NSLOT):
                s = pfx + str(i)
                self.sems[s] = self.ctx.enter_context(self.nc.semaphore(s))
                self.cnt[s] = 0
        self._const_cache = {}
        self.extra_inputs = {}
        self.const_stack = ExitStack()
        self.last_write = {}
        self.readers = {}
        self._pe_groups = {}

    def dram(self, name, shape, dtype, kind=None, **kw):
        if kind is None:
            return self.nc.dram_tensor(name, shape, dtype, **kw)
        return self.nc.dram_tensor(name, shape, dtype, kind=kind, **kw)

    def sbuf(self, stack, name, shape, dtype):
        return stack.enter_context(self.nc.sbuf_tensor(name, shape, dtype))

    def psum(self, stack, name, shape, dtype=F32):
        per_part = int(np.prod(shape[1:])) * mybir.dt.size(dtype)
        assert per_part <= 2048, (name, shape)
        h = stack.enter_context(self.nc.psum_tensor(name, shape, dtype))
        bank = self._bank_free.pop(0)
        self._bank_of[name] = bank
        stack.callback(lambda b=bank, n=name: (
            self._bank_free.insert(0, b), self._bank_of.pop(n, None)))
        return h

    def const_input(self, name, arr):
        if name not in self._const_cache:
            if arr.dtype == np.float32:
                dt = F32
            elif arr.dtype == np.uint8:
                dt = U8
            elif arr.dtype == np.uint32:
                dt = U32
            else:
                dt = BF16
            h = self.nc.dram_tensor(name, list(arr.shape), dt,
                                    kind="ExternalInput")
            self.extra_inputs[name] = arr
            self._const_cache[name] = h.ap()
        return self._const_cache[name]

    def sbuf_const(self, name, arr):
        key = "sc_" + name
        if key in self._const_cache:
            return self._const_cache[key]
        ap = self.const_input("ci_" + name, arr)
        dt = {np.dtype(np.float32): F32, np.dtype(np.uint8): U8,
              np.dtype(np.uint32): U32}.get(arr.dtype, BF16)
        sb = self.const_stack.enter_context(
            self.nc.sbuf_tensor("sb_" + name, list(arr.shape), dt,
                                side="right"))
        tok = self.dma("sync", sb[:, :], ap[:, :])
        self._const_cache[key] = (sb, tok)
        return sb, tok

    def _wait(self, e, engine, token):
        if token is None:
            return
        sem, val = token
        if self.seen[engine].get(sem, -1) >= val:
            return
        self.seen[engine][sem] = val
        e.wait_ge(self.sems[sem], val)

    @staticmethod
    def _ap_name(x):
        t = getattr(x, "tensor", None)
        return getattr(t, "name", None) if t is not None else None

    def _track(self, engine, outs, ins, extra_waits):
        waits = list(extra_waits)
        for x in ins:
            nm = self._ap_name(x)
            if nm is None:
                continue
            lw = self.last_write.get(nm)
            if lw and not (engine == "tensor" and lw[0] == "tensor"):
                waits.append(lw[1])
        for x in outs:
            nm = self._ap_name(x)
            if nm is None:
                continue
            lw = self.last_write.get(nm)
            if lw and not (engine == "tensor" and lw[0] == "tensor"):
                waits.append(lw[1])
            for (re, rt) in self.readers.get(nm, ()):
                if not (engine == "tensor" and re == "tensor"):
                    waits.append(rt)
        return waits

    def _register(self, engine, outs, ins, token):
        for x in ins:
            nm = self._ap_name(x)
            if nm is not None:
                self.readers.setdefault(nm, []).append((engine, token))
        for x in outs:
            nm = self._ap_name(x)
            if nm is not None:
                self.last_write[nm] = (engine, token)
                self.readers[nm] = []

    def op(self, engine, method, *a, waits=(), inc=True, **kw):
        outs = [a[0]] if a else []
        if kw.get("accum_out") is not None:
            outs.append(kw["accum_out"])
        ins = [x for x in a[1:] if self._ap_name(x) is not None]
        ins += [v for k, v in kw.items()
                if k != "accum_out" and self._ap_name(v) is not None]
        all_waits = self._track(engine, outs, ins, waits)
        if engine == "tensor" and method in ("matmul", "transpose"):
            pname = self._ap_name(a[0])
            s = "pb%d" % self._bank_of[pname]
            self.cnt[s] += 1
            token = [s, self.cnt[s]]
            if method == "matmul":
                grp = self._pe_groups.setdefault(pname, [])
                grp.append(token)
                if kw.get("stop", True):
                    for t in grp:
                        t[1] = self.cnt[s]
                    self._pe_groups[pname] = []
        else:
            s = self.engine_sem[engine]
            self.cnt[s] += 1
            token = [s, self.cnt[s]]
        self._register(engine, outs, ins, token)

        def fn(e, waits=tuple(all_waits), token=token, method=method,
               a=a, kw=kw, s=s):
            for w in waits:
                self._wait(e, engine, w)
            inst = getattr(e, method)(*a, **kw)
            inst.then_inc(self.sems[s], 1)
        self.plan[engine].append(fn)
        return token

    def dma(self, queue, out, in_, waits=(), **kw):
        s = self.queue_prefix[queue] + str(self.qidx[queue] % self.NSLOT)
        self.qidx[queue] += 1
        gate = [s, self.cnt[s]] if self.cnt[s] else None
        self.cnt[s] += 16
        token = [s, self.cnt[s]]
        all_waits = self._track(queue, [out], [in_], waits)
        if gate:
            all_waits.append(gate)
        self._register(queue, [out], [in_], token)

        def fn(e, waits=tuple(all_waits), out=out, in_=in_, kw=kw,
               token=token):
            for w in waits:
                self._wait(e, queue, w)
            e.dma_start(out=out, in_=in_, **kw).then_inc(
                self.sems[token[0]], 16)
        self.plan[queue].append(fn)
        return token

    def wait_only(self, engine, tokens):
        tokens = tuple(t for t in tokens if t is not None)

        def fn(e, tokens=tokens):
            for w in tokens:
                self._wait(e, engine, w)
        self.plan[engine].append(fn)

    def barrier(self):
        toks = []
        for eng in ENGINES:
            s = self.engine_sem[eng]
            if self.cnt[s]:
                toks.append([s, self.cnt[s]])
        for i in range(8):
            s = "pb%d" % i
            if self.cnt[s]:
                toks.append([s, self.cnt[s]])
        for q, pfx in self.queue_prefix.items():
            for i in range(self.NSLOT):
                s = pfx + str(i)
                if self.cnt[s]:
                    toks.append([s, self.cnt[s]])
        for eng in ENGINES:
            self.wait_only(eng, toks)

    def assemble(self):
        for k, v in self._pe_groups.items():
            assert not v, ("unclosed matmul accumulation group", k)
        self.seen = {e: {} for e in ENGINES}
        with self.nc.Block() as block:
            for eng in ENGINES:
                fns = self.plan[eng]
                if not fns:
                    continue
                getattr(block, eng)(_EngineBody(fns))
        self.const_stack.close()
        self.ctx.close()
        return self.nc


class _EngineBody:
    def __init__(self, fns):
        self.fns = fns

    def __call__(self, e):
        for fn in self.fns:
            fn(e)


def mm_acc(P, psum_ap, pairs, waits=()):
    n = len(pairs)
    tok = None
    for k, (lt, rh) in enumerate(pairs):
        tok = P.op("tensor", "matmul", psum_ap, lt, rh,
                   start=(k == 0), stop=(k == n - 1),
                   waits=waits if k == 0 else (), inc=(k == n - 1))
    return tok



def emit_load_xb(P, stack, x_hbm, name="xb"):
    XB = P.sbuf(stack, name, [128, NDC * T], BF16)
    toks = []
    for dc in range(NDC):
        toks.append(P.dma("gpsimd", XB[:, dc * T:(dc + 1) * T],
                          x_hbm[dc * 128:(dc + 1) * 128, :]))
    return XB, toks


def emit_layernorm_inplace(P, stack, XB, xb_toks, ones, eps=1e-5,
                           tag="ln", stats_stack=None):
    sst = stats_stack if stats_stack is not None else stack
    rstd = P.sbuf(sst, tag + "_rstd", [128, NTC * TC], F32)
    mr = P.sbuf(sst, tag + "_mr", [128, NTC * TC], F32)
    sum_ps = P.psum(stack, tag + "_sum", [128, TC])
    sq_ps = P.psum(stack, tag + "_sq", [128, TC])
    xsq = [P.sbuf(stack, tag + "_xsq%d" % i, [128, TC], BF16)
           for i in range(NDC)]
    mean = P.sbuf(stack, tag + "_mean", [128, TC], F32)
    ex2 = P.sbuf(stack, tag + "_ex2", [128, TC], F32)
    msq = P.sbuf(stack, tag + "_msq", [128, TC], F32)
    var = P.sbuf(stack, tag + "_var", [128, TC], F32)
    scr = P.sbuf(stack, tag + "_scr", [128, TC], F32)

    stat_toks = []
    pe_sq_toks = {}
    prev_read = []
    for tc in range(NTC):
        t0 = tc * TC
        for dc in range(NDC):
            i = (tc * NDC + dc) % NDC
            xtile = XB[:, dc * T + t0: dc * T + t0 + TC]
            war = pe_sq_toks.get(tc * NDC + dc - NDC)
            sq_tok = P.op("scalar", "activation", xsq[i][:, :], xtile,
                          func=AF.Square,
                          waits=[xb_toks[dc]] + ([war] if war else []),
                          inc=True)
            P.op("tensor", "matmul", sum_ps[:, :], ones[0][:, :], xtile,
                 start=(dc == 0), stop=(dc == NDC - 1),
                 waits=[xb_toks[dc], ones[1]] + (prev_read if dc == 0 else []))
            pe_sq_toks[tc * NDC + dc] = P.op(
                "tensor", "matmul", sq_ps[:, :], ones[0][:, :], xsq[i][:, :],
                start=(dc == 0), stop=(dc == NDC - 1),
                waits=[sq_tok], inc=True)
        mm_tok = ["epe", P.cnt["epe"]]
        t_m = P.op("vector", "tensor_scalar_mul", mean[:, :], sum_ps[:, :],
                   1.0 / D, waits=[mm_tok], inc=True)
        t_e = P.op("vector", "tensor_scalar", ex2[:, :], sq_ps[:, :],
                   1.0 / D, float(eps), op0=ALU.mult, op1=ALU.add, inc=True)
        t_s = P.op("scalar", "activation", msq[:, :], mean[:, :],
                   func=AF.Square, waits=[t_m], inc=True)
        t_v = P.op("vector", "tensor_sub", var[:, :], ex2[:, :], msq[:, :],
                   waits=[t_e, t_s], inc=True)
        t_l = P.op("scalar", "activation", scr[:, :], var[:, :],
                   func=AF.Ln, waits=[t_v], inc=True)
        t_r = P.op("scalar", "activation", rstd[:, t0:t0 + TC], scr[:, :],
                   func=AF.Exp, scale=-0.5, inc=True)
        t_mr = P.op("vector", "tensor_mul", mr[:, t0:t0 + TC], mean[:, :],
                    rstd[:, t0:t0 + TC], waits=[t_r], inc=True)
        stat_toks.append((t_r, t_mr))
        prev_read = [t_m, t_e]

    apply_toks = {}
    scr2 = P.sbuf(stack, tag + "_scr2", [128, TC], F32)
    for dc in range(NDC):
        for tc in range(NTC):
            t0 = tc * TC
            xtile = XB[:, dc * T + t0: dc * T + t0 + TC]
            P.op("vector", "tensor_mul", scr2[:, :], xtile,
                 rstd[:, t0:t0 + TC], waits=[stat_toks[tc][0]])
            tok = P.op("vector", "tensor_sub", xtile, scr2[:, :],
                       mr[:, t0:t0 + TC], waits=[stat_toks[tc][1]], inc=True)
            apply_toks[(dc, tc)] = tok
    return apply_toks, rstd, mr



def _bf16np():
    import ml_dtypes
    return ml_dtypes.bfloat16


def emit_scalar_replicate(P, stack, tag, src_ap_11, waits, scale=1.0):
    ones1, t_o = P.sbuf_const("ones1x128_f32",
                              np.ones((1, 128), dtype=np.float32))
    ps = P.psum(stack, tag + "_ps", [128, 2])
    t1 = P.op("tensor", "matmul", ps[:, 0:1], ones1[:, :], src_ap_11,
              start=True, stop=True, waits=list(waits) + [t_o], inc=True)
    S = P.sbuf(stack, tag + "_S", [128, 1], F32)
    if scale != 1.0:
        t2 = P.op("vector", "tensor_scalar_mul", S[:, :], ps[:, 0:1],
                  float(scale), waits=[t1], inc=True)
    else:
        t2 = P.op("vector", "tensor_copy", S[:, :], ps[:, 0:1],
                  waits=[t1], inc=True)
    return S, t2


class StreamedMatmul:

    def __init__(self, P, stack, tag, Wp, n_kc, n_mc, PSbanks, pool):
        self.P, self.tag, self.Wp = P, tag, Wp
        self.n_kc, self.n_mc = n_kc, n_mc
        self.PS = PSbanks
        self.nmb = len(PSbanks)
        assert len(pool) >= n_kc
        self.nbuf = len(pool)
        self.wsb = pool
        self.dma_guard = {}
        self.ps_guard = {}

    def run(self, rhs_tile, rhs_tok, evac, tcw=TC, prep_tok=None):
        P = self.P
        out_toks = []
        n_mb = (self.n_mc + self.nmb - 1) // self.nmb
        step = 0
        for mb in range(n_mb):
            mcs = [mb * self.nmb + j for j in range(self.nmb)
                   if mb * self.nmb + j < self.n_mc]
            for kc in range(self.n_kc):
                i = step % self.nbuf
                war = self.dma_guard.get(step - self.nbuf)
                wts = [war] if war else []
                if prep_tok is not None:
                    wts.append(prep_tok)
                tin = P.dma("sync", self.wsb[i][:, 0:len(mcs) * 128],
                            self.Wp[kc * 128:(kc + 1) * 128,
                                    mcs[0] * 128:(mcs[-1] + 1) * 128],
                            waits=wts)
                last_mm = None
                for j, mc in enumerate(mcs):
                    wts2 = [tin, rhs_tok(kc)]
                    if kc == 0:
                        g = self.ps_guard.get(j)
                        if g:
                            wts2.append(g)
                    last_mm = P.op(
                        "tensor", "matmul", self.PS[j][:, 0:tcw],
                        self.wsb[i][:, j * 128:(j + 1) * 128],
                        rhs_tile(kc),
                        start=(kc == 0), stop=(kc == self.n_kc - 1),
                        waits=wts2, inc=True)
                self.dma_guard[step] = last_mm
                step += 1
            for j, mc in enumerate(mcs):
                tok = evac(mc, self.PS[j][:, 0:tcw],
                           self.dma_guard[step - 1])
                self.ps_guard[j] = tok
                out_toks.append(tok)
        return out_toks



def emit_l2_ffn(P, bs, x_in, x_out, w):
    ones_bf = P.sbuf_const("ones128_bf",
                           np.ones((128, 128), dtype=np.float32)
                           .astype(_bf16np()))

    XB, xb_toks = emit_load_xb(P, bs, x_in, name="l2f_xb")
    lnstack = ExitStack()
    emit_layernorm_inplace(P, lnstack, XB, xb_toks, ones_bf,
                           eps=1e-5, tag="l2f_ln")
    P.barrier()
    lnstack.close()

    st = ExitStack()
    sc = P.sbuf(st, "l2f_sc", [1, 4], F32)
    s_toks = [P.dma("sync", sc[0:1, i:i + 1], w["s%d" % i][0:1, 0:1])
              for i in range(3)]
    P.op("vector", "tensor_mul", sc[0:1, 3:4], sc[0:1, 0:1], sc[0:1, 1:2],
         waits=s_toks)
    t_sc = P.op("vector", "tensor_mul", sc[0:1, 3:4], sc[0:1, 3:4],
                sc[0:1, 2:3], inc=True)
    S_pp, t_S = emit_scalar_replicate(P, st, "l2f_S", sc[0:1, 3:4],
                                      [t_sc], scale=8.0)

    W0p = P.dram("l2f_W0p", [1024, 4096], BF16)
    W1p = P.dram("l2f_W1p", [4096, 4096], BF16)
    W2p = P.dram("l2f_W2p", [4096, 1024], BF16)
    prepstack = ExitStack()
    ku8 = [P.sbuf(prepstack, "l2f_ku8_%d" % i, [128, 4096], U8)
           for i in range(2)]
    kbf = [P.sbuf(prepstack, "l2f_kbf_%d" % i, [128, 4096], BF16)
           for i in range(3)]
    prep0 = []
    prev_out = None
    for kc in range(8):
        wts = [prev_out] if prev_out else []
        tk_t = P.dma("sync", ku8[0][:, :], w["k0"][kc * 128:(kc + 1) * 128, :],
                     waits=wts)
        tk_b = P.dma("sync", ku8[1][:, :],
                     w["k0"][1024 + kc * 128:1024 + (kc + 1) * 128, :],
                     waits=wts)
        P.op("vector", "tensor_copy", kbf[0][:, :], ku8[0][:, :],
             waits=[tk_t])
        P.op("vector", "tensor_copy", kbf[1][:, :], ku8[1][:, :],
             waits=[tk_b])
        tsub = P.op("vector", "tensor_sub", kbf[2][:, :],
                    kbf[0][:, :], kbf[1][:, :], inc=True)
        prev_out = P.dma("gpsimd", W0p.ap()[kc * 128:(kc + 1) * 128, :],
                         kbf[2][:, :], waits=[tsub])
        prep0.append(prev_out)
    prep_w = {0: [], 1: []}
    for j, (src, dst, cols, n_kc) in enumerate(
            [(w["k1"], W1p, 4096, 32), (w["k2"], W2p, 1024, 32)]):
        for kc in range(n_kc):
            i = kc % 2
            war = prep_w[j][kc - 2] if kc >= 2 else None
            tin = P.dma("sync", ku8[i][:, 0:cols],
                        src[kc * 128:(kc + 1) * 128, :],
                        waits=[war] if war else [])
            tcast = P.op("vector", "tensor_scalar", kbf[i][:, 0:cols],
                         ku8[i][:, 0:cols], 2.0, -1.0,
                         op0=ALU.mult, op1=ALU.add, waits=[tin], inc=True)
            prep_w[j].append(P.dma(
                "gpsimd", dst.ap()[kc * 128:(kc + 1) * 128, :],
                kbf[i][:, 0:cols], waits=[tcast]))
    t_prep0 = prep0[-1]
    t_prep1 = prep_w[0][-1]
    t_prep2 = prep_w[1][-1]
    prepstack.close()

    P.barrier()
    H1 = P.sbuf(st, "l2f_h1", [128, 32 * TC], BF16)
    H2 = P.sbuf(st, "l2f_h2", [128, 32 * TC], BF16)
    PS = [P.psum(st, "l2f_ps%d" % i, [128, TC]) for i in range(4)]
    Xre = [P.sbuf(st, "l2f_xre%d" % i, [128, TC], F32) for i in range(2)]
    Xo = [P.sbuf(st, "l2f_xo%d" % i, [128, TC], F32) for i in range(2)]
    pool = [P.sbuf(st, "l2f_wp%d" % i, [128, TC], BF16) for i in range(32)]

    sm1 = StreamedMatmul(P, st, "l2f_sm1", W0p.ap(), 8, 32, PS, pool)
    sm2 = StreamedMatmul(P, st, "l2f_sm2", W1p.ap(), 32, 32, PS, pool)
    sm3 = StreamedMatmul(P, st, "l2f_sm3", W2p.ap(), 32, 8, PS, pool)

    xo_guard = {}
    for tc in range(NTC):
        t0 = tc * TC
        h1_toks = {}

        def evac1(mc, ps, mmt, tc=tc, h1_toks=h1_toks):
            tok = P.op("scalar", "activation",
                       H1[:, mc * TC:(mc + 1) * TC], ps, func=AF.Relu,
                       waits=[mmt], inc=True)
            h1_toks[mc] = tok
            return tok
        sm1.run(lambda kc, t0=t0: XB[:, kc * T + t0: kc * T + t0 + TC],
                lambda kc: None, evac1, prep_tok=t_prep0)

        h2_toks = {}

        def evac2(mc, ps, mmt, tc=tc, h2_toks=h2_toks):
            tok = P.op("scalar", "activation",
                       H2[:, mc * TC:(mc + 1) * TC], ps, func=AF.Relu,
                       waits=[mmt], inc=True)
            h2_toks[mc] = tok
            return tok
        sm2.run(lambda kc: H1[:, kc * TC:(kc + 1) * TC],
                lambda kc: h1_toks[kc], evac2, prep_tok=t_prep1)

        def evac3(dc, ps, mmt, tc=tc, t0=t0):
            i = dc % 2
            war = xo_guard.get((tc, dc - 2))
            trel = P.dma("sync", Xre[i][:, :],
                         x_in[dc * 128:(dc + 1) * 128, t0:t0 + TC],
                         waits=[war] if war else [])
            tstt = P.op("vector", "scalar_tensor_tensor", Xo[i][:, :],
                        ps, S_pp[:, :], Xre[i][:, :],
                        op0=ALU.mult, op1=ALU.add,
                        waits=[mmt, trel, t_S], inc=True)
            tout = P.dma("gpsimd",
                         x_out[dc * 128:(dc + 1) * 128, t0:t0 + TC],
                         Xo[i][:, :], waits=[tstt])
            xo_guard[(tc, dc)] = tout
            return tstt
        sm3.run(lambda kc: H2[:, kc * TC:(kc + 1) * TC],
                lambda kc: h2_toks[kc], evac3, prep_tok=t_prep2)

    P.barrier()
    st.close()


def build_l2_ffn_launch():
    P = Prog("l2ffn")
    x_in = P.dram("x_in", [D, T], F32, kind="ExternalInput").ap()
    x_out = P.dram("x_out", [D, T], F32, kind="ExternalOutput").ap()
    w = {
        "k0": P.dram("wk0", [2048, 4096], U8, kind="ExternalInput").ap(),
        "k1": P.dram("wk1", [4096, 4096], U8, kind="ExternalInput").ap(),
        "k2": P.dram("wk2", [4096, 1024], U8, kind="ExternalInput").ap(),
        "s0": P.dram("ws0", [1, 1], F32, kind="ExternalInput").ap(),
        "s1": P.dram("ws1", [1, 1], F32, kind="ExternalInput").ap(),
        "s2": P.dram("ws2", [1, 1], F32, kind="ExternalInput").ap(),
    }
    bs = ExitStack()
    emit_l2_ffn(P, bs, x_in, x_out, w)
    bs.close()
    nc = P.assemble()
    return P, nc



def emit_transpose(P, psum_ap, src_ap, n_in_part, bf16=True):
    nm = "ident_bf" if bf16 else "ident_f32"
    arr = np.eye(128, dtype=np.float32)
    ident, t_i = P.sbuf_const(nm, arr.astype(_bf16np()) if bf16 else arr)
    return P.op("tensor", "transpose", psum_ap, src_ap,
                ident[0:n_in_part, 0:n_in_part], waits=[t_i])



def emit_l3_ffn(P, bs, x_in, x_out, w):
    ones_bf = P.sbuf_const("ones128_bf",
                           np.ones((128, 128), dtype=np.float32)
                           .astype(_bf16np()))
    ones1f, t_o1 = P.sbuf_const("ones1x128_f32",
                                np.ones((1, 128), dtype=np.float32))

    XB, xb_toks = emit_load_xb(P, bs, x_in, name="l3f_xb")
    st = ExitStack()
    gst = ExitStack()
    lnstack = ExitStack()
    _, rstd, mr = emit_layernorm_inplace(P, lnstack, XB, xb_toks, ones_bf,
                                         eps=1e-5, tag="l3f_ln",
                                         stats_stack=gst)
    P.barrier()
    lnstack.close()
    csb = P.sbuf(gst, "l3f_csb", [8, D], F32)
    rsb = P.sbuf(gst, "l3f_rsb", [8, D], F32)
    P.dma("sync", csb[:, :], w["centroids"][:, :])
    P.dma("sync", rsb[:, :], w["w_route"][:, :])
    csq = P.sbuf(gst, "l3f_csq", [8, D], F32)
    cn1 = P.sbuf(gst, "l3f_cn1", [8, 1], F32)
    P.op("scalar", "activation", csq[:, :], csb[:, :], func=AF.Square)
    P.op("vector", "reduce_sum", cn1[:, :], csq[:, :], axis=AX.X)
    P.op("scalar", "activation", cn1[:, :], cn1[:, :], func=AF.Sqrt)
    P.op("vector", "tensor_scalar_max", cn1[:, :], cn1[:, :], 1e-12)
    P.op("vector", "reciprocal", cn1[:, :], cn1[:, :])
    P.op("vector", "tensor_scalar", csb[:, :], csb[:, :], cn1[:, :], None,
         op0=ALU.mult)
    trp = P.psum(gst, "l3f_trp", [128, 16])
    cnT = P.sbuf(gst, "l3f_cnT", [128, NDC * 8], F32)
    wrT = P.sbuf(gst, "l3f_wrT", [128, NDC * 8], F32)
    for dc in range(NDC):
        emit_transpose(P, trp[:, 0:8], csb[:, dc * 128:(dc + 1) * 128], 8,
                       bf16=False)
        P.op("vector", "tensor_copy", cnT[:, dc * 8:(dc + 1) * 8],
             trp[:, 0:8])
        emit_transpose(P, trp[:, 8:16], rsb[:, dc * 128:(dc + 1) * 128], 8,
                       bf16=False)
        P.op("vector", "tensor_copy", wrT[:, dc * 8:(dc + 1) * 8],
             trp[:, 8:16])


    onesf, t_of = P.sbuf_const("ones128_f32", np.ones((128, 128),
                                                      dtype=np.float32))
    ss_ps = P.psum(gst, "l3f_ssps", [128, TC])
    rcp = P.sbuf(gst, "l3f_rcp", [128, T], F32)
    scr = P.sbuf(gst, "l3f_scr", [128, TC], F32)
    xnf = [P.sbuf(gst, "l3f_xnf%d" % i, [128, TC], F32) for i in range(NDC)]
    xsqf = [P.sbuf(gst, "l3f_xsqf%d" % i, [128, TC], F32)
            for i in range(NDC)]
    cos_ps = P.psum(gst, "l3f_cosps", [8, TC])
    rt_ps = P.psum(gst, "l3f_rtps", [8, TC])
    lg = P.sbuf(gst, "l3f_lg", [8, T], F32)
    t1 = P.sbuf(gst, "l3f_t1", [8, TC], F32)
    for tc in range(NTC):
        t0 = tc * TC
        for dc in range(NDC):
            P.dma("sync", xnf[dc][:, :],
                  x_in[dc * 128:(dc + 1) * 128, t0:t0 + TC])
            P.op("vector", "tensor_mul", xnf[dc][:, :], xnf[dc][:, :],
                 rstd[:, t0:t0 + TC])
            P.op("vector", "tensor_sub", xnf[dc][:, :], xnf[dc][:, :],
                 mr[:, t0:t0 + TC])
            P.op("scalar", "activation", xsqf[dc][:, :], xnf[dc][:, :],
                 func=AF.Square)
            P.op("tensor", "matmul", ss_ps[:, :], onesf[:, :],
                 xsqf[dc][:, :], start=(dc == 0), stop=(dc == NDC - 1),
                 waits=[t_of])
            P.op("tensor", "matmul", rt_ps[:, :],
                 wrT[:, dc * 8:(dc + 1) * 8], xnf[dc][:, :],
                 start=(dc == 0), stop=(dc == NDC - 1))
            P.op("tensor", "matmul", cos_ps[:, :],
                 cnT[:, dc * 8:(dc + 1) * 8], xnf[dc][:, :],
                 start=(dc == 0), stop=(dc == NDC - 1))
        P.op("scalar", "activation", scr[:, :], ss_ps[:, :], func=AF.Sqrt)
        P.op("vector", "tensor_scalar_max", scr[:, :], scr[:, :], 1e-12)
        P.op("vector", "reciprocal", rcp[:, t0:t0 + TC], scr[:, :])
        P.op("vector", "tensor_mul", t1[:, :], cos_ps[:, :],
             rcp[0:8, t0:t0 + TC])
        P.op("vector", "tensor_add", lg[:, t0:t0 + TC], t1[:, :],
             rt_ps[:, :])

    lgt_ps = P.psum(gst, "l3f_lgtps", [128, 8])
    LGt = P.sbuf(gst, "l3f_LGt", [128, 8], F32)
    m1n = P.sbuf(gst, "l3f_m1n", [128, 1], F32)
    m1 = P.sbuf(gst, "l3f_m1", [128, 1], F32)
    ismax = P.sbuf(gst, "l3f_ismax", [128, 8], F32)
    nmax = P.sbuf(gst, "l3f_nmax", [128, 1], F32)
    msk = P.sbuf(gst, "l3f_msk", [128, 8], F32)
    m2 = P.sbuf(gst, "l3f_m2", [128, 1], F32)
    thr = P.sbuf(gst, "l3f_thr", [128, 1], F32)
    ge = P.sbuf(gst, "l3f_ge", [128, 8], F32)
    pex = P.sbuf(gst, "l3f_pex", [128, 8], F32)
    den = P.sbuf(gst, "l3f_den", [128, 1], F32)
    gt = P.sbuf(gst, "l3f_gt", [128, 8], F32)
    g_ps = P.psum(gst, "l3f_gps", [128, 128])
    gates = P.sbuf(gst, "l3f_gates", [8, T], F32)
    for tt in range(T // 128):
        emit_transpose(P, lgt_ps[:, :], lg[:, tt * 128:(tt + 1) * 128], 8,
                       bf16=False)
        P.op("vector", "tensor_copy", LGt[:, :], lgt_ps[:, :])
        P.op("vector", "tensor_reduce", m1n[:, :], LGt[:, :], axis=AX.X,
             op=ALU.max, negate=True)
        P.op("vector", "tensor_scalar_mul", m1[:, :], m1n[:, :], -1.0)
        P.op("vector", "tensor_scalar", ismax[:, :], LGt[:, :], m1[:, :],
             None, op0=ALU.is_ge)
        P.op("vector", "reduce_sum", nmax[:, :], ismax[:, :], axis=AX.X)
        P.op("vector", "scalar_tensor_tensor", msk[:, :], ismax[:, :],
             -1e30, LGt[:, :], op0=ALU.mult, op1=ALU.add)
        P.op("vector", "tensor_reduce", m2[:, :], msk[:, :], axis=AX.X,
             op=ALU.max)
        P.op("vector", "tensor_scalar", nmax[:, :], nmax[:, :], 2.0, None,
             op0=ALU.is_ge)
        P.op("vector", "tensor_sub", thr[:, :], m1[:, :], m2[:, :])
        P.op("vector", "tensor_mul", thr[:, :], thr[:, :], nmax[:, :])
        P.op("vector", "tensor_add", thr[:, :], thr[:, :], m2[:, :])
        P.op("vector", "tensor_scalar", ge[:, :], LGt[:, :], thr[:, :],
             None, op0=ALU.is_ge)
        P.op("scalar", "activation", pex[:, :], LGt[:, :], func=AF.Exp,
             bias=m1n[:, :])
        P.op("vector", "tensor_mul", pex[:, :], pex[:, :], ge[:, :])
        P.op("vector", "reduce_sum", den[:, :], pex[:, :], axis=AX.X)
        P.op("vector", "reciprocal", den[:, :], den[:, :])
        P.op("vector", "tensor_scalar", gt[:, :], pex[:, :], den[:, :],
             None, op0=ALU.mult)
        emit_transpose(P, g_ps[0:8, 0:128], gt[:, :], 128, bf16=False)
        P.op("vector", "tensor_copy", gates[:, tt * 128:(tt + 1) * 128],
             g_ps[0:8, 0:128])
    gates_hbm = P.dram("l3f_gates_hbm", [8, T], F32)
    P.dma("sync", gates_hbm.ap()[:, :], gates[:, :])
    P.barrier()
    gst.close()

    acc_hbm = P.dram("l3f_acc", [D, T], F32)
    W1sb = P.sbuf(st, "l3f_w1sb", [128, 8 * 512], BF16)
    W2sb = P.sbuf(st, "l3f_w2sb", [128, 4 * 1024], BF16)
    b1sb = P.sbuf(st, "l3f_b1sb", [128, 4], F32)
    b2sb = P.sbuf(st, "l3f_b2sb", [128, 8], F32)
    brow = P.sbuf(st, "l3f_brow", [1, 1024], F32)
    b_ps = P.psum(st, "l3f_bps", [128, 8])
    grow = P.sbuf(st, "l3f_grow", [1, TC], F32)
    gr_ps = P.psum(st, "l3f_grps", [128, TC])
    gr = P.sbuf(st, "l3f_gr", [128, TC], F32)
    PS = [P.psum(st, "l3f_ps%d" % i, [128, TC]) for i in range(4)]
    Hsb = P.sbuf(st, "l3f_h", [128, 4 * TC], BF16)
    ctr = P.sbuf(st, "l3f_ctr", [128, TC], F32)
    for e in range(8):
        for kc in range(NDC):
            P.dma("gpsimd", W1sb[:, kc * 512:(kc + 1) * 512],
                  w["w1"][e * 1024 + kc * 128: e * 1024 + (kc + 1) * 128, :])
        for k2 in range(4):
            P.dma("gpsimd", W2sb[:, k2 * 1024:(k2 + 1) * 1024],
                  w["w2"][e * 512 + k2 * 128: e * 512 + (k2 + 1) * 128, :])
        P.dma("sync", brow[:, 0:512], w["b1"][e:e + 1, :])
        for j in range(4):
            emit_transpose(P, b_ps[:, j:j + 1],
                           brow[:, j * 128:(j + 1) * 128], 1, bf16=False)
            P.op("vector", "tensor_copy", b1sb[:, j:j + 1], b_ps[:, j:j + 1])
        P.dma("sync", brow[:, :], w["b2"][e:e + 1, :])
        for j in range(8):
            emit_transpose(P, b_ps[:, j:j + 1],
                           brow[:, j * 128:(j + 1) * 128], 1, bf16=False)
            P.op("vector", "tensor_copy", b2sb[:, j:j + 1], b_ps[:, j:j + 1])
        for tc in range(NTC):
            t0 = tc * TC
            for mc in range(4):
                for kc in range(NDC):
                    P.op("tensor", "matmul", PS[mc][:, :],
                         W1sb[:, kc * 512 + mc * 128: kc * 512 + mc * 128 + 128],
                         XB[:, kc * T + t0: kc * T + t0 + TC],
                         start=(kc == 0), stop=(kc == NDC - 1))
                P.op("scalar", "activation", Hsb[:, mc * TC:(mc + 1) * TC],
                     PS[mc][:, :], func=AF.Gelu, bias=b1sb[:, mc:mc + 1])
            P.dma("sync", grow[:, :], gates_hbm.ap()[e:e + 1, t0:t0 + TC])
            P.op("tensor", "matmul", gr_ps[:, :], ones1f[:, :], grow[:, :],
                 start=True, stop=True, waits=[t_o1])
            P.op("vector", "tensor_copy", gr[:, :], gr_ps[:, :])
            for half in range(2):
                for j in range(4):
                    dc = half * 4 + j
                    for k2 in range(4):
                        P.op("tensor", "matmul", PS[j][:, :],
                             W2sb[:, k2 * 1024 + dc * 128:
                                  k2 * 1024 + dc * 128 + 128],
                             Hsb[:, k2 * TC:(k2 + 1) * TC],
                             start=(k2 == 0), stop=(k2 == 3))
                for j in range(4):
                    dc = half * 4 + j
                    P.op("vector", "scalar_tensor_tensor", ctr[:, :],
                         PS[j][:, :], b2sb[:, dc:dc + 1], gr[:, :],
                         op0=ALU.add, op1=ALU.mult)
                    P.dma("gpsimd", acc_hbm.ap()[dc * 128:(dc + 1) * 128,
                                                 t0:t0 + TC],
                          ctr[:, :],
                          accum_op=(ALU.add if e > 0 else ALU.bypass))
    Xre = [P.sbuf(st, "l3f_xre%d" % i, [128, TC], F32) for i in range(2)]
    Acc = [P.sbuf(st, "l3f_acc%d" % i, [128, TC], F32) for i in range(2)]
    Xo = [P.sbuf(st, "l3f_xo%d" % i, [128, TC], F32) for i in range(2)]
    for tc in range(NTC):
        t0 = tc * TC
        for dc in range(NDC):
            i = dc % 2
            P.dma("sync", Xre[i][:, :], x_in[dc * 128:(dc + 1) * 128,
                                             t0:t0 + TC])
            P.dma("sync", Acc[i][:, :], acc_hbm.ap()[dc * 128:(dc + 1) * 128,
                                                     t0:t0 + TC])
            P.op("vector", "tensor_add", Xo[i][:, :], Xre[i][:, :],
                 Acc[i][:, :])
            P.dma("gpsimd", x_out[dc * 128:(dc + 1) * 128, t0:t0 + TC],
                  Xo[i][:, :])
    P.barrier()
    st.close()


def build_l3_ffn_launch():
    P = Prog("l3ffn")
    x_in = P.dram("x_in", [D, T], F32, kind="ExternalInput").ap()
    x_out = P.dram("x_out", [D, T], F32, kind="ExternalOutput").ap()
    w = {
        "w1": P.dram("ww1", [8 * 1024, 512], F32, kind="ExternalInput").ap(),
        "b1": P.dram("wb1", [8, 512], F32, kind="ExternalInput").ap(),
        "w2": P.dram("ww2", [8 * 512, 1024], F32, kind="ExternalInput").ap(),
        "b2": P.dram("wb2", [8, 1024], F32, kind="ExternalInput").ap(),
        "centroids": P.dram("wcen", [8, 1024], F32,
                            kind="ExternalInput").ap(),
        "w_route": P.dram("wrt", [8, 1024], F32, kind="ExternalInput").ap(),
    }
    bs = ExitStack()
    emit_l3_ffn(P, bs, x_in, x_out, w)
    bs.close()
    nc = P.assemble()
    return P, nc



def emit_l1_ffn(P, bs, x_in, x_out, w, counts_all, prev_mask,
                counts_out=None, mode="full"):
    ones_bf = P.sbuf_const("ones128_bf",
                           np.ones((128, 128), dtype=np.float32)
                           .astype(_bf16np()))
    onesf, t_of = P.sbuf_const("ones128_f32",
                               np.ones((128, 128), dtype=np.float32))
    ones1f, t_o1 = P.sbuf_const("ones1x128_f32",
                                np.ones((1, 128), dtype=np.float32))

    XB, xb_toks = emit_load_xb(P, bs, x_in, name="l1f_xb")
    st = ExitStack()
    gst = ExitStack()
    lnstack = ExitStack()
    _, rstd, mr = emit_layernorm_inplace(P, lnstack, XB, xb_toks, ones_bf,
                                         eps=1e-5, tag="l1f_ln",
                                         stats_stack=gst)
    P.barrier()
    lnstack.close()

    WgT = P.sbuf(gst, "l1f_wgT", [128, NDC * 8], F32)
    for kc in range(NDC):
        P.dma("sync", WgT[:, kc * 8:(kc + 1) * 8],
              w["Wg"][kc * 128:(kc + 1) * 128, :])
    lg_ps = P.psum(gst, "l1f_lgps", [8, TC])
    lg = P.sbuf(gst, "l1f_lg", [8, TC], F32)
    xnf = [P.sbuf(gst, "l1f_xnf%d" % i, [128, TC], F32) for i in range(NDC)]
    lt_ps = P.psum(gst, "l1f_ltps", [128, 8])
    LT = P.sbuf(gst, "l1f_LT", [128, 8], F32)
    m1n = P.sbuf(gst, "l1f_m1n", [128, 1], F32)
    pex = P.sbuf(gst, "l1f_pex", [128, 8], F32)
    den = P.sbuf(gst, "l1f_den", [128, 1], F32)
    v1 = P.sbuf(gst, "l1f_v1", [128, 1], F32)
    v2 = P.sbuf(gst, "l1f_v2", [128, 1], F32)
    oh1 = P.sbuf(gst, "l1f_oh1", [128, 8], F32)
    oh2 = P.sbuf(gst, "l1f_oh2", [128, 8], F32)
    mskd = P.sbuf(gst, "l1f_mskd", [128, 8], F32)
    wk = P.sbuf(gst, "l1f_wk", [128, 2], F32)
    tr_ps = P.psum(gst, "l1f_trps", [128, 128])
    oh1T = P.sbuf(gst, "l1f_oh1T", [8, T], BF16)
    oh2T = P.sbuf(gst, "l1f_oh2T", [8, T], BF16)
    w0T = P.sbuf(gst, "l1f_w0T", [1, T], BF16)
    w1T = P.sbuf(gst, "l1f_w1T", [1, T], BF16)
    for tc in range(NTC):
        t0 = tc * TC
        for dc in range(NDC):
            P.dma("sync", xnf[dc][:, :],
                  x_in[dc * 128:(dc + 1) * 128, t0:t0 + TC])
            P.op("vector", "tensor_mul", xnf[dc][:, :], xnf[dc][:, :],
                 rstd[:, t0:t0 + TC])
            P.op("vector", "tensor_sub", xnf[dc][:, :], xnf[dc][:, :],
                 mr[:, t0:t0 + TC])
            P.op("tensor", "matmul", lg_ps[:, :],
                 WgT[:, dc * 8:(dc + 1) * 8], xnf[dc][:, :],
                 start=(dc == 0), stop=(dc == NDC - 1))
        P.op("vector", "tensor_copy", lg[:, :], lg_ps[:, :])
        for tti in range(TC // 128):
            s0 = tti * 128
            g0 = t0 + s0
            emit_transpose(P, lt_ps[:, 0:8], lg[:, s0:s0 + 128], 8,
                           bf16=False)
            P.op("vector", "tensor_copy", LT[:, :], lt_ps[:, 0:8])
            P.op("vector", "tensor_reduce", m1n[:, :], LT[:, :], axis=AX.X,
                 op=ALU.max, negate=True)
            P.op("scalar", "activation", pex[:, :], LT[:, :], func=AF.Exp,
                 bias=m1n[:, :])
            P.op("vector", "reduce_sum", den[:, :], pex[:, :], axis=AX.X)
            P.op("vector", "reciprocal", den[:, :], den[:, :])
            P.op("vector", "tensor_scalar", pex[:, :], pex[:, :],
                 den[:, :], None, op0=ALU.mult)
            P.op("vector", "tensor_reduce", v1[:, :], pex[:, :], axis=AX.X,
                 op=ALU.max)
            P.op("vector", "tensor_scalar", oh1[:, :], pex[:, :], v1[:, :],
                 None, op0=ALU.is_ge)
            P.op("vector", "scalar_tensor_tensor", mskd[:, :], oh1[:, :],
                 -2.0, pex[:, :], op0=ALU.mult, op1=ALU.add)
            P.op("vector", "tensor_reduce", v2[:, :], mskd[:, :],
                 axis=AX.X, op=ALU.max)
            P.op("vector", "tensor_scalar", oh2[:, :], mskd[:, :],
                 v2[:, :], None, op0=ALU.is_ge)
            P.op("vector", "tensor_add", wk[:, 0:1], v1[:, :], v2[:, :])
            P.op("vector", "tensor_scalar_add", wk[:, 0:1], wk[:, 0:1],
                 1e-9)
            P.op("vector", "reciprocal", wk[:, 1:2], wk[:, 0:1])
            P.op("vector", "tensor_mul", wk[:, 0:1], v1[:, :], wk[:, 1:2])
            P.op("vector", "tensor_mul", wk[:, 1:2], v2[:, :], wk[:, 1:2])
            emit_transpose(P, tr_ps[0:8, 0:128], oh1[:, :], 128,
                           bf16=False)
            P.op("vector", "tensor_copy", oh1T[:, g0:g0 + 128],
                 tr_ps[0:8, 0:128])
            emit_transpose(P, tr_ps[0:8, 0:128], oh2[:, :], 128,
                           bf16=False)
            P.op("vector", "tensor_copy", oh2T[:, g0:g0 + 128],
                 tr_ps[0:8, 0:128])
            emit_transpose(P, tr_ps[0:1, 0:128], wk[:, 0:1], 128,
                           bf16=False)
            P.op("vector", "tensor_copy", w0T[0:1, g0:g0 + 128],
                 tr_ps[0:1, 0:128])
            emit_transpose(P, tr_ps[0:1, 0:128], wk[:, 1:2], 128,
                           bf16=False)
            P.op("vector", "tensor_copy", w1T[0:1, g0:g0 + 128],
                 tr_ps[0:1, 0:128])

    ones8T = P.sbuf_const("ones8xT_bf",
                          np.ones((8, T), dtype=np.float32)
                          .astype(_bf16np()))
    pos = P.sbuf(gst, "l1f_pos", [8, T], F32)
    kp0 = P.sbuf(gst, "l1f_kp0", [8, T], BF16)
    kp1 = P.sbuf(gst, "l1f_kp1", [8, T], BF16)
    casb = P.sbuf(gst, "l1f_casb", [8, 16], F32)
    pmsb = P.sbuf(gst, "l1f_pmsb", [8, 2], F32)
    off_ps = P.psum(gst, "l1f_offps", [16, 2])
    offs = P.sbuf(gst, "l1f_offs", [16, 2], F32)
    offs_hi = P.sbuf(gst, "l1f_offshi", [8, 2], F32)
    cap0 = P.sbuf(gst, "l1f_cap0", [8, 2], F32)
    cout = P.sbuf(gst, "l1f_cout", [8, 2], F32)
    if mode == "counts":
        P.op("vector", "tensor_tensor_scan", pos[:, :], ones8T[0][:, :],
             oh1T[:, :], 0.0, op0=ALU.mult, op1=ALU.add,
             waits=[ones8T[1]])
        P.op("vector", "tensor_copy", cout[:, 0:1], pos[:, T - 1:T])
        P.op("vector", "tensor_tensor_scan", pos[:, :], ones8T[0][:, :],
             oh2T[:, :], 0.0, op0=ALU.mult, op1=ALU.add)
        P.op("vector", "tensor_copy", cout[:, 1:2], pos[:, T - 1:T])
        P.dma("sync", counts_out, cout[:, :])
        P.barrier()
        gst.close()
        st.close()
        return
    P.dma("sync", casb[:, :], counts_all)
    P.dma("sync", pmsb[:, 0:1], prev_mask)
    P.op("vector", "memset", pmsb[:, 1:2], 1.0)
    P.op("tensor", "matmul", off_ps[:, 0:1], casb[:, :], pmsb[:, 0:1],
         start=True, stop=True)
    P.op("tensor", "matmul", off_ps[:, 1:2], casb[:, :], pmsb[:, 1:2],
         start=True, stop=True)
    P.op("vector", "tensor_copy", offs[:, :], off_ps[:, :])
    P.dma("sync", offs_hi[:, :], offs[8:16, :])
    P.op("vector", "tensor_scalar_min", cap0[:, 0:1], offs[0:8, 1:2],
         2048.0)
    P.op("vector", "tensor_add", cap0[:, 1:2], offs_hi[:, 0:1],
         cap0[:, 0:1])

    P.op("vector", "tensor_tensor_scan", pos[:, :], ones8T[0][:, :],
         oh1T[:, :], 0.0, op0=ALU.mult, op1=ALU.add, waits=[ones8T[1]])
    if counts_out is not None:
        P.op("vector", "tensor_copy", cout[:, 0:1], pos[:, T - 1:T])
    P.op("vector", "tensor_scalar", pos[:, :], pos[:, :], offs[0:8, 0:1],
         None, op0=ALU.add)
    P.op("vector", "tensor_scalar", pos[:, :], pos[:, :], 2048.0, None,
         op0=ALU.is_le)
    P.op("vector", "tensor_mul", kp0[:, :], pos[:, :], oh1T[:, :])
    P.op("vector", "tensor_tensor_scan", pos[:, :], ones8T[0][:, :],
         oh2T[:, :], 0.0, op0=ALU.mult, op1=ALU.add)
    if counts_out is not None:
        P.op("vector", "tensor_copy", cout[:, 1:2], pos[:, T - 1:T])
        P.dma("sync", counts_out, cout[:, :])
    P.op("vector", "tensor_scalar", pos[:, :], pos[:, :], cap0[:, 1:2],
         None, op0=ALU.add)
    P.op("vector", "tensor_scalar", pos[:, :], pos[:, :], 2048.0, None,
         op0=ALU.is_le)
    P.op("vector", "tensor_mul", kp1[:, :], pos[:, :], oh2T[:, :])

    ones1x8 = P.sbuf_const("ones1x8_bf", np.ones((1, 8), dtype=np.float32)
                           .astype(_bf16np()))
    wrep_ps = P.psum(gst, "l1f_wrepps", [8, TC])
    comb = P.sbuf(gst, "l1f_comb", [8, TC], F32)
    scr8 = P.sbuf(gst, "l1f_scr8", [8, TC], F32)
    comb_hbm = P.dram("l1f_comb_hbm", [8, T], F32)
    for tc in range(NTC):
        t0 = tc * TC
        P.op("tensor", "matmul", wrep_ps[:, :], ones1x8[0][:, :],
             w0T[0:1, t0:t0 + TC], start=True, stop=True,
             waits=[ones1x8[1]])
        P.op("vector", "tensor_mul", scr8[:, :], wrep_ps[:, :],
             kp0[:, t0:t0 + TC])
        P.op("tensor", "matmul", wrep_ps[:, :], ones1x8[0][:, :],
             w1T[0:1, t0:t0 + TC], start=True, stop=True)
        P.op("vector", "tensor_mul", comb[:, :], wrep_ps[:, :],
             kp1[:, t0:t0 + TC])
        P.op("vector", "tensor_add", comb[:, :], comb[:, :], scr8[:, :])
        P.dma("sync", comb_hbm.ap()[:, t0:t0 + TC], comb[:, :])
    P.barrier()
    gst.close()

    acc_hbm = P.dram("l1f_acc", [D, T], F32)
    W1p = P.dram("l1f_W1p", [D, 4096], BF16)
    W2sb = P.sbuf(st, "l1f_w2sb", [128, 32 * 1024], BF16)
    b1sb = P.sbuf(st, "l1f_b1sb", [128, 32], F32)
    b2sb = P.sbuf(st, "l1f_b2sb", [128, 8], F32)
    brow = P.sbuf(st, "l1f_brow", [1, 1024], F32)
    b_ps = P.psum(st, "l1f_bps", [128, 8])
    grow = P.sbuf(st, "l1f_grow", [1, TC], F32)
    gr_ps = P.psum(st, "l1f_grps", [128, TC])
    gr = P.sbuf(st, "l1f_gr", [128, TC], F32)
    PS = [P.psum(st, "l1f_ps%d" % i, [128, TC]) for i in range(4)]
    wpool = [P.sbuf(st, "l1f_wp%d" % i, [128, TC], BF16) for i in range(8)]
    Hsb = P.sbuf(st, "l1f_h", [128, 32 * TC], BF16)
    ctr = P.sbuf(st, "l1f_ctr", [128, TC], F32)
    for e in range(8):
        for kc in range(NDC):
            P.dma("gpsimd", W1p.ap()[kc * 128:(kc + 1) * 128, :],
                  w["W1"][e * 1024 + kc * 128:e * 1024 + (kc + 1) * 128, :])
        for k2 in range(32):
            P.dma("gpsimd", W2sb[:, k2 * 1024:(k2 + 1) * 1024],
                  w["W2"][e * 4096 + k2 * 128:e * 4096 + (k2 + 1) * 128, :])
        for q in range(4):
            P.dma("sync", brow[:, :], w["b1"][e:e + 1,
                                              q * 1024:(q + 1) * 1024])
            for j in range(8):
                emit_transpose(P, b_ps[:, j:j + 1],
                               brow[:, j * 128:(j + 1) * 128], 1,
                               bf16=False)
                P.op("vector", "tensor_copy", b1sb[:, q * 8 + j:q * 8 + j + 1],
                     b_ps[:, j:j + 1])
        P.dma("sync", brow[:, 0:1024], w["b2"][e:e + 1, :])
        for j in range(8):
            emit_transpose(P, b_ps[:, j:j + 1],
                           brow[:, j * 128:(j + 1) * 128], 1, bf16=False)
            P.op("vector", "tensor_copy", b2sb[:, j:j + 1], b_ps[:, j:j + 1])
        sm1 = StreamedMatmul(P, st, "l1f_sm1_%d" % e, W1p.ap(), NDC, 32,
                             PS, wpool)
        for tc in range(NTC):
            t0 = tc * TC
            h_toks = {}

            def evach(mc, ps, mmt, h_toks=h_toks):
                tok = P.op("scalar", "activation",
                           Hsb[:, mc * TC:(mc + 1) * TC], ps,
                           func=AF.Gelu_apprx_tanh, bias=b1sb[:, mc:mc + 1],
                           waits=[mmt])
                h_toks[mc] = tok
                return tok
            sm1.run(lambda kc, t0=t0: XB[:, kc * T + t0: kc * T + t0 + TC],
                    lambda kc: None, evach)
            P.dma("sync", grow[:, :], comb_hbm.ap()[e:e + 1, t0:t0 + TC])
            P.op("tensor", "matmul", gr_ps[:, :], ones1f[:, :], grow[:, :],
                 start=True, stop=True, waits=[t_o1])
            P.op("vector", "tensor_copy", gr[:, :], gr_ps[:, :])
            for half in range(2):
                for j in range(4):
                    dc = half * 4 + j
                    for k2 in range(32):
                        P.op("tensor", "matmul", PS[j][:, :],
                             W2sb[:, k2 * 1024 + dc * 128:
                                  k2 * 1024 + dc * 128 + 128],
                             Hsb[:, k2 * TC:(k2 + 1) * TC],
                             start=(k2 == 0), stop=(k2 == 31))
                for j in range(4):
                    dc = half * 4 + j
                    P.op("vector", "scalar_tensor_tensor", ctr[:, :],
                         PS[j][:, :], b2sb[:, dc:dc + 1], gr[:, :],
                         op0=ALU.add, op1=ALU.mult)
                    P.dma("gpsimd", acc_hbm.ap()[dc * 128:(dc + 1) * 128,
                                                 t0:t0 + TC],
                          ctr[:, :],
                          accum_op=(ALU.add if e > 0 else ALU.bypass))
    Xre = [P.sbuf(st, "l1f_xre%d" % i, [128, TC], F32) for i in range(2)]
    Acc = [P.sbuf(st, "l1f_acc%d" % i, [128, TC], F32) for i in range(2)]
    Xo = [P.sbuf(st, "l1f_xo%d" % i, [128, TC], F32) for i in range(2)]
    for tc in range(NTC):
        t0 = tc * TC
        for dc in range(NDC):
            i = dc % 2
            P.dma("sync", Xre[i][:, :], x_in[dc * 128:(dc + 1) * 128,
                                             t0:t0 + TC])
            P.dma("sync", Acc[i][:, :], acc_hbm.ap()[dc * 128:(dc + 1) * 128,
                                                     t0:t0 + TC])
            P.op("vector", "tensor_add", Xo[i][:, :], Xre[i][:, :],
                 Acc[i][:, :])
            P.dma("gpsimd", x_out[dc * 128:(dc + 1) * 128, t0:t0 + TC],
                  Xo[i][:, :])
    P.barrier()
    st.close()


def build_l1_ffn_launch():
    P = Prog("l1ffn")
    x_in = P.dram("x_in", [D, T], F32, kind="ExternalInput").ap()
    x_out = P.dram("x_out", [D, T], F32, kind="ExternalOutput").ap()
    counts_out = P.dram("counts_out", [8, 2], F32, kind="ExternalOutput").ap()
    w = {
        "Wg": P.dram("wWg", [D, 8], F32, kind="ExternalInput").ap(),
        "W1": P.dram("wW1", [8 * D, 4096], F32, kind="ExternalInput").ap(),
        "b1": P.dram("wb1", [8, 4096], F32, kind="ExternalInput").ap(),
        "W2": P.dram("wW2", [8 * 4096, D], F32, kind="ExternalInput").ap(),
        "b2": P.dram("wb2", [8, 1024], F32, kind="ExternalInput").ap(),
    }
    counts_all = P.dram("counts_all", [8, 16], F32,
                        kind="ExternalInput").ap()
    prev_mask = P.dram("prev_mask", [8, 1], F32, kind="ExternalInput").ap()
    bs = ExitStack()
    emit_l1_ffn(P, bs, x_in, x_out, w, counts_all, prev_mask,
                counts_out=counts_out)
    bs.close()
    nc = P.assemble()
    return P, nc



def make_attn_scratch(P, st, tag, dh):
    TCQ = 512
    return dict(
        spt=P.psum(st, tag + "_spt", [128, TCQ]),
        opt=P.psum(st, tag + "_opt", [dh, TCQ]),
        dpt=P.psum(st, tag + "_dpt", [1, TCQ]),
        rpt=P.psum(st, tag + "_rpt", [dh, TCQ]),
        ssb=P.sbuf(st, tag + "_ssb", [128, TCQ], F32),
        PT=[P.sbuf(st, tag + "_pt%d" % i, [128, TCQ], BF16)
            for i in range(T // 128)],
        dsb=P.sbuf(st, tag + "_dsb", [1, TCQ], F32),
        rsb=P.sbuf(st, tag + "_rsb", [dh, TCQ], F32),
    )


def emit_causal_attn_rt(P, scr, tag, rt, QRtile, KR, VT, n_h, dh, n_kv,
                        scale, mhat_rep, out_cb, masks, ones_bf,
                        qr_tok=None):
    TCQ = 512
    spt, opt, dpt, rpt = scr["spt"], scr["opt"], scr["dpt"], scr["rpt"]
    ssb, PT, dsb, rsb = scr["ssb"], scr["PT"], scr["dsb"], scr["rsb"]
    ones1f, t_o1 = P.sbuf_const("ones1x128_f32",
                                np.ones((1, 128), dtype=np.float32))
    hpt = 128 // dh
    grp = n_h // n_kv
    for hr in range(hpt):
        h = rt * hpt + hr
        kv = h // grp
        for tq in range(T // TCQ):
            q0 = tq * TCQ
            n_tk = (q0 + TCQ) // 128
            for tk in range(n_tk):
                k0 = tk * 128
                lhs = KR[hr][hr * dh:(hr + 1) * dh,
                             kv * T + k0: kv * T + k0 + 128]
                rhs = QRtile[hr * dh:(hr + 1) * dh, q0:q0 + TCQ]
                P.op("tensor", "matmul", spt[:, :], lhs, rhs,
                     start=True, stop=True,
                     waits=[qr_tok] if qr_tok else [])
                if k0 >= q0:
                    dlt = (k0 - q0) // 128
                    P.op("vector", "tensor_add", ssb[:, :], spt[:, :],
                         masks[0][dlt][:, :], waits=[masks[1][dlt]])
                    src = ssb[:, :]
                else:
                    src = spt[:, :]
                pt = PT[tk]
                P.op("scalar", "activation", pt[:, :], src, func=AF.Exp,
                     scale=float(scale), bias=mhat_rep[:, h:h + 1])
                P.op("tensor", "matmul", dpt[:, :], ones_bf[0][:, 0:1],
                     pt[:, :], start=(tk == 0), stop=(tk == n_tk - 1),
                     waits=[ones_bf[1]])
                P.op("tensor", "matmul", opt[:, :],
                     VT[:, tk * n_kv * dh + kv * dh:
                        tk * n_kv * dh + (kv + 1) * dh],
                     pt[:, :], start=(tk == 0), stop=(tk == n_tk - 1))
            P.op("vector", "reciprocal", dsb[:, :], dpt[:, :])
            P.op("tensor", "matmul", rpt[:, :], ones1f[0:1, 0:dh],
                 dsb[:, :], start=True, stop=True, waits=[t_o1])
            P.op("vector", "tensor_copy", rsb[:, :], rpt[:, :])
            out_cb(h, tq, opt, rsb)


def make_causal_masks(P):
    outs, toks = [], []
    for dlt in range(4):
        i = np.arange(128)[:, None] + 128 * dlt
        j = np.arange(512)[None, :]
        m = np.where(i <= j, 0.0, -1e30).astype(np.float32)
        h, t = P.sbuf_const("cmask%d" % dlt, m)
        outs.append(h)
        toks.append(t)
    return outs, toks


def emit_row_to_pp(P, stack, tag, row_hbm_ap, n):
    nch = n // 128
    row = P.sbuf(stack, tag + "_row", [1, min(n, 1024)], F32)
    tp = P.psum(stack, tag + "_tp", [128, min(nch, 8)])
    out = P.sbuf(stack, tag + "_pp", [128, nch], F32)
    for q in range((n + 1023) // 1024):
        cw = min(1024, n - q * 1024)
        P.dma("sync", row[:, 0:cw], row_hbm_ap[0:1, q * 1024:q * 1024 + cw])
        for j in range(cw // 128):
            emit_transpose(P, tp[:, j % 8:j % 8 + 1],
                           row[:, j * 128:(j + 1) * 128], 1, bf16=False)
            P.op("vector", "tensor_copy", out[:, q * 8 + j:q * 8 + j + 1],
                 tp[:, j % 8:j % 8 + 1])
    return out


def emit_rmsnorm_inplace(P, stack, XB, ones_bf, eps, gpp, tag,
                         stats_stack=None):
    sst = stats_stack if stats_stack is not None else stack
    rstd = P.sbuf(sst, tag + "_rstd", [128, NTC * TC], F32)
    sq_ps = P.psum(stack, tag + "_sq", [128, TC])
    xsq = [P.sbuf(stack, tag + "_xsq%d" % i, [128, TC], BF16)
           for i in range(NDC)]
    scr = P.sbuf(stack, tag + "_scr", [128, TC], F32)
    for tc in range(NTC):
        t0 = tc * TC
        for dc in range(NDC):
            xtile = XB[:, dc * T + t0: dc * T + t0 + TC]
            P.op("scalar", "activation", xsq[dc][:, :], xtile,
                 func=AF.Square)
            P.op("tensor", "matmul", sq_ps[:, :], ones_bf[0][:, :],
                 xsq[dc][:, :], start=(dc == 0), stop=(dc == NDC - 1),
                 waits=[ones_bf[1]])
        P.op("vector", "tensor_scalar", scr[:, :], sq_ps[:, :],
             1.0 / D, float(eps), op0=ALU.mult, op1=ALU.add)
        P.op("scalar", "activation", scr[:, :], scr[:, :], func=AF.Ln)
        P.op("scalar", "activation", rstd[:, t0:t0 + TC], scr[:, :],
             func=AF.Exp, scale=-0.5)
    scr2 = P.sbuf(stack, tag + "_scr2", [128, TC], F32)
    for dc in range(NDC):
        for tc in range(NTC):
            t0 = tc * TC
            xtile = XB[:, dc * T + t0: dc * T + t0 + TC]
            P.op("vector", "tensor_mul", scr2[:, :], xtile,
                 rstd[:, t0:t0 + TC])
            P.op("vector", "tensor_scalar", xtile, scr2[:, :],
                 gpp[:, dc:dc + 1], None, op0=ALU.mult)
    return rstd



def emit_l0_ffn(P, bs, x_in, x_out, w, counts_all, prev_mask,
                counts_out=None, mode="full", h_ext=None):
    ones_bf = P.sbuf_const("ones128_bf",
                           np.ones((128, 128), dtype=np.float32)
                           .astype(_bf16np()))
    ones1f, t_o1 = P.sbuf_const("ones1x128_f32",
                                np.ones((1, 128), dtype=np.float32))
    if h_ext is not None:
        h_ap = h_ext
    else:
        h_ap = P.dram("l0f_h", [D, T], F32).ap()
    if mode == "apply":
        emit_l0_ffn_phaseE(P, bs, x_in, x_out, w, counts_all, prev_mask,
                           counts_out, mode, h_ap)
        return
    masks = make_causal_masks(P)
    lnx_hbm = P.dram("l0f_lnx", [D, T], F32)
    qr_hbm = P.dram("l0f_qrh", [D, T], BF16)
    kr_hbm = P.dram("l0f_krh", [256, T], BF16)
    vt_hbm = P.dram("l0f_vth", [T, 256], BF16)

    sa = ExitStack()
    XB, xb_toks = emit_load_xb(P, sa, x_in, name="l0f_xb")
    lnstack = ExitStack()
    ln_toks, rstd0, mr0 = emit_layernorm_inplace(
        P, lnstack, XB, xb_toks, ones_bf, eps=1e-5, tag="l0f_ln")
    lf = [P.sbuf(lnstack, "l0f_lf%d" % i, [128, TC], F32) for i in range(2)]
    for tc in range(NTC):
        t0 = tc * TC
        for dc in range(NDC):
            i = dc % 2
            P.dma("sync", lf[i][:, :], x_in[dc * 128:(dc + 1) * 128,
                                            t0:t0 + TC])
            P.op("vector", "tensor_mul", lf[i][:, :], lf[i][:, :],
                 rstd0[:, t0:t0 + TC])
            P.op("vector", "tensor_sub", lf[i][:, :], lf[i][:, :],
                 mr0[:, t0:t0 + TC])
            P.dma("gpsimd", lnx_hbm.ap()[dc * 128:(dc + 1) * 128,
                                         t0:t0 + TC], lf[i][:, :])
    P.barrier()
    lnstack.close()
    g1pp = emit_row_to_pp(P, sa, "l0f_g1", w["g1"], D)
    r1stack = ExitStack()
    emit_rmsnorm_inplace(P, r1stack, XB, ones_bf, 1e-6, g1pp, "l0f_r1")
    P.barrier()
    r1stack.close()

    th = 10000.0 ** (-np.arange(0, 64, 2, dtype=np.float64) / 64)
    ang = np.outer(th, np.arange(T, dtype=np.float64))
    cs = np.concatenate([np.cos(ang), np.cos(ang)], 0).astype(np.float32)
    sn = np.concatenate([np.sin(ang), np.sin(ang)], 0).astype(np.float32)
    cosD = P.const_input("ropec_d", cs.astype(_bf16np()))
    sinD = P.const_input("ropes_d", sn.astype(_bf16np()))
    cosT = P.sbuf(sa, "l0f_cos", [64, T], BF16)
    sinT = P.sbuf(sa, "l0f_sin", [64, T], BF16)
    P.dma("sync", cosT[:, :], cosD[:, :])
    P.dma("sync", sinT[:, :], sinD[:, :])

    wsl = [P.sbuf(sa, "l0f_wsl%d" % i, [128, 128], BF16) for i in range(8)]
    nsl = [P.sbuf(sa, "l0f_nsl%d" % i, [128, 128], BF16) for i in range(8)]
    q_ps = P.psum(sa, "l0f_qps", [128, TC])
    n_ps = P.psum(sa, "l0f_nps", [128, TC])
    rsc = P.sbuf(sa, "l0f_rsc", [64, TC], F32)
    rsc2 = P.sbuf(sa, "l0f_rsc2", [64, TC], F32)
    qrt = P.sbuf(sa, "l0f_qrt", [128, TC], BF16)
    sqb = P.sbuf(sa, "l0f_sqb", [128, TC], BF16)
    nq_ps = P.psum(sa, "l0f_nqps", [2, TC])
    qn2 = P.sbuf(sa, "l0f_qn2", [2, 8], F32)
    kn2 = P.sbuf(sa, "l0f_kn2", [2, 2], F32)
    nred = P.sbuf(sa, "l0f_nred", [2, 1], F32)
    sel2 = P.sbuf_const("sel2h", np.kron(np.eye(2), np.ones((64, 1)))
                        .astype(np.float32).astype(_bf16np()))

    def rope_proj(dst_hbm, n_rows, w_ap, dstn2):
        for rt in range(n_rows // 128):
            for tc in range(NTC):
                t0 = tc * TC
                for kc in range(NDC):
                    P.dma("gpsimd", wsl[kc][:, :],
                          w_ap[kc * 128:(kc + 1) * 128,
                               rt * 128:(rt + 1) * 128])
                    nh = nsl[kc]
                    P.op("vector", "tensor_scalar_mul",
                         nh[:, :].rearrange("p (h x) -> p h x", x=64)
                         [:, :, 0:32],
                         wsl[kc][:, :].rearrange("p (h x) -> p h x", x=64)
                         [:, :, 32:64], -1.0)
                    P.op("vector", "tensor_copy",
                         nh[:, :].rearrange("p (h x) -> p h x", x=64)
                         [:, :, 32:64],
                         wsl[kc][:, :].rearrange("p (h x) -> p h x", x=64)
                         [:, :, 0:32])
                    P.op("tensor", "matmul", q_ps[:, :], wsl[kc][:, :],
                         XB[:, kc * T + t0:kc * T + t0 + TC],
                         start=(kc == 0), stop=(kc == NDC - 1))
                    P.op("tensor", "matmul", n_ps[:, :], nh[:, :],
                         XB[:, kc * T + t0:kc * T + t0 + TC],
                         start=(kc == 0), stop=(kc == NDC - 1))
                for hf in range(2):
                    r0 = hf * 64
                    P.op("vector", "tensor_mul", rsc[:, :],
                         q_ps[r0:r0 + 64, 0:TC], cosT[:, t0:t0 + TC])
                    P.op("vector", "tensor_mul", rsc2[:, :],
                         n_ps[r0:r0 + 64, 0:TC], sinT[:, t0:t0 + TC])
                    P.op("vector", "tensor_add",
                         qrt[r0:r0 + 64, 0:TC], rsc[:, :], rsc2[:, :])
                P.dma("gpsimd", dst_hbm.ap()[rt * 128:(rt + 1) * 128,
                                             t0:t0 + TC], qrt[:, :])
                P.op("scalar", "activation", sqb[:, :], qrt[:, :],
                     func=AF.Square)
                P.op("tensor", "matmul", nq_ps[:, :], sel2[0][:, :],
                     sqb[:, :], start=True, stop=True, waits=[sel2[1]])
                P.op("vector", "tensor_reduce", nred[:, :], nq_ps[:, :],
                     axis=AX.X, op=ALU.max)
                if tc == 0:
                    P.op("vector", "tensor_copy", dstn2[:, rt:rt + 1],
                         nred[:, :])
                else:
                    P.op("vector", "tensor_max", dstn2[:, rt:rt + 1],
                         dstn2[:, rt:rt + 1], nred[:, :])
    rope_proj(qr_hbm, 1024, w["wq"], qn2)
    rope_proj(kr_hbm, 256, w["wk"], kn2)

    wvp = P.sbuf(sa, "l0f_wvp", [128, 8 * 256], BF16)
    for kc in range(NDC):
        P.dma("gpsimd", wvp[:, kc * 256:(kc + 1) * 256],
              w["wv"][kc * 128:(kc + 1) * 128, :])
    vt_ps = P.psum(sa, "l0f_vtps", [128, 256])
    vt_sb = P.sbuf(sa, "l0f_vtsb", [128, 256], BF16)
    for tt in range(T // 128):
        for kc in range(NDC):
            P.op("tensor", "matmul", vt_ps[:, :],
                 XB[:, kc * T + tt * 128: kc * T + (tt + 1) * 128],
                 wvp[:, kc * 256:(kc + 1) * 256],
                 start=(kc == 0), stop=(kc == NDC - 1))
        P.op("vector", "tensor_copy", vt_sb[:, :], vt_ps[:, :])
        P.dma("gpsimd", vt_hbm.ap()[tt * 128:(tt + 1) * 128, :],
              vt_sb[:, :])
    nq_hbm = P.dram("l0f_nqh", [2, 8], F32)
    nk_hbm = P.dram("l0f_nkh", [2, 2], F32)
    P.dma("sync", nq_hbm.ap()[:, :], qn2[:, :])
    P.dma("sync", nk_hbm.ap()[:, :], kn2[:, :])
    P.barrier()
    sa.close()

    sc_ = ExitStack()
    KRa = P.sbuf(sc_, "l0f_kra", [64, 4 * T], BF16)
    KRb = P.sbuf(sc_, "l0f_krb", [128, 4 * T], BF16)
    KR = [KRa, KRb]
    VT = P.sbuf(sc_, "l0f_vt", [128, 32 * 256], BF16)
    ATT = P.sbuf(sc_, "l0f_attb", [128, 8 * T], BF16)
    for kv in range(4):
        P.dma("sync", KRa[0:64, kv * T:(kv + 1) * T],
              kr_hbm.ap()[kv * 64:(kv + 1) * 64, :])
        P.dma("sync", KRb[64:128, kv * T:(kv + 1) * T],
              kr_hbm.ap()[kv * 64:(kv + 1) * 64, :])
    for tt in range(32):
        P.dma("sync", VT[:, tt * 256:(tt + 1) * 256],
              vt_hbm.ap()[tt * 128:(tt + 1) * 128, :])
    qrow = P.sbuf(sc_, "l0f_qrow", [1, 16], F32)
    krow = P.sbuf(sc_, "l0f_krow", [1, 16], F32)
    P.dma("sync", qrow[0:1, :].rearrange("a (q r) -> a q r", r=2),
          nq_hbm.ap().rearrange("r q -> q r")[None, :, :],
          allow_slow_non_contiguous=True)
    for h in range(16):
        kv = h // 4
        P.dma("sync", krow[0:1, h:h + 1],
              nk_hbm.ap()[kv % 2:kv % 2 + 1, kv // 2:kv // 2 + 1],
              allow_slow_non_contiguous=True)
    negm = P.sbuf(sc_, "l0f_negm", [1, 16], F32)
    P.op("vector", "tensor_mul", negm[:, :], qrow[:, :], krow[:, :])
    P.op("scalar", "activation", negm[:, :], negm[:, :], func=AF.Sqrt)
    P.op("vector", "tensor_scalar_mul", negm[:, :], negm[:, :],
         -1.02 / 8.0)
    mh_ps = P.psum(sc_, "l0f_mhps", [128, 16])
    P.op("tensor", "matmul", mh_ps[:, :], ones1f[0:1, :], negm[:, :],
         start=True, stop=True, waits=[t_o1])
    mhat_rep = P.sbuf(sc_, "l0f_mhat", [128, 16], F32)
    P.op("vector", "tensor_copy", mhat_rep[:, :], mh_ps[:, :])

    QRt = [P.sbuf(sc_, "l0f_qrt0", [128, T], BF16)]
    att_sc = P.sbuf(sc_, "l0f_attsc", [64, TC], F32)
    ascr = make_attn_scratch(P, sc_, "l0f_at", 64)
    qr_loads = {}
    for rt in range(8):
        qr_loads[rt] = P.dma("sync", QRt[0][:, :],
                             qr_hbm.ap()[rt * 128:(rt + 1) * 128, :])

        def out_cb(h, tq, opt, rsb, rt=rt):
            half = h % 2
            t0 = tq * TC
            P.op("vector", "tensor_mul", att_sc[:, :], opt[:, 0:TC],
                 rsb[:, 0:TC])
            P.op("vector", "tensor_copy",
                 ATT[half * 64:(half + 1) * 64,
                     rt * T + t0: rt * T + t0 + TC], att_sc[:, :])
        emit_causal_attn_rt(P, ascr, "l0f_at%d" % rt, rt, QRt[0], KR,
                            VT, 16, 64, 4, 0.125, mhat_rep, out_cb, masks,
                            ones_bf, qr_loads[rt])

    wop = [P.sbuf(sc_, "l0f_wop%d" % i, [128, 128], BF16) for i in range(8)]
    ho_ps = P.psum(sc_, "l0f_hops", [128, TC])
    lnx_sl = P.sbuf(sc_, "l0f_lnxsl", [128, TC], F32)
    hosb = P.sbuf(sc_, "l0f_hosb", [128, TC], F32)
    for tc in range(NTC):
        t0 = tc * TC
        for dc in range(NDC):
            for kc in range(NDC):
                P.dma("gpsimd", wop[kc][:, :],
                      w["wo"][kc * 128:(kc + 1) * 128,
                              dc * 128:(dc + 1) * 128])
                P.op("tensor", "matmul", ho_ps[:, :], wop[kc][:, :],
                     ATT[:, kc * T + t0: kc * T + t0 + TC],
                     start=(kc == 0), stop=(kc == NDC - 1))
            P.dma("sync", lnx_sl[:, :],
                  lnx_hbm.ap()[dc * 128:(dc + 1) * 128, t0:t0 + TC])
            P.op("vector", "tensor_add", hosb[:, :], ho_ps[:, :],
                 lnx_sl[:, :])
            P.dma("gpsimd", h_ap[dc * 128:(dc + 1) * 128,
                                 t0:t0 + TC], hosb[:, :])
    P.barrier()
    sc_.close()
    emit_l0_ffn_phaseE(P, bs, x_in, x_out, w, counts_all, prev_mask,
                       counts_out, mode, h_ap)


def emit_l0_ffn_phaseE(P, bs, x_in, x_out, w, counts_all, prev_mask,
                       counts_out, mode, h_ap):
    ones_bf = P.sbuf_const("ones128_bf",
                           np.ones((128, 128), dtype=np.float32)
                           .astype(_bf16np()))
    ones1f, t_o1 = P.sbuf_const("ones1x128_f32",
                                np.ones((1, 128), dtype=np.float32))
    st = ExitStack()
    gst = ExitStack()
    XH, xh_toks = emit_load_xb(P, st, h_ap, name="l0f_xh2")
    g2pp = emit_row_to_pp(P, gst, "l0f_g2", w["g2"], D)
    r2stack = ExitStack()
    rstd2 = emit_rmsnorm_inplace(P, r2stack, XH, ones_bf, 1e-6, g2pp,
                                 "l0f_r2", stats_stack=gst)
    P.barrier()
    r2stack.close()

    rwT = P.sbuf(gst, "l0f_rwT", [128, NDC * 8], F32)
    for kc in range(NDC):
        P.dma("sync", rwT[:, kc * 8:(kc + 1) * 8],
              w["router_w"][kc * 128:(kc + 1) * 128, :])
    rb = P.sbuf(gst, "l0f_rb", [1, 8], F32)
    P.dma("sync", rb[:, :], w["router_b"][0:1, :])
    rb8 = P.sbuf(gst, "l0f_rb8", [8, 1], F32)
    rb_ps = P.psum(gst, "l0f_rbps", [8, 8])
    ones1x8 = P.sbuf_const("ones1x8_f32b", np.ones((1, 8), dtype=np.float32))
    emit_transpose(P, rb_ps[0:8, 0:1], rb[0:1, 0:8], 1, bf16=False)
    P.op("vector", "tensor_copy", rb8[:, :], rb_ps[0:8, 0:1])

    lg_ps = P.psum(gst, "l0f_lgps", [8, TC])
    lg = P.sbuf(gst, "l0f_lg", [8, TC], F32)
    xnf = [P.sbuf(gst, "l0f_xnf%d" % i, [128, TC], F32) for i in range(NDC)]
    lt_ps = P.psum(gst, "l0f_ltps", [128, 8])
    LT = P.sbuf(gst, "l0f_LT", [128, 8], F32)
    v1 = P.sbuf(gst, "l0f_v1", [128, 1], F32)
    v2 = P.sbuf(gst, "l0f_v2", [128, 1], F32)
    oh1 = P.sbuf(gst, "l0f_oh1", [128, 8], F32)
    oh2 = P.sbuf(gst, "l0f_oh2", [128, 8], F32)
    mskd = P.sbuf(gst, "l0f_mskd", [128, 8], F32)
    sk = P.sbuf(gst, "l0f_sk", [128, 2], F32)
    tr_ps = P.psum(gst, "l0f_trps", [128, 128])
    oh1T = P.sbuf(gst, "l0f_oh1T", [8, T], BF16)
    oh2T = P.sbuf(gst, "l0f_oh2T", [8, T], BF16)
    s0T = P.sbuf(gst, "l0f_s0T", [1, T], BF16)
    s1T = P.sbuf(gst, "l0f_s1T", [1, T], BF16)
    for tc in range(NTC):
        t0 = tc * TC
        for dc in range(NDC):
            P.dma("sync", xnf[dc][:, :],
                  h_ap[dc * 128:(dc + 1) * 128, t0:t0 + TC])
            P.op("vector", "tensor_mul", xnf[dc][:, :], xnf[dc][:, :],
                 rstd2[:, t0:t0 + TC])
            P.op("vector", "tensor_scalar", xnf[dc][:, :], xnf[dc][:, :],
                 g2pp[:, dc:dc + 1], None, op0=ALU.mult)
            P.op("tensor", "matmul", lg_ps[:, :],
                 rwT[:, dc * 8:(dc + 1) * 8], xnf[dc][:, :],
                 start=(dc == 0), stop=(dc == NDC - 1))
        P.op("vector", "tensor_scalar", lg[:, :], lg_ps[:, :], rb8[:, 0:1],
             None, op0=ALU.add)
        for tti in range(TC // 128):
            s0_ = tti * 128
            g0 = t0 + s0_
            emit_transpose(P, lt_ps[:, 0:8], lg[:, s0_:s0_ + 128], 8,
                           bf16=False)
            P.op("vector", "tensor_copy", LT[:, :], lt_ps[:, 0:8])
            P.op("vector", "tensor_reduce", v1[:, :], LT[:, :], axis=AX.X,
                 op=ALU.max)
            P.op("vector", "tensor_scalar", oh1[:, :], LT[:, :], v1[:, :],
                 None, op0=ALU.is_ge)
            P.op("vector", "scalar_tensor_tensor", mskd[:, :], oh1[:, :],
                 -1e30, LT[:, :], op0=ALU.mult, op1=ALU.add)
            P.op("vector", "tensor_reduce", v2[:, :], mskd[:, :],
                 axis=AX.X, op=ALU.max)
            P.op("vector", "tensor_scalar", oh2[:, :], mskd[:, :],
                 v2[:, :], None, op0=ALU.is_ge)
            P.op("vector", "tensor_sub", sk[:, 0:1], v1[:, :], v2[:, :])
            P.op("scalar", "activation", sk[:, 0:1], sk[:, 0:1],
                 func=AF.Sigmoid)
            P.op("vector", "scalar_tensor_tensor", sk[:, 1:2], sk[:, 0:1],
                 -1.0, sk[:, 0:1], op0=ALU.mult, op1=ALU.bypass)
            P.op("vector", "tensor_scalar", sk[:, 1:2], sk[:, 0:1], -1.0,
                 1.0, op0=ALU.mult, op1=ALU.add)
            emit_transpose(P, tr_ps[0:8, 0:128], oh1[:, :], 128,
                           bf16=False)
            P.op("vector", "tensor_copy", oh1T[:, g0:g0 + 128],
                 tr_ps[0:8, 0:128])
            emit_transpose(P, tr_ps[0:8, 0:128], oh2[:, :], 128,
                           bf16=False)
            P.op("vector", "tensor_copy", oh2T[:, g0:g0 + 128],
                 tr_ps[0:8, 0:128])
            emit_transpose(P, tr_ps[0:1, 0:128], sk[:, 0:1], 128,
                           bf16=False)
            P.op("vector", "tensor_copy", s0T[:, g0:g0 + 128],
                 tr_ps[0:1, 0:128])
            emit_transpose(P, tr_ps[0:1, 0:128], sk[:, 1:2], 128,
                           bf16=False)
            P.op("vector", "tensor_copy", s1T[:, g0:g0 + 128],
                 tr_ps[0:1, 0:128])

    ones8T = P.sbuf_const("ones8xT_bf", np.ones((8, T), dtype=np.float32)
                          .astype(_bf16np()))
    cstk = ExitStack()
    pos0 = P.sbuf(cstk, "l0f_pos0", [8, T], F32)
    pos1 = P.sbuf(cstk, "l0f_pos1", [8, T], F32)
    casb = P.sbuf(cstk, "l0f_casb", [8, 16], F32)
    pmsb = P.sbuf(cstk, "l0f_pmsb", [8, 1], F32)
    off_ps = P.psum(cstk, "l0f_offps", [16, 1])
    offs = P.sbuf(cstk, "l0f_offs", [16, 1], F32)
    offs_hi = P.sbuf(cstk, "l0f_offshi", [8, 1], F32)
    cout = P.sbuf(cstk, "l0f_cout", [8, 2], F32)
    P.op("vector", "tensor_tensor_scan", pos0[:, :], ones8T[0][:, :],
         oh1T[:, :], 0.0, op0=ALU.mult, op1=ALU.add, waits=[ones8T[1]])
    P.op("vector", "tensor_tensor_scan", pos1[:, :], ones8T[0][:, :],
         oh2T[:, :], 0.0, op0=ALU.mult, op1=ALU.add)
    if counts_out is not None:
        P.op("vector", "tensor_copy", cout[:, 0:1], pos0[:, T - 1:T])
        P.op("vector", "tensor_copy", cout[:, 1:2], pos1[:, T - 1:T])
        P.dma("sync", counts_out, cout[:, :])
    if mode == "counts":
        P.barrier()
        cstk.close()
        gst.close()
        st.close()
        return
    P.dma("sync", casb[:, :], counts_all)
    P.dma("sync", pmsb[:, :], prev_mask)
    P.op("tensor", "matmul", off_ps[:, :], casb[:, :], pmsb[:, :],
         start=True, stop=True)
    P.op("vector", "tensor_copy", offs[:, :], off_ps[:, :])
    P.dma("sync", offs_hi[:, :], offs[8:16, :])
    P.op("vector", "tensor_scalar", pos0[:, :], pos0[:, :], offs[0:8, 0:1],
         None, op0=ALU.add)
    P.op("vector", "tensor_add", pos1[:, :], pos1[:, :], pos0[:, :])
    P.op("vector", "tensor_scalar", pos1[:, :], pos1[:, :],
         offs_hi[:, 0:1], None, op0=ALU.add)
    P.op("vector", "tensor_scalar", pos0[:, :], pos0[:, :], 4096.0, None,
         op0=ALU.is_le)
    P.op("vector", "tensor_mul", oh1T[:, :], pos0[:, :], oh1T[:, :])
    P.op("vector", "tensor_scalar", pos1[:, :], pos1[:, :], 4096.0, None,
         op0=ALU.is_le)
    P.op("vector", "tensor_mul", oh2T[:, :], pos1[:, :], oh2T[:, :])
    P.barrier()
    cstk.close()

    evec = P.sbuf_const("evec8_bf", np.arange(8, dtype=np.float32)
                        .reshape(8, 1).astype(_bf16np()))
    evecf = P.sbuf_const("evec8_f32", np.arange(8, dtype=np.float32)
                         .reshape(8, 1))
    asg_ps = P.psum(gst, "l0f_asgps", [1, TC])
    mrow = P.sbuf(gst, "l0f_mrow", [1, TC], F32)
    a1row = P.sbuf(gst, "l0f_a1row", [1, TC], F32)
    wy = P.sbuf(gst, "l0f_wy", [1, TC], F32)
    tm0 = P.sbuf(gst, "l0f_tm0", [1, TC], F32)
    tm1 = P.sbuf(gst, "l0f_tm1", [1, TC], F32)
    g8_ps = P.psum(gst, "l0f_g8ps", [8, TC])
    we_hbm = P.dram("l0f_we", [8, T], F32)
    wx_hbm = P.dram("l0f_wx", [1, T], F32)
    wesb = P.sbuf(gst, "l0f_wesb", [8, TC], F32)
    for tc in range(NTC):
        t0 = tc * TC
        P.op("tensor", "matmul", asg_ps[:, :], evec[0][:, :],
             oh1T[:, t0:t0 + TC], start=True, stop=True, waits=[evec[1]])
        P.op("vector", "tensor_copy", mrow[:, :], asg_ps[:, :])
        P.op("tensor", "matmul", asg_ps[:, :], evec[0][:, :],
             oh2T[:, t0:t0 + TC], start=True, stop=True)
        P.op("vector", "tensor_copy", a1row[:, :], asg_ps[:, :])
        P.op("vector", "tensor_max", mrow[:, :], mrow[:, :], a1row[:, :])
        P.op("tensor", "matmul", asg_ps[:, :], ones8T[0][:, 0:1],
             oh1T[:, t0:t0 + TC], start=True, stop=True)
        P.op("vector", "tensor_mul", tm0[:, :], asg_ps[:, :],
             s0T[:, t0:t0 + TC])
        P.op("tensor", "matmul", asg_ps[:, :], ones8T[0][:, 0:1],
             oh2T[:, t0:t0 + TC], start=True, stop=True)
        P.op("vector", "tensor_mul", tm1[:, :], asg_ps[:, :],
             s1T[:, t0:t0 + TC])
        P.op("vector", "tensor_add", wy[:, :], tm0[:, :], tm1[:, :])
        P.op("tensor", "matmul", g8_ps[:, :], ones1x8[0][0:1, :],
             mrow[:, :], start=True, stop=True)
        P.op("vector", "tensor_scalar", wesb[:, :], g8_ps[:, :],
             evecf[0][:, :], None, op0=ALU.is_equal, waits=[evecf[1]])
        P.op("tensor", "matmul", g8_ps[:, :], ones1x8[0][0:1, :],
             wy[:, :], start=True, stop=True)
        P.op("vector", "tensor_mul", wesb[:, :], wesb[:, :], g8_ps[:, :])
        P.dma("sync", we_hbm.ap()[:, t0:t0 + TC], wesb[:, :])
        P.op("vector", "tensor_scalar", wy[:, :],
             wy[:, :], -1.0, 1.0, op0=ALU.mult, op1=ALU.add)
        P.dma("sync", wx_hbm.ap()[:, t0:t0 + TC], wy[:, :])
    P.barrier()
    gst.close()

    acc_hbm = P.dram("l0f_acc", [D, T], F32)
    w1sb = P.sbuf(st, "l0f_w1sb", [128, 8 * 2048], BF16)
    w2sb = P.sbuf(st, "l0f_w2sb", [128, 8 * 2048], BF16)
    w3sb = P.sbuf(st, "l0f_w3sb", [128, 16 * 1024], BF16)
    PS = [P.psum(st, "l0f_ps%d" % i, [128, TC]) for i in range(8)]
    H2 = P.sbuf(st, "l0f_h2sb", [128, 16 * TC], BF16)
    asb = P.sbuf(st, "l0f_asb", [128, TC], F32)
    usb = P.sbuf(st, "l0f_usb", [128, TC], F32)
    grow = P.sbuf(st, "l0f_grow", [1, TC], F32)
    gr = P.sbuf(st, "l0f_gr", [128, TC], F32)
    ctr = P.sbuf(st, "l0f_ctr", [128, TC], F32)
    for e in range(8):
        for kc in range(NDC):
            P.dma("gpsimd", w1sb[:, kc * 2048:(kc + 1) * 2048],
                  w["w1"][e * 1024 + kc * 128:e * 1024 + (kc + 1) * 128, :])
            P.dma("gpsimd", w2sb[:, kc * 2048:(kc + 1) * 2048],
                  w["w2"][e * 1024 + kc * 128:e * 1024 + (kc + 1) * 128, :])
        for k3 in range(16):
            P.dma("gpsimd", w3sb[:, k3 * 1024:(k3 + 1) * 1024],
                  w["w3"][e * 2048 + k3 * 128:e * 2048 + (k3 + 1) * 128, :])
        for tc in range(NTC):
            t0 = tc * TC
            for hb in range(4):
                for j in range(4):
                    mc = hb * 4 + j
                    for kc in range(NDC):
                        P.op("tensor", "matmul", PS[j][:, :],
                             w1sb[:, kc * 2048 + mc * 128:
                                  kc * 2048 + mc * 128 + 128],
                             XH[:, kc * T + t0:kc * T + t0 + TC],
                             start=(kc == 0), stop=(kc == NDC - 1))
                        P.op("tensor", "matmul", PS[4 + j][:, :],
                             w2sb[:, kc * 2048 + mc * 128:
                                  kc * 2048 + mc * 128 + 128],
                             XH[:, kc * T + t0:kc * T + t0 + TC],
                             start=(kc == 0), stop=(kc == NDC - 1))
                for j in range(4):
                    mc = hb * 4 + j
                    P.op("vector", "tensor_copy", asb[:, :], PS[j][:, :])
                    P.op("vector", "tensor_mul", usb[:, :], PS[4 + j][:, :],
                         asb[:, :])
                    P.op("scalar", "activation",
                         H2[:, mc * TC:(mc + 1) * TC], usb[:, :],
                         func=AF.Gelu)
            P.dma("sync", grow[:, :], we_hbm.ap()[e:e + 1, t0:t0 + TC])
            P.op("tensor", "matmul", PS[0][:, :], ones1f[:, :], grow[:, :],
                 start=True, stop=True, waits=[t_o1])
            P.op("vector", "tensor_copy", gr[:, :], PS[0][:, :])
            for half in range(2):
                for j in range(4):
                    dc = half * 4 + j
                    for k3 in range(16):
                        P.op("tensor", "matmul", PS[j][:, :],
                             w3sb[:, k3 * 1024 + dc * 128:
                                  k3 * 1024 + dc * 128 + 128],
                             H2[:, k3 * TC:(k3 + 1) * TC],
                             start=(k3 == 0), stop=(k3 == 15))
                for j in range(4):
                    dc = half * 4 + j
                    P.op("vector", "tensor_mul", ctr[:, :], PS[j][:, :],
                         gr[:, :])
                    P.dma("gpsimd", acc_hbm.ap()[dc * 128:(dc + 1) * 128,
                                                 t0:t0 + TC],
                          ctr[:, :],
                          accum_op=(ALU.add if e > 0 else ALU.bypass))
    wxr = P.sbuf(st, "l0f_wxr", [128, TC], F32)
    Xre, Acc, Hld, Xo = asb, usb, ctr, gr
    for tc in range(NTC):
        t0 = tc * TC
        P.dma("sync", grow[:, :], wx_hbm.ap()[0:1, t0:t0 + TC])
        P.op("tensor", "matmul", PS[1][:, :], ones1f[:, :], grow[:, :],
             start=True, stop=True)
        P.op("vector", "tensor_copy", wxr[:, :], PS[1][:, :])
        for dc in range(NDC):
            P.dma("sync", Xre[:, :], x_in[dc * 128:(dc + 1) * 128,
                                          t0:t0 + TC])
            P.dma("sync", Acc[:, :],
                  acc_hbm.ap()[dc * 128:(dc + 1) * 128, t0:t0 + TC])
            P.dma("sync", Hld[:, :],
                  h_ap[dc * 128:(dc + 1) * 128, t0:t0 + TC])
            P.op("vector", "tensor_mul", Xo[:, :], wxr[:, :],
                 XH[:, dc * T + t0: dc * T + t0 + TC])
            P.op("vector", "tensor_add", Xo[:, :], Xo[:, :], Acc[:, :])
            P.op("vector", "tensor_add", Xo[:, :], Xo[:, :], Hld[:, :])
            P.op("vector", "tensor_add", Xo[:, :], Xo[:, :], Xre[:, :])
            P.dma("gpsimd", x_out[dc * 128:(dc + 1) * 128, t0:t0 + TC],
                  Xo[:, :])
    P.barrier()
    st.close()


def build_l0_ffn_launch():
    P = Prog("l0ffn")
    x_in = P.dram("x_in", [D, T], F32, kind="ExternalInput").ap()
    x_out = P.dram("x_out", [D, T], F32, kind="ExternalOutput").ap()
    counts_out = P.dram("counts_out", [8, 2], F32, kind="ExternalOutput").ap()
    w = {
        "g1": P.dram("wg1", [1, D], F32, kind="ExternalInput").ap(),
        "g2": P.dram("wg2", [1, D], F32, kind="ExternalInput").ap(),
        "wq": P.dram("wwq", [D, 1024], F32, kind="ExternalInput").ap(),
        "wk": P.dram("wwk", [D, 256], F32, kind="ExternalInput").ap(),
        "wv": P.dram("wwv", [D, 256], F32, kind="ExternalInput").ap(),
        "wo": P.dram("wwo", [1024, D], F32, kind="ExternalInput").ap(),
        "router_w": P.dram("wrw", [D, 8], F32, kind="ExternalInput").ap(),
        "router_b": P.dram("wrb", [1, 8], F32, kind="ExternalInput").ap(),
        "w1": P.dram("ww1", [8 * D, 2048], F32, kind="ExternalInput").ap(),
        "w2": P.dram("ww2", [8 * D, 2048], F32, kind="ExternalInput").ap(),
        "w3": P.dram("ww3", [8 * 2048, D], F32, kind="ExternalInput").ap(),
    }
    counts_all = P.dram("counts_all", [8, 16], F32,
                        kind="ExternalInput").ap()
    prev_mask = P.dram("prev_mask", [8, 1], F32, kind="ExternalInput").ap()
    bs = ExitStack()
    emit_l0_ffn(P, bs, x_in, x_out, w, counts_all, prev_mask,
                counts_out=counts_out)
    bs.close()
    nc = P.assemble()
    return P, nc



def emit_l2_attn(P, bs, x_in, x_out, w):
    ones_bf = P.sbuf_const("ones128_bf",
                           np.ones((128, 128), dtype=np.float32)
                           .astype(_bf16np()))
    ones1f, t_o1 = P.sbuf_const("ones1x128_f32",
                                np.ones((1, 128), dtype=np.float32))
    masks = make_causal_masks(P)

    dim = 128
    inv_f = np.exp(-np.log(10000.0) * (np.arange(0, dim, 2) / dim))
    angT = np.outer(np.arange(T, dtype=np.float64), inv_f)
    pe = np.zeros((T, dim))
    pe[:, 0::2] = np.sin(angT)
    pe[:, 1::2] = np.cos(angT)
    peT = P.const_input("l2a_peT", pe.T.astype(np.float32)
                        .astype(_bf16np()))
    ang0 = np.zeros((1, 64))
    pe0 = np.zeros((dim, 1))
    pe0[0::2, 0] = np.sin(ang0)[0]
    pe0[1::2, 0] = np.cos(ang0)[0]
    pe0c = P.const_input("l2a_pe0", pe0.astype(np.float32))

    st = ExitStack()
    XB, xb_toks = emit_load_xb(P, st, x_in, name="l2a_xb")
    lnstack = ExitStack()
    emit_layernorm_inplace(P, lnstack, XB, xb_toks, ones_bf,
                           eps=1e-5, tag="l2a_ln")
    P.barrier()
    lnstack.close()

    woT_hbm = P.dram("l2a_woTh", [D, 1024], BF16)
    wqT_hbm = P.dram("l2a_wqTh", [D, 1024], BF16)
    prepst = ExitStack()
    wk_T = P.sbuf(st, "l2a_wkT", [128, NDC * 128], BF16)
    wv_T = P.sbuf(st, "l2a_wvT", [128, NDC * 128], BF16)
    wrow = P.sbuf(prepst, "l2a_wrow", [128, 1024], BF16)
    wcol = P.sbuf(prepst, "l2a_wcol", [128, 128], BF16)
    tr_ps = P.psum(prepst, "l2a_trps", [128, 128], BF16)
    for (w_ap, dsth, n_out) in ((w["Wq"], wqT_hbm, 1024),
                                (w["Wo"], woT_hbm, 1024)):
        for ro in range(n_out // 128):
            P.dma("gpsimd", wrow[:, :],
                  w_ap[ro * 128:(ro + 1) * 128, :])
            for ci in range(NDC):
                emit_transpose(P, tr_ps[:, :],
                               wrow[:, ci * 128:(ci + 1) * 128], 128)
                P.op("vector", "tensor_copy", wcol[:, :], tr_ps[:, :])
                P.dma("gpsimd",
                      dsth.ap()[ci * 128:(ci + 1) * 128,
                                ro * 128:(ro + 1) * 128], wcol[:, :])
    for (w_ap, dst, n_out) in ((w["Wk"], wk_T, 128), (w["Wv"], wv_T, 128)):
        for ro in range(n_out // 128):
            P.dma("gpsimd", wrow[:, :], w_ap[ro * 128:(ro + 1) * 128, :])
            for ci in range(NDC):
                emit_transpose(P, tr_ps[:, :],
                               wrow[:, ci * 128:(ci + 1) * 128], 128)
                P.op("vector", "tensor_copy",
                     dst[:, ci * n_out + ro * 128:
                         ci * n_out + (ro + 1) * 128], tr_ps[:, :])
    P.barrier()
    prepst.close()

    qr_hbm = P.dram("l2a_qrh", [1024, T], BF16)
    kr_hbm = P.dram("l2a_krh", [128, T], BF16)
    vt_hbm = P.dram("l2a_vth", [T, 128], BF16)
    att_hbm = P.dram("l2a_atth", [1024, T], BF16)
    wq_T = P.sbuf(st, "l2a_wqT", [128, NDC * 128], BF16)
    pesb = P.sbuf(st, "l2a_pesb", [128, T], BF16)
    P.dma("gpsimd", pesb[:, :], peT[:, :])
    pe0sb = P.sbuf(st, "l2a_pe0sb", [128, 1], F32)
    P.dma("sync", pe0sb[:, :], pe0c[:, :])
    q_ps = P.psum(st, "l2a_qps", [128, TC])
    qsc = P.sbuf(st, "l2a_qsc", [128, TC], F32)
    qbf = P.sbuf(st, "l2a_qbf", [128, TC], BF16)
    sqb = P.sbuf(st, "l2a_sqb", [128, TC], BF16)
    nq_ps = P.psum(st, "l2a_nqps", [1, TC])
    qn2 = P.sbuf(st, "l2a_qn2", [1, 16], F32)
    nred = P.sbuf(st, "l2a_nred", [1, 1], F32)

    def qk_norm_acc(col, tc):
        P.op("scalar", "activation", sqb[:, :], qbf[:, :], func=AF.Square)
        P.op("tensor", "matmul", nq_ps[:, :], ones_bf[0][:, 0:1],
             sqb[:, :], start=True, stop=True, waits=[ones_bf[1]])
        P.op("vector", "tensor_reduce", nred[:, :], nq_ps[:, :],
             axis=AX.X, op=ALU.max)
        if tc == 0:
            P.op("vector", "tensor_copy", qn2[:, col:col + 1], nred[:, :])
        else:
            P.op("vector", "tensor_max", qn2[:, col:col + 1],
                 qn2[:, col:col + 1], nred[:, :])
    for rt in range(8):
        for kc in range(NDC):
            P.dma("sync", wq_T[:, kc * 128:(kc + 1) * 128],
                  wqT_hbm.ap()[kc * 128:(kc + 1) * 128,
                               rt * 128:(rt + 1) * 128])
        for tc in range(NTC):
            t0 = tc * TC
            for kc in range(NDC):
                P.op("tensor", "matmul", q_ps[:, :],
                     wq_T[:, kc * 128:(kc + 1) * 128],
                     XB[:, kc * T + t0:kc * T + t0 + TC],
                     start=(kc == 0), stop=(kc == NDC - 1))
            P.op("vector", "tensor_add", qbf[:, :], q_ps[:, :],
                 pesb[:, t0:t0 + TC])
            P.dma("gpsimd", qr_hbm.ap()[rt * 128:(rt + 1) * 128,
                                        t0:t0 + TC], qbf[:, :])
            qk_norm_acc(rt, tc)
    for tc in range(NTC):
        t0 = tc * TC
        for kc in range(NDC):
            P.op("tensor", "matmul", q_ps[:, :],
                 wk_T[:, kc * 128:(kc + 1) * 128],
                 XB[:, kc * T + t0:kc * T + t0 + TC],
                 start=(kc == 0), stop=(kc == NDC - 1))
        P.op("vector", "tensor_scalar", qbf[:, :], q_ps[:, :],
             pe0sb[:, :], None, op0=ALU.add)
        P.dma("gpsimd", kr_hbm.ap()[:, t0:t0 + TC], qbf[:, :])
        qk_norm_acc(8, tc)
    for tt in range(T // 128):
        for kc in range(NDC):
            P.op("tensor", "matmul", q_ps[:, 0:128],
                 XB[:, kc * T + tt * 128: kc * T + (tt + 1) * 128],
                 wv_T[:, kc * 128:(kc + 1) * 128],
                 start=(kc == 0), stop=(kc == NDC - 1))
        P.op("vector", "tensor_copy", qbf[:, 0:128], q_ps[:, 0:128])
        P.dma("gpsimd", vt_hbm.ap()[tt * 128:(tt + 1) * 128, :],
              qbf[:, 0:128])
    negm = P.sbuf(st, "l2a_negm", [1, 8], F32)
    P.op("vector", "tensor_scalar", negm[:, :], qn2[:, 0:8],
         qn2[:, 8:9], None, op0=ALU.mult)
    P.op("scalar", "activation", negm[:, :], negm[:, :], func=AF.Sqrt)
    P.op("vector", "tensor_scalar_mul", negm[:, :], negm[:, :],
         -1.02 / math.sqrt(128.0))
    negm_hbm = P.dram("l2a_negmh", [1, 8], F32)
    P.dma("sync", negm_hbm.ap()[:, :], negm[:, :])
    P.barrier()
    st.close()

    st2 = ExitStack()
    KRk = P.sbuf(st2, "l2a_kr", [128, T], BF16)
    VT = P.sbuf(st2, "l2a_vt", [128, 32 * 128], BF16)
    P.dma("sync", KRk[:, :], kr_hbm.ap()[:, :])
    for tt in range(32):
        P.dma("sync", VT[:, tt * 128:(tt + 1) * 128],
              vt_hbm.ap()[tt * 128:(tt + 1) * 128, :])
    negm2 = P.sbuf(st2, "l2a_negm2", [1, 8], F32)
    P.dma("sync", negm2[:, :], negm_hbm.ap()[:, :])
    mh_ps = P.psum(st2, "l2a_mhps", [128, 8])
    P.op("tensor", "matmul", mh_ps[:, :], ones1f[0:1, :], negm2[:, :],
         start=True, stop=True, waits=[t_o1])
    mhat_rep = P.sbuf(st2, "l2a_mhat", [128, 8], F32)
    P.op("vector", "tensor_copy", mhat_rep[:, :], mh_ps[:, :])
    QRh = P.sbuf(st2, "l2a_qrh_sb", [128, T], BF16)
    att_sc = P.sbuf(st2, "l2a_attsc", [128, TC], F32)
    attb = P.sbuf(st2, "l2a_attb1", [128, TC], BF16)
    ascr = make_attn_scratch(P, st2, "l2a_at", 128)

    def out_cb(h, tq, opt, rsb):
        t0 = tq * TC
        P.op("vector", "tensor_mul", att_sc[:, :], opt[:, 0:TC],
             rsb[:, 0:TC])
        P.op("vector", "tensor_copy", attb[:, :], att_sc[:, :])
        P.dma("gpsimd", att_hbm.ap()[h * 128:(h + 1) * 128, t0:t0 + TC],
              attb[:, :])
    for h in range(8):
        qtok = P.dma("sync", QRh[:, :],
                     qr_hbm.ap()[h * 128:(h + 1) * 128, :])
        emit_causal_attn_rt(P, ascr, "l2a_h%d" % h, h, QRh, [KRk, None],
                            VT, 8, 128, 1, 1.0 / math.sqrt(128.0),
                            mhat_rep, out_cb, masks, ones_bf, qtok)
    P.barrier()
    st2.close()

    st3 = ExitStack()
    ho_ps = P.psum(st3, "l2a_hops", [128, TC])
    xre = P.sbuf(st3, "l2a_xre", [128, TC], F32)
    xo = P.sbuf(st3, "l2a_xo", [128, TC], F32)
    wosl = [P.sbuf(st3, "l2a_wosl%d" % i, [128, 128], BF16)
            for i in range(8)]
    atsl = [P.sbuf(st3, "l2a_atsl%d" % i, [128, TC], BF16)
            for i in range(8)]
    for tc in range(NTC):
        t0 = tc * TC
        for kc in range(NDC):
            P.dma("sync", atsl[kc][:, :],
                  att_hbm.ap()[kc * 128:(kc + 1) * 128, t0:t0 + TC])
        for dc in range(NDC):
            for kc in range(NDC):
                P.dma("sync", wosl[kc][:, :],
                      woT_hbm.ap()[kc * 128:(kc + 1) * 128,
                                   dc * 128:(dc + 1) * 128])
                P.op("tensor", "matmul", ho_ps[:, :], wosl[kc][:, :],
                     atsl[kc][:, :],
                     start=(kc == 0), stop=(kc == NDC - 1))
            P.dma("sync", xre[:, :], x_in[dc * 128:(dc + 1) * 128,
                                          t0:t0 + TC])
            P.op("vector", "tensor_add", xo[:, :], ho_ps[:, :], xre[:, :])
            P.dma("gpsimd", x_out[dc * 128:(dc + 1) * 128, t0:t0 + TC],
                  xo[:, :])
    P.barrier()
    st3.close()


def build_l2_attn_launch():
    P = Prog("l2attn")
    x_in = P.dram("x_in", [D, T], F32, kind="ExternalInput").ap()
    x_out = P.dram("x_out", [D, T], F32, kind="ExternalOutput").ap()
    w = {
        "Wq": P.dram("wWq", [1024, D], F32, kind="ExternalInput").ap(),
        "Wk": P.dram("wWk", [128, D], F32, kind="ExternalInput").ap(),
        "Wv": P.dram("wWv", [128, D], F32, kind="ExternalInput").ap(),
        "Wo": P.dram("wWo", [D, 1024], F32, kind="ExternalInput").ap(),
    }
    bs = ExitStack()
    emit_l2_attn(P, bs, x_in, x_out, w)
    bs.close()
    nc = P.assemble()
    return P, nc



def emit_l3_attn(P, bs, x_in, x_out, w):
    ones_bf = P.sbuf_const("ones128_bf",
                           np.ones((128, 128), dtype=np.float32)
                           .astype(_bf16np()))
    ones1f, t_o1 = P.sbuf_const("ones1x128_f32",
                                np.ones((1, 128), dtype=np.float32))
    CH = 128
    NCH = T // CH
    qf_hbm = P.dram("l3a_qf", [D, T], BF16)
    kf_hbm = P.dram("l3a_kf", [D, T], BF16)
    vf_hbm = P.dram("l3a_vf", [D, T], BF16)
    of_hbm = P.dram("l3a_of", [D, T], BF16)

    st = ExitStack()
    XB, xb_toks = emit_load_xb(P, st, x_in, name="l3a_xb")
    lnstack = ExitStack()
    emit_layernorm_inplace(P, lnstack, XB, xb_toks, ones_bf,
                           eps=1e-5, tag="l3a_ln")
    P.barrier()
    lnstack.close()
    bpp = emit_row_to_pp(P, st, "l3a_bq", w["b_qkv"], 3072)
    wsl = [P.sbuf(st, "l3a_wsl%d" % i, [128, 128], BF16) for i in range(8)]
    q_ps = P.psum(st, "l3a_qps", [128, TC])
    qbf = P.sbuf(st, "l3a_qbf", [128, TC], BF16)
    for rt in range(24):
        dst = (qf_hbm, kf_hbm, vf_hbm)[rt // 8]
        ro = rt % 8
        for tc in range(NTC):
            t0 = tc * TC
            for kc in range(NDC):
                P.dma("gpsimd", wsl[kc][:, :],
                      w["w_qkv"][kc * 128:(kc + 1) * 128,
                                 rt * 128:(rt + 1) * 128])
                P.op("tensor", "matmul", q_ps[:, :], wsl[kc][:, :],
                     XB[:, kc * T + t0:kc * T + t0 + TC],
                     start=(kc == 0), stop=(kc == NDC - 1))
            P.op("vector", "tensor_scalar", qbf[:, :], q_ps[:, :],
                 bpp[:, rt:rt + 1], None, op0=ALU.add)
            P.dma("gpsimd", dst.ap()[ro * 128:(ro + 1) * 128, t0:t0 + TC],
                  qbf[:, :])
    P.barrier()
    st.close()

    st2 = ExitStack()
    onesf2 = P.sbuf_const("ones128_f32", np.ones((128, 128),
                                                  dtype=np.float32))
    ones64T = P.sbuf_const("ones64T_bf",
                           np.ones((64, T), dtype=np.float32)
                           .astype(_bf16np()))
    trilm = P.sbuf_const("tril01",
                         (np.arange(128)[:, None] <=
                          np.arange(128)[None, :]).astype(np.float32))
    projb = P.sbuf(st2, "l3a_projb", [128, 64], BF16)
    qh = P.sbuf(st2, "l3a_qh", [64, T], BF16)
    kh = P.sbuf(st2, "l3a_kh", [64, T], BF16)
    vh = P.sbuf(st2, "l3a_vh", [64, T], BF16)
    pr_ps = P.psum(st2, "l3a_prps", [128, TC])
    qp = P.sbuf(st2, "l3a_qp", [64, T], BF16)
    kpf = P.sbuf(st2, "l3a_kpf", [64, T], BF16)
    kpt = P.sbuf(st2, "l3a_kpt", [128, NCH * 64], BF16)
    vtk = P.sbuf(st2, "l3a_vtk", [128, NCH * 64], BF16)
    tp_ps = P.psum(st2, "l3a_tpps", [128, 64], BF16)
    kcs = P.sbuf(st2, "l3a_kcs", [64, T], F32)
    qkc = P.sbuf(st2, "l3a_qkc", [64, T], F32)
    nrm_ps = P.psum(st2, "l3a_nrmps", [1, TC])
    nrm = P.sbuf(st2, "l3a_nrm", [1, T], F32)
    nrp_ps = P.psum(st2, "l3a_nrpps", [64, TC])
    nrp = P.sbuf(st2, "l3a_nrp", [64, T], F32)
    s_ps = P.psum(st2, "l3a_sps", [64, 64])
    S_sb = P.sbuf(st2, "l3a_S", [64, 64], F32)
    S_bf = P.sbuf(st2, "l3a_Sbf", [64, 64], BF16)
    at_ps = P.psum(st2, "l3a_atps", [128, CH])
    asb = P.sbuf(st2, "l3a_asb", [128, CH], F32)
    abf = [P.sbuf(st2, "l3a_abf%d" % i, [128, CH], BF16) for i in range(2)]
    ctx_ps = P.psum(st2, "l3a_ctxps", [64, CH])
    of_sc = P.sbuf(st2, "l3a_ofsc", [64, CH], F32)
    of_h = P.sbuf(st2, "l3a_ofh", [64, T], BF16)
    for h in range(16):
        hb = (h % 2) * 64
        P.dma("gpsimd", projb[0:64, :], w["proj"][h * 64:(h + 1) * 64, :])
        P.dma("gpsimd", projb[64:128, :], w["proj"][h * 64:(h + 1) * 64, :])
        pj = projb[0:64, :] if hb == 0 else projb[64:128, :]
        pj0 = projb[0:64, :]
        P.dma("sync", qh[:, :], qf_hbm.ap()[h * 64:(h + 1) * 64, :])
        P.dma("sync", kh[:, :], kf_hbm.ap()[h * 64:(h + 1) * 64, :])
        P.dma("sync", vh[:, :], vf_hbm.ap()[h * 64:(h + 1) * 64, :])
        for tc in range(NTC):
            t0 = tc * TC
            P.op("tensor", "matmul", pr_ps[0:64, :], pj0, qh[:, t0:t0 + TC],
                 start=True, stop=True)
            P.op("scalar", "activation", qp[:, t0:t0 + TC], pr_ps[0:64, :],
                 func=AF.Relu, scale=0.125)
            P.op("tensor", "matmul", pr_ps[0:64, :], pj0, kh[:, t0:t0 + TC],
                 start=True, stop=True)
            P.op("scalar", "activation", kpf[:, t0:t0 + TC], pr_ps[0:64, :],
                 func=AF.Relu, scale=0.125)
        for c in range(NCH):
            emit_transpose(P, tp_ps[:, :], kpf[:, c * CH:(c + 1) * CH], 64)
            P.op("vector", "tensor_copy", kpt[:, c * 64:(c + 1) * 64],
                 tp_ps[:, :])
            emit_transpose(P, tp_ps[:, :], vh[:, c * CH:(c + 1) * CH], 64)
            P.op("vector", "tensor_copy", vtk[:, c * 64:(c + 1) * 64],
                 tp_ps[:, :])
        P.op("vector", "tensor_tensor_scan", kcs[:, :], ones64T[0][:, :],
             kpf[:, :], 0.0, op0=ALU.mult, op1=ALU.add, waits=[ones64T[1]])
        P.op("vector", "tensor_mul", qkc[:, :], qp[:, :], kcs[:, :])
        for tc in range(NTC):
            t0 = tc * TC
            P.op("tensor", "matmul", nrm_ps[:, :], onesf2[0][0:64, 0:1],
                 qkc[:, t0:t0 + TC], start=True, stop=True,
                 waits=[onesf2[1]])
            P.op("vector", "tensor_scalar", nrm[:, t0:t0 + TC],
                 nrm_ps[:, :], 1e-6, None, op0=ALU.add)
            P.op("vector", "reciprocal", nrm[:, t0:t0 + TC],
                 nrm[:, t0:t0 + TC])
            P.op("tensor", "matmul", nrp_ps[:, :], ones1f[0:1, 0:64],
                 nrm[:, t0:t0 + TC], start=True, stop=True)
            P.op("vector", "tensor_copy", nrp[:, t0:t0 + TC], nrp_ps[:, :])
        P.op("vector", "memset", S_sb[:, :], 0.0)
        for c in range(NCH):
            c0 = c * CH
            P.op("vector", "tensor_copy", S_bf[:, :], S_sb[:, :])
            P.op("tensor", "matmul", at_ps[:, :], kpf[:, c0:c0 + CH],
                 qp[:, c0:c0 + CH], start=True, stop=True)
            P.op("vector", "tensor_mul", asb[:, :], at_ps[:, :],
                 trilm[0][:, :], waits=[trilm[1]])
            P.op("vector", "tensor_copy", abf[c % 2][:, :], asb[:, :])
            P.op("tensor", "matmul", ctx_ps[:, :],
                 vtk[:, c * 64:(c + 1) * 64], abf[c % 2][:, :],
                 start=True, stop=False)
            P.op("tensor", "matmul", ctx_ps[:, :], S_bf[:, :],
                 qp[:, c0:c0 + CH], start=False, stop=True)
            P.op("tensor", "matmul", s_ps[:, :], kpt[:, c * 64:(c + 1) * 64],
                 vtk[:, c * 64:(c + 1) * 64], start=True, stop=True)
            P.op("vector", "tensor_add", S_sb[:, :], S_sb[:, :],
                 s_ps[:, :])
            P.op("vector", "tensor_mul", of_sc[:, :], ctx_ps[:, :],
                 nrp[:, c0:c0 + CH])
            P.op("vector", "tensor_copy", of_h[:, c0:c0 + CH], of_sc[:, :])
        P.dma("gpsimd", of_hbm.ap()[h * 64:(h + 1) * 64, :], of_h[:, :])
    P.barrier()
    st2.close()

    st3 = ExitStack()
    bo = emit_row_to_pp(P, st3, "l3a_bo", w["b_out"], D)
    wsl3 = [P.sbuf(st3, "l3a_w3sl%d" % i, [128, 128], BF16) for i in range(8)]
    ofsl = [P.sbuf(st3, "l3a_ofsl%d" % i, [128, TC], BF16) for i in range(8)]
    ho_ps = P.psum(st3, "l3a_hops", [128, TC])
    xre = P.sbuf(st3, "l3a_xre", [128, TC], F32)
    xo = P.sbuf(st3, "l3a_xo", [128, TC], F32)
    for tc in range(NTC):
        t0 = tc * TC
        for kc in range(NDC):
            P.dma("sync", ofsl[kc][:, :],
                  of_hbm.ap()[kc * 128:(kc + 1) * 128, t0:t0 + TC])
        for dc in range(NDC):
            for kc in range(NDC):
                P.dma("gpsimd", wsl3[kc][:, :],
                      w["w_out"][kc * 128:(kc + 1) * 128,
                                 dc * 128:(dc + 1) * 128])
                P.op("tensor", "matmul", ho_ps[:, :], wsl3[kc][:, :],
                     ofsl[kc][:, :], start=(kc == 0), stop=(kc == NDC - 1))
            P.dma("sync", xre[:, :], x_in[dc * 128:(dc + 1) * 128,
                                          t0:t0 + TC])
            P.op("vector", "tensor_scalar", xo[:, :], ho_ps[:, :],
                 bo[:, dc:dc + 1], None, op0=ALU.add)
            P.op("vector", "tensor_add", xo[:, :], xo[:, :], xre[:, :])
            P.dma("gpsimd", x_out[dc * 128:(dc + 1) * 128, t0:t0 + TC],
                  xo[:, :])
    P.barrier()
    st3.close()


def build_l3_attn_launch():
    P = Prog("l3attn")
    x_in = P.dram("x_in", [D, T], F32, kind="ExternalInput").ap()
    x_out = P.dram("x_out", [D, T], F32, kind="ExternalOutput").ap()
    w = {
        "w_qkv": P.dram("wqkv", [D, 3072], F32, kind="ExternalInput").ap(),
        "b_qkv": P.dram("wbqkv", [1, 3072], F32, kind="ExternalInput").ap(),
        "w_out": P.dram("wout", [D, D], F32, kind="ExternalInput").ap(),
        "b_out": P.dram("wbout", [1, D], F32, kind="ExternalInput").ap(),
        "proj": P.dram("wproj", [16 * 64, 64], F32, kind="ExternalInput").ap(),
    }
    bs = ExitStack()
    emit_l3_attn(P, bs, x_in, x_out, w)
    bs.close()
    nc = P.assemble()
    return P, nc



def emit_l1_attn(P, bs, x_in, x_out, w):
    ones_bf = P.sbuf_const("ones128_bf",
                           np.ones((128, 128), dtype=np.float32)
                           .astype(_bf16np()))
    onesf2 = P.sbuf_const("ones128_f32", np.ones((128, 128),
                                                 dtype=np.float32))
    ones1f, t_o1 = P.sbuf_const("ones1x128_f32",
                                np.ones((1, 128), dtype=np.float32))
    idc = P.sbuf_const("ident128_f32", np.eye(128, dtype=np.float32))

    st = ExitStack()
    wsl = [P.sbuf(bs, "l1a_wsl%d" % i, [128, 128], BF16) for i in range(8)]
    ql = P.sbuf(bs, "l1a_ql", [128, 8 * 64], BF16)
    kl = P.sbuf(bs, "l1a_kl", [128, 8 * 64], BF16)
    qn2 = P.sbuf(bs, "l1a_qn2", [2, 8], F32)
    kn2 = P.sbuf(bs, "l1a_kn2", [2, 8], F32)
    qrow = P.sbuf(bs, "l1a_qrow", [1, 16], F32)
    krow = P.sbuf(bs, "l1a_krow", [1, 16], F32)
    negm = P.sbuf(bs, "l1a_negm", [1, 16], F32)
    mhat = P.sbuf(bs, "l1a_mhat", [128, 16], F32)
    XB, xb_toks = emit_load_xb(P, st, x_in, name="l1a_xb")
    lnstack = ExitStack()
    emit_layernorm_inplace(P, lnstack, XB, xb_toks, ones_bf,
                           eps=1e-5, tag="l1a_ln")
    P.barrier()
    lnstack.close()

    qf_hbm = P.dram("l1a_qfh", [D, T], BF16)
    kf_hbm = P.dram("l1a_kfh", [D, T], BF16)
    vf_hbm = P.dram("l1a_vfh", [D, T], BF16)
    of_hbm = P.dram("l1a_of", [D, T], BF16)
    s3_hbm = P.dram("l1a_s3h", [64, T], F32)
    q_ps = P.psum(st, "l1a_qps", [128, TC])
    qtile = P.sbuf(st, "l1a_qtile", [128, TC], BF16)
    lred = P.sbuf(st, "l1a_lred", [128, 8], F32)
    lacc = P.sbuf(st, "l1a_lacc", [128, 64], F32)
    sqb = P.sbuf(st, "l1a_sqb", [128, TC], BF16)
    nq_ps = P.psum(st, "l1a_nqps", [2, TC])
    nred = P.sbuf(st, "l1a_nred", [2, 1], F32)
    sel2 = P.sbuf_const("sel2h", np.kron(np.eye(2), np.ones((64, 1)))
                        .astype(np.float32).astype(_bf16np()))
    for (w_ap, dsth, scl, ln_dst, do_n) in (
            (w["Wq"], qf_hbm, 0.125, ql, True),
            (w["Wk"], kf_hbm, 1.0, kl, False),
            (w["Wv"], vf_hbm, 1.0, None, False)):
        for rt in range(8):
            for tc in range(NTC):
                t0 = tc * TC
                for kc in range(NDC):
                    P.dma("gpsimd", wsl[kc][:, :],
                          w_ap[kc * 128:(kc + 1) * 128,
                               rt * 128:(rt + 1) * 128])
                    P.op("tensor", "matmul", q_ps[:, :], wsl[kc][:, :],
                         XB[:, kc * T + t0:kc * T + t0 + TC],
                         start=(kc == 0), stop=(kc == NDC - 1))
                P.op("scalar", "activation", qtile[:, :], q_ps[:, :],
                     func=AF.Identity, scale=float(scl))
                P.dma("gpsimd", dsth.ap()[rt * 128:(rt + 1) * 128,
                                          t0:t0 + TC], qtile[:, :])
                if ln_dst is not None:
                    P.op("vector", "tensor_reduce", lred[:, :],
                         qtile[:, :].rearrange("p (l j) -> p l j", j=64),
                         axis=AX.X, op=ALU.add)
                    P.op("vector", "tensor_scalar_mul",
                         lacc[:, tc * 8:(tc + 1) * 8], lred[:, :],
                         1.0 / 64)
                if do_n:
                    P.op("scalar", "activation", sqb[:, :], qtile[:, :],
                         func=AF.Square)
                    P.op("tensor", "matmul", nq_ps[:, :], sel2[0][:, :],
                         sqb[:, :], start=True, stop=True,
                         waits=[sel2[1]])
                    P.op("vector", "tensor_reduce", nred[:, :],
                         nq_ps[:, :], axis=AX.X, op=ALU.max)
                    if tc == 0:
                        P.op("vector", "tensor_copy", qn2[:, rt:rt + 1],
                             nred[:, :])
                    else:
                        P.op("vector", "tensor_max", qn2[:, rt:rt + 1],
                             qn2[:, rt:rt + 1], nred[:, :])
            if ln_dst is not None:
                P.op("vector", "tensor_copy",
                     ln_dst[:, rt * 64:(rt + 1) * 64], lacc[:, :])
    for rt in range(8):
        P.op("scalar", "activation", sqb[:, 0:64],
             kl[:, rt * 64:(rt + 1) * 64], func=AF.Square)
        P.op("tensor", "matmul", nq_ps[:, 0:64], sel2[0][:, :],
             sqb[:, 0:64], start=True, stop=True)
        P.op("vector", "tensor_reduce", kn2[:, rt:rt + 1], nq_ps[:, 0:64],
             axis=AX.X, op=ALU.max)
    nq_hbm = P.dram("l1a_nqh", [2, 8], F32)
    nk_hbm = P.dram("l1a_nkh", [2, 8], F32)
    P.dma("sync", nq_hbm.ap()[:, :], qn2[:, :])
    P.dma("sync", nk_hbm.ap()[:, :], kn2[:, :])
    P.dma("sync", qrow[0:1, :].rearrange("a (q r) -> a q r", r=2),
          nq_hbm.ap().rearrange("r q -> q r")[None, :, :],
          allow_slow_non_contiguous=True)
    P.dma("sync", krow[0:1, :].rearrange("a (q r) -> a q r", r=2),
          nk_hbm.ap().rearrange("r q -> q r")[None, :, :],
          allow_slow_non_contiguous=True)
    P.op("vector", "tensor_mul", negm[:, :], qrow[:, :], krow[:, :])
    P.op("scalar", "activation", negm[:, :], negm[:, :], func=AF.Sqrt)
    P.op("vector", "tensor_scalar_mul", negm[:, :], negm[:, :], -1.02)
    mh_ps = P.psum(st, "l1a_mhps", [128, 16])
    P.op("tensor", "matmul", mh_ps[:, :], ones1f[0:1, :], negm[:, :],
         start=True, stop=True, waits=[t_o1])
    P.op("vector", "tensor_copy", mhat[:, :], mh_ps[:, :])
    P.barrier()
    st.close()

    st = ExitStack()
    qfh = P.sbuf(st, "l1a_qfh_sb", [64, T], BF16)
    kfh = P.sbuf(st, "l1a_kfh_sb", [64, T], BF16)
    vfh = P.sbuf(st, "l1a_vfh_sb", [64, T], BF16)
    s2p = P.psum(st, "l1a_s2p", [64, 64])
    s2f = P.sbuf(st, "l1a_s2f", [64, 64], F32)
    s2T = P.sbuf(st, "l1a_s2T", [64, 64], BF16)
    zf = P.sbuf(st, "l1a_zf", [64, 64], F32)
    zb = P.sbuf(st, "l1a_zb", [64, 64], BF16)
    zT = P.sbuf(st, "l1a_zT", [64, 64], BF16)
    xzT = P.sbuf(st, "l1a_xzT", [64, 64], BF16)
    wb = P.sbuf(st, "l1a_wb", [64, 64], BF16)
    scw = P.sbuf(st, "l1a_scw", [64, 64], F32)
    red1 = P.sbuf(st, "l1a_red1", [64, 1], F32)
    red2 = P.sbuf(st, "l1a_red2", [1, 64], F32)
    sc11 = P.sbuf(st, "l1a_sc11", [1, 2], F32)
    rep64 = P.sbuf(st, "l1a_rep64", [64, 1], F32)
    tp_ps = P.psum(st, "l1a_tpps", [128, 128])
    e3 = P.sbuf(st, "l1a_e3", [64, T], BF16)
    s3f = P.sbuf(st, "l1a_s3f", [64, TC], F32)
    den3 = P.sbuf(st, "l1a_den3", [64, 1], F32)
    et = P.sbuf(st, "l1a_et", [128, 32 * 64], BF16)
    vtk = P.sbuf(st, "l1a_vtk", [128, 32 * 64], BF16)
    svb = P.sbuf(st, "l1a_svb", [64, 64], BF16)
    zvb = P.sbuf(st, "l1a_zvb", [64, 64], BF16)
    sv_ps = s2p
    zv_ps = s2p
    tp_bs = P.psum(st, "l1a_tpbs", [128, 128], BF16)
    e1_ps = P.psum(st, "l1a_e1ps", [64, TC])
    e1 = P.sbuf(st, "l1a_e1", [64, TC], BF16)
    d1_ps = P.psum(st, "l1a_d1ps", [1, TC])
    d1 = P.sbuf(st, "l1a_d1", [1, TC], F32)
    r1_ps = e1_ps
    r1 = P.sbuf(st, "l1a_r1", [64, TC], F32)
    o_ps = P.psum(st, "l1a_ops", [64, TC])
    osb = P.sbuf(st, "l1a_osb", [64, TC], F32)
    ofh = P.sbuf(st, "l1a_ofh", [64, T], BF16)
    qlh = P.sbuf(st, "l1a_qlh", [64, 64], BF16)
    klh = P.sbuf(st, "l1a_klh", [64, 64], BF16)
    for h in range(16):
        rt, hb = divmod(h, 2)
        r0 = hb * 64
        P.op("vector", "tensor_copy", qlh[:, :],
             ql[r0:r0 + 64, rt * 64:(rt + 1) * 64])
        P.op("vector", "tensor_copy", klh[:, :],
             kl[r0:r0 + 64, rt * 64:(rt + 1) * 64])
        P.dma("sync", qfh[:, :], qf_hbm.ap()[h * 64:(h + 1) * 64, :])
        P.dma("sync", kfh[:, :], kf_hbm.ap()[h * 64:(h + 1) * 64, :])
        P.dma("sync", vfh[:, :], vf_hbm.ap()[h * 64:(h + 1) * 64, :])
        P.op("tensor", "matmul", s2p[:, :], qlh[:, :], klh[:, :],
             start=True, stop=True)
        P.op("vector", "tensor_reduce", red1[:, :], s2p[:, :], axis=AX.X,
             op=ALU.max, negate=True)
        P.op("scalar", "activation", s2f[:, :], s2p[:, :], func=AF.Exp,
             bias=red1[:, :])
        P.op("vector", "reduce_sum", red1[:, :], s2f[:, :], axis=AX.X)
        P.op("vector", "reciprocal", red1[:, :], red1[:, :])
        P.op("vector", "tensor_scalar", s2f[:, :], s2f[:, :], red1[:, :],
             None, op0=ALU.mult)
        P.op("scalar", "activation", scw[:, :], s2f[:, :], func=AF.Abs)
        P.op("vector", "reduce_sum", red1[:, :], scw[:, :], axis=AX.X)
        emit_transpose(P, tp_ps[0:1, 0:64], red1[:, 0:1], 64, bf16=False)
        P.op("vector", "tensor_reduce", sc11[:, 0:1], tp_ps[0:1, 0:64],
             axis=AX.X, op=ALU.max)
        P.op("tensor", "matmul", tp_ps[0:1, 0:64], onesf2[0][0:64, 0:1],
             scw[:, :], start=True, stop=True, waits=[onesf2[1]])
        P.op("vector", "tensor_reduce", sc11[:, 1:2], tp_ps[0:1, 0:64],
             axis=AX.X, op=ALU.max)
        P.op("vector", "tensor_mul", sc11[:, 0:1], sc11[:, 0:1],
             sc11[:, 1:2])
        P.op("vector", "reciprocal", sc11[:, 0:1], sc11[:, 0:1])
        P.op("tensor", "matmul", tp_ps[0:64, 0:1], ones1f[0:1, 0:64],
             sc11[:, 0:1], start=True, stop=True)
        P.op("vector", "tensor_copy", rep64[:, :], tp_ps[0:64, 0:1])
        P.op("vector", "tensor_copy", s2T[:, :], s2f[:, :])
        emit_transpose(P, tp_bs[0:64, 0:64], s2T[:, :], 64)
        P.op("vector", "tensor_scalar", zf[:, :], tp_bs[0:64, 0:64],
             rep64[:, :], None, op0=ALU.mult)
        P.op("vector", "tensor_copy", zb[:, :], zf[:, :])
        id7 = idc[0][0:64, 0:64]
        for it in range(6):
            P.op("tensor", "matmul", tp_ps[0:64, 0:64], s2T[:, :],
                 zb[:, :], start=True, stop=True)
            P.op("vector", "tensor_copy", scw[:, :], tp_ps[0:64, 0:64])
            P.op("vector", "tensor_copy", wb[:, :], scw[:, :])
            emit_transpose(P, tp_bs[64:128, 0:64], wb[:, :], 64)
            P.op("vector", "tensor_copy", xzT[:, :], tp_bs[64:128, 0:64])
            P.op("vector", "scalar_tensor_tensor", scw[:, :], idc[0][0:64, 0:64],
                 7.0, scw[:, :], op0=ALU.mult, op1=ALU.subtract,
                 waits=[idc[1]])
            P.op("vector", "tensor_copy", wb[:, :], scw[:, :])
            P.op("tensor", "matmul", tp_ps[0:64, 0:64], xzT[:, :],
                 wb[:, :], start=True, stop=True)
            P.op("vector", "scalar_tensor_tensor", scw[:, :],
                 idc[0][0:64, 0:64], 15.0, tp_ps[0:64, 0:64],
                 op0=ALU.mult, op1=ALU.subtract)
            P.op("vector", "tensor_copy", wb[:, :], scw[:, :])
            P.op("tensor", "matmul", tp_ps[0:64, 0:64], xzT[:, :],
                 wb[:, :], start=True, stop=True)
            P.op("vector", "scalar_tensor_tensor", scw[:, :],
                 idc[0][0:64, 0:64], 13.0, tp_ps[0:64, 0:64],
                 op0=ALU.mult, op1=ALU.subtract)
            P.op("vector", "tensor_copy", wb[:, :], scw[:, :])
            emit_transpose(P, tp_bs[64:128, 0:64], zb[:, :], 64)
            P.op("vector", "tensor_copy", zT[:, :], tp_bs[64:128, 0:64])
            P.op("tensor", "matmul", tp_ps[0:64, 0:64], zT[:, :], wb[:, :],
                 start=True, stop=True)
            P.op("vector", "tensor_scalar_mul", zf[:, :],
                 tp_ps[0:64, 0:64], 0.25)
            P.op("vector", "tensor_copy", zb[:, :], zf[:, :])
        for tc in range(NTC):
            t0 = tc * TC
            P.op("tensor", "matmul", e1_ps[:, :], qlh[:, :],
                 kfh[:, t0:t0 + TC], start=True, stop=True)
            P.op("vector", "tensor_copy", s3f[:, :], e1_ps[:, :])
            if tc == 0:
                P.op("vector", "tensor_reduce", den3[:, :], s3f[:, :],
                     axis=AX.X, op=ALU.max)
            else:
                P.op("vector", "tensor_reduce", red1[:, :], s3f[:, :],
                     axis=AX.X, op=ALU.max)
                P.op("vector", "tensor_max", den3[:, :], den3[:, :],
                     red1[:, :])
            P.dma("gpsimd", s3_hbm.ap()[0:64, t0:t0 + TC], s3f[:, :])
        P.op("vector", "tensor_scalar_mul", den3[:, :], den3[:, :], -1.0)
        for tc in range(NTC):
            t0 = tc * TC
            P.dma("sync", s3f[:, :], s3_hbm.ap()[0:64, t0:t0 + TC])
            P.op("scalar", "activation", e3[:, t0:t0 + TC], s3f[:, :],
                 func=AF.Exp, bias=den3[:, :],
                 accum_out=red1[:, :])
            if tc == 0:
                P.op("vector", "tensor_copy", den3[:, :].rearrange(
                    "p a -> p a") if False else red2[0:1, 0:1], red1[0:1, 0:1])
            if tc == 0:
                P.op("vector", "tensor_copy", scw[:, 0:1], red1[:, :])
            else:
                P.op("vector", "tensor_add", scw[:, 0:1], scw[:, 0:1],
                     red1[:, :])
        P.op("vector", "reciprocal", scw[:, 0:1], scw[:, 0:1])
        for c in range(32):
            emit_transpose(P, tp_bs[:, 0:64], e3[:, c * 128:(c + 1) * 128],
                           64)
            P.op("vector", "tensor_copy", et[:, c * 64:(c + 1) * 64],
                 tp_bs[:, 0:64])
            emit_transpose(P, tp_bs[:, 0:64],
                           vfh[:, c * 128:(c + 1) * 128], 64)
            P.op("vector", "tensor_copy", vtk[:, c * 64:(c + 1) * 64],
                 tp_bs[:, 0:64])
        for c in range(32):
            P.op("tensor", "matmul", sv_ps[:, :], et[:, c * 64:(c + 1) * 64],
                 vtk[:, c * 64:(c + 1) * 64], start=(c == 0), stop=(c == 31))
        P.op("vector", "tensor_scalar", scw[:, 1:2].rearrange("p a -> p a")
             if False else svb[:, :], sv_ps[:, :], scw[:, 0:1], None,
             op0=ALU.mult)
        emit_transpose(P, tp_bs[64:128, 0:64], zb[:, :], 64)
        P.op("vector", "tensor_copy", zT[:, :], tp_bs[64:128, 0:64])
        P.op("tensor", "matmul", zv_ps[:, :], zT[:, :], svb[:, :],
             start=True, stop=True)
        P.op("vector", "tensor_copy", zvb[:, :], zv_ps[:, :])
        for tc in range(NTC):
            t0 = tc * TC
            P.op("tensor", "matmul", e1_ps[:, :], klh[:, :],
                 qfh[:, t0:t0 + TC], start=True, stop=True)
            P.op("scalar", "activation", e1[:, :], e1_ps[:, :],
                 func=AF.Exp, bias=mhat[0:64, h:h + 1])
            P.op("tensor", "matmul", d1_ps[:, :], ones_bf[0][0:64, 0:1],
                 e1[:, :], start=True, stop=True, waits=[ones_bf[1]])
            P.op("vector", "reciprocal", d1[:, :], d1_ps[:, :])
            P.op("tensor", "matmul", r1_ps[:, :], ones1f[0:1, 0:64],
                 d1[:, :], start=True, stop=True)
            P.op("vector", "tensor_copy", r1[:, :], r1_ps[:, :])
            P.op("tensor", "matmul", o_ps[:, :], zvb[:, :], e1[:, :],
                 start=True, stop=True)
            P.op("vector", "tensor_mul", osb[:, :], o_ps[:, :], r1[:, :])
            P.op("vector", "tensor_copy", ofh[:, t0:t0 + TC], osb[:, :])
        P.dma("gpsimd", of_hbm.ap()[h * 64:(h + 1) * 64, :], ofh[:, :])
    P.barrier()

    st.close()
    st = ExitStack()
    bo = emit_row_to_pp(P, st, "l1a_bo", w["bo"], D)
    ofsl = [P.sbuf(st, "l1a_ofsl%d" % i, [128, TC], BF16) for i in range(8)]
    ho_ps = P.psum(st, "l1a_hops", [128, TC])
    xre = P.sbuf(st, "l1a_xre", [128, TC], F32)
    xo = P.sbuf(st, "l1a_xo", [128, TC], F32)
    for tc in range(NTC):
        t0 = tc * TC
        for kc in range(NDC):
            P.dma("sync", ofsl[kc][:, :],
                  of_hbm.ap()[kc * 128:(kc + 1) * 128, t0:t0 + TC])
        for dc in range(NDC):
            for kc in range(NDC):
                P.dma("gpsimd", wsl[kc][:, :],
                      w["Wo"][kc * 128:(kc + 1) * 128,
                              dc * 128:(dc + 1) * 128])
                P.op("tensor", "matmul", ho_ps[:, :], wsl[kc][:, :],
                     ofsl[kc][:, :], start=(kc == 0), stop=(kc == NDC - 1))
            P.dma("sync", xre[:, :], x_in[dc * 128:(dc + 1) * 128,
                                          t0:t0 + TC])
            P.op("vector", "tensor_scalar", xo[:, :], ho_ps[:, :],
                 bo[:, dc:dc + 1], None, op0=ALU.add)
            P.op("vector", "tensor_add", xo[:, :], xo[:, :], xre[:, :])
            P.dma("gpsimd", x_out[dc * 128:(dc + 1) * 128, t0:t0 + TC],
                  xo[:, :])
    P.barrier()
    st.close()


def build_l1_attn_launch():
    P = Prog("l1attn")
    x_in = P.dram("x_in", [D, T], F32, kind="ExternalInput").ap()
    x_out = P.dram("x_out", [D, T], F32, kind="ExternalOutput").ap()
    w = {
        "Wk": P.dram("wWk", [D, D], F32, kind="ExternalInput").ap(),
        "Wq": P.dram("wWq", [D, D], F32, kind="ExternalInput").ap(),
        "Wv": P.dram("wWv", [D, D], F32, kind="ExternalInput").ap(),
        "Wo": P.dram("wWo", [D, D], F32, kind="ExternalInput").ap(),
        "bo": P.dram("wbo", [1, D], F32, kind="ExternalInput").ap(),
    }
    bs = ExitStack()
    emit_l1_attn(P, bs, x_in, x_out, w)
    bs.close()
    nc = P.assemble()
    return P, nc



def emit_l0_attn(P, bs, x_in, x_out, w):
    ones_bf = P.sbuf_const("ones128_bf",
                           np.ones((128, 128), dtype=np.float32)
                           .astype(_bf16np()))
    DI = 2048
    NDI = DI // 128
    u_hbm = P.dram("l0a_u", [DI, T], F32)
    ubf_hbm = P.dram("l0a_ubf", [DI, T], BF16)
    res_hbm = P.dram("l0a_res", [DI, T], F32)
    dlt_hbm = P.dram("l0a_dlt", [DI, T], F32)
    du_hbm = P.dram("l0a_du", [DI, T], F32)
    y_hbm = P.dram("l0a_y", [DI, T], BF16)

    st = ExitStack()
    XB, xb_toks = emit_load_xb(P, st, x_in, name="l0a_xb")
    lnstack = ExitStack()
    emit_layernorm_inplace(P, lnstack, XB, xb_toks, ones_bf,
                           eps=1e-5, tag="l0a_ln")
    P.barrier()
    lnstack.close()
    gpp = emit_row_to_pp(P, st, "l0a_nw", w["norm_w"], D)
    r1stack = ExitStack()
    emit_rmsnorm_inplace(P, r1stack, XB, ones_bf, 1e-5, gpp, "l0a_r1")
    P.barrier()
    r1stack.close()

    wsl = [P.sbuf(st, "l0a_wsl%d" % i, [128, 128], BF16) for i in range(8)]
    q_ps = P.psum(st, "l0a_qps", [128, TC])
    cw_pp = P.sbuf(st, "l0a_cw", [128, 4], F32)
    cb_pp = P.sbuf(st, "l0a_cb", [128, 1], F32)
    upad = P.sbuf(st, "l0a_upad", [128, T + 3], F32)
    ucv = P.sbuf(st, "l0a_ucv", [128, T], F32)
    ubf = P.sbuf(st, "l0a_ubf_sb", [128, T], BF16)
    rsb = P.sbuf(st, "l0a_rsb", [128, TC], F32)
    for rt in range(NDI):
        P.dma("sync", cw_pp[:, :], w["conv_w"][rt * 128:(rt + 1) * 128, :])
        P.dma("sync", cb_pp[:, :],
              w["conv_b"][rt * 128:(rt + 1) * 128, 0:1])
        P.op("vector", "memset", upad[:, 0:3], 0.0)
        for tc in range(NTC):
            t0 = tc * TC
            for kc in range(NDC):
                P.dma("gpsimd", wsl[kc][:, :],
                      w["in_proj_w"][kc * 128:(kc + 1) * 128,
                                     rt * 128:(rt + 1) * 128])
                P.op("tensor", "matmul", q_ps[:, :], wsl[kc][:, :],
                     XB[:, kc * T + t0:kc * T + t0 + TC],
                     start=(kc == 0), stop=(kc == NDC - 1))
            P.op("scalar", "activation", upad[:, 3 + t0:3 + t0 + TC],
                 q_ps[:, :], func=AF.Identity)
        P.op("vector", "tensor_scalar", ucv[:, :], upad[:, 0:T],
             cw_pp[:, 0:1], None, op0=ALU.mult)
        for kk in range(1, 4):
            P.op("vector", "scalar_tensor_tensor", ucv[:, :],
                 upad[:, kk:kk + T], cw_pp[:, kk:kk + 1], ucv[:, :],
                 op0=ALU.mult, op1=ALU.add)
        P.op("scalar", "activation", ucv[:, :], ucv[:, :], func=AF.Silu,
             bias=cb_pp[:, :])
        P.dma("gpsimd", u_hbm.ap()[rt * 128:(rt + 1) * 128, :], ucv[:, :])
        P.op("vector", "tensor_copy", ubf[:, :], ucv[:, :])
        P.dma("gpsimd", ubf_hbm.ap()[rt * 128:(rt + 1) * 128, :],
              ubf[:, :])
    for rt in range(NDI):
        for tc in range(NTC):
            t0 = tc * TC
            for kc in range(NDC):
                P.dma("gpsimd", wsl[kc][:, :],
                      w["in_proj_w"][kc * 128:(kc + 1) * 128,
                                     2048 + rt * 128:2048 + (rt + 1) * 128])
                P.op("tensor", "matmul", q_ps[:, :], wsl[kc][:, :],
                     XB[:, kc * T + t0:kc * T + t0 + TC],
                     start=(kc == 0), stop=(kc == NDC - 1))
            P.op("vector", "tensor_copy", rsb[:, :], q_ps[:, :])
            P.dma("gpsimd", res_hbm.ap()[rt * 128:(rt + 1) * 128,
                                         t0:t0 + TC], rsb[:, :])
    P.barrier()
    st.close()

    st = ExitStack()
    dtbf = P.sbuf(bs, "l0a_dtbf", [64, T], BF16)
    Bbf = P.sbuf(bs, "l0a_Bbf", [16, T], BF16)
    Cbf = P.sbuf(bs, "l0a_Cbf", [16, T], BF16)
    A_sb = P.sbuf(bs, "l0a_A", [128, 16 * 16], F32)
    D_pp = P.sbuf(bs, "l0a_D", [128, 16], F32)
    xpw = [P.sbuf(st, "l0a_xpw%d" % i, [128, 96], BF16) for i in range(16)]
    for kc in range(16):
        P.dma("gpsimd", xpw[kc][:, :],
              w["x_proj_w"][kc * 128:(kc + 1) * 128, :])
    usl = [P.sbuf(st, "l0a_usl%d" % i, [128, TC], BF16) for i in range(16)]
    dt_ps = P.psum(st, "l0a_dtps", [64, TC])
    b_ps = P.psum(st, "l0a_bps", [16, TC])
    c_ps = P.psum(st, "l0a_cps", [16, TC])
    for tc in range(NTC):
        t0 = tc * TC
        for kc in range(16):
            P.dma("sync", usl[kc][:, :],
                  ubf_hbm.ap()[kc * 128:(kc + 1) * 128, t0:t0 + TC])
            P.op("tensor", "matmul", dt_ps[:, :], xpw[kc][:, 0:64],
                 usl[kc][:, :], start=(kc == 0), stop=(kc == 15))
            P.op("tensor", "matmul", b_ps[:, :], xpw[kc][:, 64:80],
                 usl[kc][:, :], start=(kc == 0), stop=(kc == 15))
            P.op("tensor", "matmul", c_ps[:, :], xpw[kc][:, 80:96],
                 usl[kc][:, :], start=(kc == 0), stop=(kc == 15))
        P.op("vector", "tensor_copy", dtbf[:, t0:t0 + TC], dt_ps[:, :])
        P.op("vector", "tensor_copy", Bbf[:, t0:t0 + TC], b_ps[:, :])
        P.op("vector", "tensor_copy", Cbf[:, t0:t0 + TC], c_ps[:, :])
    dpw = [P.sbuf(st, "l0a_dpw%d" % i, [64, 128], BF16) for i in range(2)]
    dpb = P.sbuf(st, "l0a_dpb", [128, 1], F32)
    d_ps = P.psum(st, "l0a_dps", [128, TC])
    sabs = P.sbuf(st, "l0a_sabs", [128, TC], F32)
    srel = P.sbuf(st, "l0a_srel", [128, TC], F32)
    sdel = P.sbuf(st, "l0a_sdel", [128, TC], F32)
    udu = P.sbuf(st, "l0a_udu", [128, TC], F32)
    for rt in range(NDI):
        i = rt % 2
        P.dma("gpsimd", dpw[i][:, :],
              w["dt_proj_w"][:, rt * 128:(rt + 1) * 128])
        P.dma("sync", dpb[:, :],
              w["dt_proj_b"][rt * 128:(rt + 1) * 128, 0:1])
        for tc in range(NTC):
            t0 = tc * TC
            P.op("tensor", "matmul", d_ps[:, :], dpw[i][:, :],
                 dtbf[:, t0:t0 + TC], start=True, stop=True)
            P.op("scalar", "activation", sabs[:, :], d_ps[:, :],
                 func=AF.Abs, bias=dpb[:, :])
            P.op("scalar", "activation", sabs[:, :], sabs[:, :],
                 func=AF.Exp, scale=-1.0)
            P.op("vector", "tensor_scalar_add", sabs[:, :], sabs[:, :],
                 1.0)
            P.op("scalar", "activation", sabs[:, :], sabs[:, :],
                 func=AF.Ln)
            P.op("vector", "tensor_scalar", srel[:, :], d_ps[:, :],
                 dpb[:, :], 0.0, op0=ALU.add, op1=ALU.max)
            P.op("vector", "tensor_add", sdel[:, :], sabs[:, :],
                 srel[:, :])
            P.dma("gpsimd", dlt_hbm.ap()[rt * 128:(rt + 1) * 128,
                                         t0:t0 + TC], sdel[:, :])
            P.dma("sync", udu[:, :],
                  u_hbm.ap()[rt * 128:(rt + 1) * 128, t0:t0 + TC])
            P.op("vector", "tensor_mul", udu[:, :], udu[:, :], sdel[:, :])
            P.dma("gpsimd", du_hbm.ap()[rt * 128:(rt + 1) * 128,
                                        t0:t0 + TC], udu[:, :])
    at = P.sbuf(st, "l0a_at", [128, 16], F32)
    for dblk in range(16):
        P.dma("sync", at[:, :],
              w["A_log"][dblk * 128:(dblk + 1) * 128, :])
        P.op("scalar", "activation", at[:, :], at[:, :], func=AF.Exp)
        P.op("vector", "tensor_scalar_mul",
             A_sb[:, dblk * 16:(dblk + 1) * 16], at[:, :], -1.0)
    for dblk in range(16):
        P.dma("sync", D_pp[:, dblk:dblk + 1],
              w["Dv"][dblk * 128:(dblk + 1) * 128, 0:1])
    P.barrier()
    st.close()

    st = ExitStack()
    sels = []
    for s in range(16):
        m = np.zeros((16, 128), dtype=np.float32)
        m[s, :] = 1.0
        sels.append(P.sbuf_const("sel_s%d" % s, m.astype(_bf16np())))
    state = P.sbuf(st, "l0a_state", [128, 256], F32)
    P.op("vector", "memset", state[:, :], 0.0)
    hbuf = P.sbuf(st, "l0a_hbuf", [128, 16 * TC], F32)
    dlt_t = [P.sbuf(st, "l0a_dlt%d" % i, [128, TC], F32) for i in range(2)]
    du_t = [P.sbuf(st, "l0a_du%d" % i, [128, TC], F32) for i in range(2)]
    B_ps = [P.psum(st, "l0a_Bps%d" % i, [128, TC]) for i in range(2)]
    C_ps = [P.psum(st, "l0a_Cps%d" % i, [128, TC]) for i in range(2)]
    dA = [P.sbuf(st, "l0a_dA%d" % i, [128, TC], F32) for i in range(2)]
    dBu = [P.sbuf(st, "l0a_dBu%d" % i, [128, TC], F32) for i in range(2)]
    hC = [P.sbuf(st, "l0a_hC%d" % i, [128, TC], F32) for i in range(2)]
    y_acc = P.sbuf(st, "l0a_yacc", [128, TC], F32)
    u_t = P.sbuf(st, "l0a_ut", [128, TC], F32)
    res_t = P.sbuf(st, "l0a_rest", [128, TC], F32)
    sil = P.sbuf(st, "l0a_sil", [128, TC], F32)
    ybf = P.sbuf(st, "l0a_ybf", [128, TC], BF16)
    for tc in range(NTC):
        t0 = tc * TC
        for dblk in range(16):
            i = dblk % 2
            P.dma("sync", dlt_t[i][:, :],
                  dlt_hbm.ap()[dblk * 128:(dblk + 1) * 128, t0:t0 + TC])
            P.dma("sync", du_t[i][:, :],
                  du_hbm.ap()[dblk * 128:(dblk + 1) * 128, t0:t0 + TC])
            for s in range(16):
                j = s % 2
                P.op("tensor", "matmul", B_ps[j][:, :], sels[s][0][:, :],
                     Bbf[:, t0:t0 + TC], start=True, stop=True,
                     waits=[sels[s][1]])
                P.op("tensor", "matmul", C_ps[j][:, :], sels[s][0][:, :],
                     Cbf[:, t0:t0 + TC], start=True, stop=True)
                P.op("scalar", "activation", dA[j][:, :], dlt_t[i][:, :],
                     func=AF.Exp,
                     scale=A_sb[:, dblk * 16 + s:dblk * 16 + s + 1])
                P.op("vector", "tensor_mul", dBu[j][:, :], du_t[i][:, :],
                     B_ps[j][:, :])
                P.op("vector", "tensor_tensor_scan",
                     hbuf[:, s * TC:(s + 1) * TC], dA[j][:, :],
                     dBu[j][:, :],
                     state[:, dblk * 16 + s:dblk * 16 + s + 1],
                     op0=ALU.mult, op1=ALU.add)
                P.op("vector", "tensor_mul", hC[j][:, :],
                     hbuf[:, s * TC:(s + 1) * TC], C_ps[j][:, :])
                if s == 0:
                    P.op("gpsimd", "tensor_copy", y_acc[:, :], hC[j][:, :])
                else:
                    P.op("gpsimd", "tensor_add", y_acc[:, :], y_acc[:, :],
                         hC[j][:, :])
            P.op("vector", "tensor_copy",
                 state[:, dblk * 16:(dblk + 1) * 16],
                 hbuf[:, :].rearrange("p (s t) -> p s t", t=TC)
                 [:, :, TC - 1:TC])
            P.dma("sync", u_t[:, :],
                  u_hbm.ap()[dblk * 128:(dblk + 1) * 128, t0:t0 + TC])
            P.dma("sync", res_t[:, :],
                  res_hbm.ap()[dblk * 128:(dblk + 1) * 128, t0:t0 + TC])
            P.op("vector", "tensor_scalar", u_t[:, :], u_t[:, :],
                 D_pp[:, dblk:dblk + 1], None, op0=ALU.mult)
            P.op("vector", "tensor_add", y_acc[:, :], y_acc[:, :],
                 u_t[:, :])
            P.op("scalar", "activation", sil[:, :], res_t[:, :],
                 func=AF.Silu)
            P.op("vector", "tensor_mul", y_acc[:, :], y_acc[:, :],
                 sil[:, :])
            P.op("vector", "tensor_copy", ybf[:, :], y_acc[:, :])
            P.dma("gpsimd", y_hbm.ap()[dblk * 128:(dblk + 1) * 128,
                                       t0:t0 + TC], ybf[:, :])
    P.barrier()
    st.close()

    st = ExitStack()
    wsl4 = [P.sbuf(st, "l0a_w4sl%d" % i, [128, 128], BF16)
            for i in range(16)]
    ysl = [P.sbuf(st, "l0a_ysl%d" % i, [128, TC], BF16) for i in range(16)]
    o_ps = P.psum(st, "l0a_ops", [128, TC])
    xre = P.sbuf(st, "l0a_xre", [128, TC], F32)
    xo = P.sbuf(st, "l0a_xo", [128, TC], F32)
    for tc in range(NTC):
        t0 = tc * TC
        for kc in range(16):
            P.dma("sync", ysl[kc][:, :],
                  y_hbm.ap()[kc * 128:(kc + 1) * 128, t0:t0 + TC])
        for dc in range(NDC):
            for kc in range(16):
                P.dma("gpsimd", wsl4[kc][:, :],
                      w["out_proj_w"][kc * 128:(kc + 1) * 128,
                                      dc * 128:(dc + 1) * 128])
                P.op("tensor", "matmul", o_ps[:, :], wsl4[kc][:, :],
                     ysl[kc][:, :], start=(kc == 0), stop=(kc == 15))
            P.dma("sync", xre[:, :], x_in[dc * 128:(dc + 1) * 128,
                                          t0:t0 + TC])
            P.op("vector", "tensor_add", xo[:, :], o_ps[:, :], xre[:, :])
            P.dma("gpsimd", x_out[dc * 128:(dc + 1) * 128, t0:t0 + TC],
                  xo[:, :])
    P.barrier()
    st.close()


def build_l0_attn_launch():
    P = Prog("l0attn")
    x_in = P.dram("x_in", [D, T], F32, kind="ExternalInput").ap()
    x_out = P.dram("x_out", [D, T], F32, kind="ExternalOutput").ap()
    w = {
        "norm_w": P.dram("wnw", [1, D], F32, kind="ExternalInput").ap(),
        "in_proj_w": P.dram("wip", [D, 4096], F32,
                            kind="ExternalInput").ap(),
        "conv_w": P.dram("wcw", [2048, 4], F32, kind="ExternalInput").ap(),
        "conv_b": P.dram("wcb", [2048, 1], F32, kind="ExternalInput").ap(),
        "x_proj_w": P.dram("wxp", [2048, 96], F32,
                           kind="ExternalInput").ap(),
        "dt_proj_w": P.dram("wdp", [64, 2048], F32,
                            kind="ExternalInput").ap(),
        "dt_proj_b": P.dram("wdpb", [2048, 1], F32,
                            kind="ExternalInput").ap(),
        "A_log": P.dram("wAl", [2048, 16], F32, kind="ExternalInput").ap(),
        "Dv": P.dram("wDv", [2048, 1], F32, kind="ExternalInput").ap(),
        "out_proj_w": P.dram("wop", [2048, D], F32,
                             kind="ExternalInput").ap(),
    }
    bs = ExitStack()
    emit_l0_attn(P, bs, x_in, x_out, w)
    bs.close()
    nc = P.assemble()
    return P, nc



def build_transpose_launch(direction):
    P = Prog("tr_" + direction)
    if direction == "in":
        src = P.dram("x_tok", [T, D], F32, kind="ExternalInput").ap()
        dst = P.dram("x_feat", [D, T], F32, kind="ExternalOutput").ap()
        n_p, n_f = T, D
    else:
        src = P.dram("x_feat", [D, T], F32, kind="ExternalInput").ap()
        dst = P.dram("x_tok", [T, D], F32, kind="ExternalOutput").ap()
        n_p, n_f = D, T
    st = ExitStack()
    tile = [P.sbuf(st, "tr_t%d" % i, [128, n_f], F32) for i in range(2)]
    tps = P.psum(st, "tr_ps", [128, 128])
    osb = [P.sbuf(st, "tr_o%d" % i, [128, 128], F32) for i in range(4)]
    for tt in range(n_p // 128):
        i = tt % 2
        P.dma("sync", tile[i][:, :], src[tt * 128:(tt + 1) * 128, :])
        for c in range(n_f // 128):
            j = c % 4
            emit_transpose(P, tps[:, :], tile[i][:, c * 128:(c + 1) * 128],
                           128, bf16=False)
            P.op("vector", "tensor_copy", osb[j][:, :], tps[:, :])
            P.dma("gpsimd", dst[c * 128:(c + 1) * 128,
                                tt * 128:(tt + 1) * 128], osb[j][:, :])
    P.barrier()
    st.close()
    nc = P.assemble()
    return P, nc


def build_l0_ffn_mode(mode):
    P = Prog("l0ffn_" + mode)
    x_in = P.dram("x_in", [D, T], F32, kind="ExternalInput").ap()
    w = {
        "g1": P.dram("wg1", [1, D], F32, kind="ExternalInput").ap(),
        "g2": P.dram("wg2", [1, D], F32, kind="ExternalInput").ap(),
        "wq": P.dram("wwq", [D, 1024], F32, kind="ExternalInput").ap(),
        "wk": P.dram("wwk", [D, 256], F32, kind="ExternalInput").ap(),
        "wv": P.dram("wwv", [D, 256], F32, kind="ExternalInput").ap(),
        "wo": P.dram("wwo", [1024, D], F32, kind="ExternalInput").ap(),
        "router_w": P.dram("wrw", [D, 8], F32, kind="ExternalInput").ap(),
        "router_b": P.dram("wrb", [1, 8], F32, kind="ExternalInput").ap(),
        "w1": P.dram("ww1", [8 * D, 2048], F32, kind="ExternalInput").ap(),
        "w2": P.dram("ww2", [8 * D, 2048], F32, kind="ExternalInput").ap(),
        "w3": P.dram("ww3", [8 * 2048, D], F32, kind="ExternalInput").ap(),
    }
    bs = ExitStack()
    if mode == "counts":
        h_ext = P.dram("h_ext", [D, T], F32, kind="ExternalOutput").ap()
        counts_out = P.dram("counts_out", [8, 2], F32,
                            kind="ExternalOutput").ap()
        emit_l0_ffn(P, bs, x_in, None, w, None, None,
                    counts_out=counts_out, mode="counts", h_ext=h_ext)
    else:
        h_ext = P.dram("h_ext", [D, T], F32, kind="ExternalInput").ap()
        x_out = P.dram("x_out", [D, T], F32, kind="ExternalOutput").ap()
        counts_all = P.dram("counts_all", [8, 16], F32,
                            kind="ExternalInput").ap()
        prev_mask = P.dram("prev_mask", [8, 1], F32,
                           kind="ExternalInput").ap()
        emit_l0_ffn(P, bs, x_in, x_out, w, counts_all, prev_mask,
                    counts_out=None, mode="apply", h_ext=h_ext)
    bs.close()
    nc = P.assemble()
    return P, nc


def build_l1_ffn_mode(mode):
    P = Prog("l1ffn_" + mode)
    x_in = P.dram("x_in", [D, T], F32, kind="ExternalInput").ap()
    w = {
        "Wg": P.dram("wWg", [D, 8], F32, kind="ExternalInput").ap(),
        "W1": P.dram("wW1", [8 * D, 4096], F32, kind="ExternalInput").ap(),
        "b1": P.dram("wb1", [8, 4096], F32, kind="ExternalInput").ap(),
        "W2": P.dram("wW2", [8 * 4096, D], F32, kind="ExternalInput").ap(),
        "b2": P.dram("wb2", [8, 1024], F32, kind="ExternalInput").ap(),
    }
    bs = ExitStack()
    if mode == "counts":
        counts_out = P.dram("counts_out", [8, 2], F32,
                            kind="ExternalOutput").ap()
        emit_l1_ffn(P, bs, x_in, None, w, None, None,
                    counts_out=counts_out, mode="counts")
    else:
        x_out = P.dram("x_out", [D, T], F32, kind="ExternalOutput").ap()
        counts_all = P.dram("counts_all", [8, 16], F32,
                            kind="ExternalInput").ap()
        prev_mask = P.dram("prev_mask", [8, 1], F32,
                           kind="ExternalInput").ap()
        emit_l1_ffn(P, bs, x_in, x_out, w, counts_all, prev_mask,
                    counts_out=None, mode="apply")
    bs.close()
    nc = P.assemble()
    return P, nc



_BUILD_CACHE = {}


def _get(name, builder):
    if name not in _BUILD_CACHE:
        _BUILD_CACHE[name] = builder()
    return _BUILD_CACHE[name]


def _run(P, nc, per_core, shared):
    in_maps = []
    for c in range(8):
        m = dict(shared)
        m.update(per_core[c])
        m.update(P.extra_inputs)
        in_maps.append(m)
    res = run_bass_kernel_spmd(nc, in_maps, core_ids=list(range(8)))
    return res.results


def _counts_all(couts):
    ca = np.zeros((8, 16), np.float32)
    for c in range(8):
        ca[c, 0:8] = couts[c][:, 0]
        ca[c, 8:16] = couts[c][:, 1]
    return ca


_PREV_MASKS = [np.asarray([[1.0] if i < c else [0.0] for i in range(8)],
                          np.float32) for c in range(8)]


def kernel(**inputs):
    ii = {k: np.ascontiguousarray(np.asarray(v)) for k, v in inputs.items()}
    x0 = ii["L0_attn_x"].astype(np.float32)

    P, nc = _get("tr_in", lambda: build_transpose_launch("in"))
    r = _run(P, nc, [{"x_tok": x0[c]} for c in range(8)], {})
    xT = [r[c]["x_feat"] for c in range(8)]

    P, nc = _get("l0a", build_l0_attn_launch)
    sh = {"wnw": ii["L0_attn_norm_w"].reshape(1, D),
          "wip": ii["L0_attn_in_proj_w"],
          "wcw": ii["L0_attn_conv_w"],
          "wcb": ii["L0_attn_conv_b"].reshape(-1, 1),
          "wxp": ii["L0_attn_x_proj_w"],
          "wdp": ii["L0_attn_dt_proj_w"],
          "wdpb": ii["L0_attn_dt_proj_b"].reshape(-1, 1),
          "wAl": ii["L0_attn_A_log"],
          "wDv": ii["L0_attn_D"].reshape(-1, 1),
          "wop": ii["L0_attn_out_proj_w"]}
    r = _run(P, nc, [{"x_in": xT[c]} for c in range(8)], sh)
    x1 = [r[c]["x_out"] for c in range(8)]

    sh = {"wg1": ii["L0_ffn_g1"].reshape(1, D),
          "wg2": ii["L0_ffn_g2"].reshape(1, D),
          "wwq": ii["L0_ffn_wq"], "wwk": ii["L0_ffn_wk"],
          "wwv": ii["L0_ffn_wv"], "wwo": ii["L0_ffn_wo"],
          "wrw": ii["L0_ffn_router_w"],
          "wrb": ii["L0_ffn_router_b"].reshape(1, 8),
          "ww1": ii["L0_ffn_w1"].reshape(8 * D, 2048),
          "ww2": ii["L0_ffn_w2"].reshape(8 * D, 2048),
          "ww3": ii["L0_ffn_w3"].reshape(8 * 2048, D)}
    P, nc = _get("l0f_c", lambda: build_l0_ffn_mode("counts"))
    r = _run(P, nc, [{"x_in": x1[c]} for c in range(8)], sh)
    ca = _counts_all([r[c]["counts_out"] for c in range(8)])
    hs = [r[c]["h_ext"] for c in range(8)]
    P, nc = _get("l0f_a", lambda: build_l0_ffn_mode("apply"))
    r = _run(P, nc, [{"x_in": x1[c], "h_ext": hs[c], "counts_all": ca,
                      "prev_mask": _PREV_MASKS[c]} for c in range(8)], sh)
    x2 = [r[c]["x_out"] for c in range(8)]

    P, nc = _get("l1a", build_l1_attn_launch)
    sh = {"wWk": ii["L1_attn_Wk"], "wWq": ii["L1_attn_Wq"],
          "wWv": ii["L1_attn_Wv"], "wWo": ii["L1_attn_Wo"],
          "wbo": ii["L1_attn_bo"].reshape(1, D)}
    r = _run(P, nc, [{"x_in": x2[c]} for c in range(8)], sh)
    x3 = [r[c]["x_out"] for c in range(8)]

    sh = {"wWg": ii["L1_ffn_Wg"],
          "wW1": ii["L1_ffn_W1"].reshape(8 * D, 4096),
          "wb1": ii["L1_ffn_b1"],
          "wW2": ii["L1_ffn_W2"].reshape(8 * 4096, D),
          "wb2": ii["L1_ffn_b2"]}
    P, nc = _get("l1f_c", lambda: build_l1_ffn_mode("counts"))
    r = _run(P, nc, [{"x_in": x3[c]} for c in range(8)], sh)
    ca = _counts_all([r[c]["counts_out"] for c in range(8)])
    P, nc = _get("l1f_a", lambda: build_l1_ffn_mode("apply"))
    r = _run(P, nc, [{"x_in": x3[c], "counts_all": ca,
                      "prev_mask": _PREV_MASKS[c]} for c in range(8)], sh)
    x4 = [r[c]["x_out"] for c in range(8)]

    P, nc = _get("l2a", build_l2_attn_launch)
    sh = {"wWq": ii["L2_attn_Wq"], "wWk": ii["L2_attn_Wk"],
          "wWv": ii["L2_attn_Wv"], "wWo": ii["L2_attn_Wo"]}
    r = _run(P, nc, [{"x_in": x4[c]} for c in range(8)], sh)
    x5 = [r[c]["x_out"] for c in range(8)]

    P, nc = _get("l2f", build_l2_ffn_launch)
    sh = {"wk0": ii["L2_ffn_k0"].view(np.uint8),
          "wk1": ii["L2_ffn_k1"].view(np.uint8),
          "wk2": ii["L2_ffn_k2"].view(np.uint8),
          "ws0": ii["L2_ffn_s0"].reshape(1, 1),
          "ws1": ii["L2_ffn_s1"].reshape(1, 1),
          "ws2": ii["L2_ffn_s2"].reshape(1, 1)}
    r = _run(P, nc, [{"x_in": x5[c]} for c in range(8)], sh)
    x6 = [r[c]["x_out"] for c in range(8)]

    P, nc = _get("l3a", build_l3_attn_launch)
    sh = {"wqkv": ii["L3_attn_w_qkv"],
          "wbqkv": ii["L3_attn_b_qkv"].reshape(1, 3072),
          "wout": ii["L3_attn_w_out"],
          "wbout": ii["L3_attn_b_out"].reshape(1, D),
          "wproj": ii["L3_attn_proj"].reshape(16 * 64, 64)}
    r = _run(P, nc, [{"x_in": x6[c]} for c in range(8)], sh)
    x7 = [r[c]["x_out"] for c in range(8)]

    P, nc = _get("l3f", build_l3_ffn_launch)
    sh = {"ww1": ii["L3_ffn_w1"].reshape(8 * D, 512),
          "wb1": ii["L3_ffn_b1"],
          "ww2": ii["L3_ffn_w2"].reshape(8 * 512, D),
          "wb2": ii["L3_ffn_b2"],
          "wcen": ii["L3_ffn_centroids"],
          "wrt": ii["L3_ffn_w_route"]}
    r = _run(P, nc, [{"x_in": x7[c]} for c in range(8)], sh)
    x8 = [r[c]["x_out"] for c in range(8)]

    P, nc = _get("tr_out", lambda: build_transpose_launch("out"))
    r = _run(P, nc, [{"x_feat": x8[c]} for c in range(8)], {})
    return np.stack([r[c]["x_tok"] for c in range(8)], 0)


# version 04
# speedup vs baseline: 1.0494x; 1.0494x over previous
import math
from contextlib import ExitStack

import numpy as np
import concourse.bass as bass
import concourse.mybir as mybir
from concourse.bass_utils import run_bass_kernel_spmd

F32 = mybir.dt.float32
BF16 = mybir.dt.bfloat16
U8 = mybir.dt.uint8
U32 = mybir.dt.uint32
AF = mybir.ActivationFunctionType
ALU = mybir.AluOpType
AX = mybir.AxisListType

T = 4096
D = 1024
NDC = D // 128
TC = 512
NTC = T // TC

ENGINES = ("sync", "scalar", "vector", "tensor", "gpsimd")


class Prog:

    def __init__(self, name="prog"):
        self.nc = bass.Bass("TRN2", target_bir_lowering=False)
        self.ctx = ExitStack()
        self.plan = {e: [] for e in ENGINES}
        self.sems = {}
        self.cnt = {}
        self.seen = {e: {} for e in ENGINES}
        for s in ("epe", "eact", "edve", "epool", "esp"):
            self.sems[s] = self.ctx.enter_context(self.nc.semaphore(s))
            self.cnt[s] = 0
        for i in range(8):
            s = "pb%d" % i
            self.sems[s] = self.ctx.enter_context(self.nc.semaphore(s))
            self.cnt[s] = 0
        self._bank_free = list(range(8))
        self._bank_of = {}
        self.engine_sem = {"sync": "esp", "scalar": "eact", "vector": "edve",
                          "tensor": "epe", "gpsimd": "epool"}
        self.NSLOT = 8
        self.queue_prefix = {"sync": "qs", "scalar": "qa",
                            "gpsimd": "qg", "vector": "qv"}
        self.qidx = {}
        for q, pfx in self.queue_prefix.items():
            self.qidx[q] = 0
            for i in range(self.NSLOT):
                s = pfx + str(i)
                self.sems[s] = self.ctx.enter_context(self.nc.semaphore(s))
                self.cnt[s] = 0
        self._const_cache = {}
        self.extra_inputs = {}
        self.const_stack = ExitStack()
        self.last_write = {}
        self.readers = {}
        self._pe_groups = {}

    def dram(self, name, shape, dtype, kind=None, **kw):
        if kind is None:
            return self.nc.dram_tensor(name, shape, dtype, **kw)
        return self.nc.dram_tensor(name, shape, dtype, kind=kind, **kw)

    def sbuf(self, stack, name, shape, dtype):
        return stack.enter_context(self.nc.sbuf_tensor(name, shape, dtype))

    def psum(self, stack, name, shape, dtype=F32):
        per_part = int(np.prod(shape[1:])) * mybir.dt.size(dtype)
        assert per_part <= 2048, (name, shape)
        h = stack.enter_context(self.nc.psum_tensor(name, shape, dtype))
        bank = self._bank_free.pop(0)
        self._bank_of[name] = bank
        stack.callback(lambda b=bank, n=name: (
            self._bank_free.insert(0, b), self._bank_of.pop(n, None)))
        return h

    def const_input(self, name, arr):
        if name not in self._const_cache:
            if arr.dtype == np.float32:
                dt = F32
            elif arr.dtype == np.uint8:
                dt = U8
            elif arr.dtype == np.uint32:
                dt = U32
            else:
                dt = BF16
            h = self.nc.dram_tensor(name, list(arr.shape), dt,
                                    kind="ExternalInput")
            self.extra_inputs[name] = arr
            self._const_cache[name] = h.ap()
        return self._const_cache[name]

    def sbuf_const(self, name, arr):
        key = "sc_" + name
        if key in self._const_cache:
            return self._const_cache[key]
        ap = self.const_input("ci_" + name, arr)
        dt = {np.dtype(np.float32): F32, np.dtype(np.uint8): U8,
              np.dtype(np.uint32): U32}.get(arr.dtype, BF16)
        sb = self.const_stack.enter_context(
            self.nc.sbuf_tensor("sb_" + name, list(arr.shape), dt,
                                side="right"))
        tok = self.dma("sync", sb[:, :], ap[:, :])
        self._const_cache[key] = (sb, tok)
        return sb, tok

    def _wait(self, e, engine, token):
        if token is None:
            return
        sem, val = token
        if self.seen[engine].get(sem, -1) >= val:
            return
        self.seen[engine][sem] = val
        e.wait_ge(self.sems[sem], val)

    @staticmethod
    def _ap_name(x):
        t = getattr(x, "tensor", None)
        return getattr(t, "name", None) if t is not None else None

    def _track(self, engine, outs, ins, extra_waits):
        waits = list(extra_waits)
        for x in ins:
            nm = self._ap_name(x)
            if nm is None:
                continue
            lw = self.last_write.get(nm)
            if lw and not (engine == "tensor" and lw[0] == "tensor"):
                waits.append(lw[1])
        for x in outs:
            nm = self._ap_name(x)
            if nm is None:
                continue
            lw = self.last_write.get(nm)
            if lw and not (engine == "tensor" and lw[0] == "tensor"):
                waits.append(lw[1])
            for (re, rt) in self.readers.get(nm, ()):
                if not (engine == "tensor" and re == "tensor"):
                    waits.append(rt)
        return waits

    def _register(self, engine, outs, ins, token):
        for x in ins:
            nm = self._ap_name(x)
            if nm is not None:
                self.readers.setdefault(nm, []).append((engine, token))
        for x in outs:
            nm = self._ap_name(x)
            if nm is not None:
                self.last_write[nm] = (engine, token)
                self.readers[nm] = []

    def op(self, engine, method, *a, waits=(), inc=True, **kw):
        outs = [a[0]] if a else []
        if kw.get("accum_out") is not None:
            outs.append(kw["accum_out"])
        ins = [x for x in a[1:] if self._ap_name(x) is not None]
        ins += [v for k, v in kw.items()
                if k != "accum_out" and self._ap_name(v) is not None]
        all_waits = self._track(engine, outs, ins, waits)
        if engine == "tensor" and method in ("matmul", "transpose"):
            pname = self._ap_name(a[0])
            s = "pb%d" % self._bank_of[pname]
            self.cnt[s] += 1
            token = [s, self.cnt[s]]
            if method == "matmul":
                grp = self._pe_groups.setdefault(pname, [])
                grp.append(token)
                if kw.get("stop", True):
                    for t in grp:
                        t[1] = self.cnt[s]
                    self._pe_groups[pname] = []
        else:
            s = self.engine_sem[engine]
            self.cnt[s] += 1
            token = [s, self.cnt[s]]
        self._register(engine, outs, ins, token)

        def fn(e, waits=tuple(all_waits), token=token, method=method,
               a=a, kw=kw, s=s):
            for w in waits:
                self._wait(e, engine, w)
            inst = getattr(e, method)(*a, **kw)
            inst.then_inc(self.sems[s], 1)
        self.plan[engine].append(fn)
        return token

    def dma(self, queue, out, in_, waits=(), **kw):
        s = self.queue_prefix[queue] + str(self.qidx[queue] % self.NSLOT)
        self.qidx[queue] += 1
        gate = [s, self.cnt[s]] if self.cnt[s] else None
        self.cnt[s] += 16
        token = [s, self.cnt[s]]
        all_waits = self._track(queue, [out], [in_], waits)
        if gate:
            all_waits.append(gate)
        self._register(queue, [out], [in_], token)

        def fn(e, waits=tuple(all_waits), out=out, in_=in_, kw=kw,
               token=token):
            for w in waits:
                self._wait(e, queue, w)
            e.dma_start(out=out, in_=in_, **kw).then_inc(
                self.sems[token[0]], 16)
        self.plan[queue].append(fn)
        return token

    def wait_only(self, engine, tokens):
        tokens = tuple(t for t in tokens if t is not None)

        def fn(e, tokens=tokens):
            for w in tokens:
                self._wait(e, engine, w)
        self.plan[engine].append(fn)

    def barrier(self):
        toks = []
        for eng in ENGINES:
            s = self.engine_sem[eng]
            if self.cnt[s]:
                toks.append([s, self.cnt[s]])
        for i in range(8):
            s = "pb%d" % i
            if self.cnt[s]:
                toks.append([s, self.cnt[s]])
        for q, pfx in self.queue_prefix.items():
            for i in range(self.NSLOT):
                s = pfx + str(i)
                if self.cnt[s]:
                    toks.append([s, self.cnt[s]])
        for eng in ENGINES:
            self.wait_only(eng, toks)

    def assemble(self):
        for k, v in self._pe_groups.items():
            assert not v, ("unclosed matmul accumulation group", k)
        self.seen = {e: {} for e in ENGINES}
        with self.nc.Block() as block:
            for eng in ENGINES:
                fns = self.plan[eng]
                if not fns:
                    continue
                getattr(block, eng)(_EngineBody(fns))
        self.const_stack.close()
        self.ctx.close()
        return self.nc


class _EngineBody:
    def __init__(self, fns):
        self.fns = fns

    def __call__(self, e):
        for fn in self.fns:
            fn(e)


def load_panel(P, dst, w_ap, c0, cw, n_kc, queue="gpsimd"):
    return P.dma(queue,
                 dst[:, 0:n_kc * cw].rearrange("p (kc m) -> p kc m", m=cw),
                 w_ap[:, c0:c0 + cw].rearrange("(kc p) m -> p kc m", p=128))


def mm_acc(P, psum_ap, pairs, waits=()):
    n = len(pairs)
    tok = None
    for k, (lt, rh) in enumerate(pairs):
        tok = P.op("tensor", "matmul", psum_ap, lt, rh,
                   start=(k == 0), stop=(k == n - 1),
                   waits=waits if k == 0 else (), inc=(k == n - 1))
    return tok



def emit_load_xb(P, stack, x_hbm, name="xb"):
    XB = P.sbuf(stack, name, [128, NDC * T], BF16)
    toks = []
    for dc in range(NDC):
        toks.append(P.dma("gpsimd", XB[:, dc * T:(dc + 1) * T],
                          x_hbm[dc * 128:(dc + 1) * 128, :]))
    return XB, toks


def emit_layernorm_inplace(P, stack, XB, xb_toks, ones, eps=1e-5,
                           tag="ln", stats_stack=None):
    sst = stats_stack if stats_stack is not None else stack
    rstd = P.sbuf(sst, tag + "_rstd", [128, NTC * TC], F32)
    mr = P.sbuf(sst, tag + "_mr", [128, NTC * TC], F32)
    sum_ps = P.psum(stack, tag + "_sum", [128, TC])
    sq_ps = P.psum(stack, tag + "_sq", [128, TC])
    xsq = [P.sbuf(stack, tag + "_xsq%d" % i, [128, TC], BF16)
           for i in range(NDC)]
    mean = P.sbuf(stack, tag + "_mean", [128, TC], F32)
    ex2 = P.sbuf(stack, tag + "_ex2", [128, TC], F32)
    msq = P.sbuf(stack, tag + "_msq", [128, TC], F32)
    var = P.sbuf(stack, tag + "_var", [128, TC], F32)
    scr = P.sbuf(stack, tag + "_scr", [128, TC], F32)

    stat_toks = []
    pe_sq_toks = {}
    prev_read = []
    for tc in range(NTC):
        t0 = tc * TC
        for dc in range(NDC):
            i = (tc * NDC + dc) % NDC
            xtile = XB[:, dc * T + t0: dc * T + t0 + TC]
            war = pe_sq_toks.get(tc * NDC + dc - NDC)
            sq_tok = P.op("scalar", "activation", xsq[i][:, :], xtile,
                          func=AF.Square,
                          waits=[xb_toks[dc]] + ([war] if war else []),
                          inc=True)
            P.op("tensor", "matmul", sum_ps[:, :], ones[0][:, :], xtile,
                 start=(dc == 0), stop=(dc == NDC - 1),
                 waits=[xb_toks[dc], ones[1]] + (prev_read if dc == 0 else []))
            pe_sq_toks[tc * NDC + dc] = P.op(
                "tensor", "matmul", sq_ps[:, :], ones[0][:, :], xsq[i][:, :],
                start=(dc == 0), stop=(dc == NDC - 1),
                waits=[sq_tok], inc=True)
        mm_tok = ["epe", P.cnt["epe"]]
        t_m = P.op("vector", "tensor_scalar_mul", mean[:, :], sum_ps[:, :],
                   1.0 / D, waits=[mm_tok], inc=True)
        t_e = P.op("vector", "tensor_scalar", ex2[:, :], sq_ps[:, :],
                   1.0 / D, float(eps), op0=ALU.mult, op1=ALU.add, inc=True)
        t_s = P.op("scalar", "activation", msq[:, :], mean[:, :],
                   func=AF.Square, waits=[t_m], inc=True)
        t_v = P.op("vector", "tensor_sub", var[:, :], ex2[:, :], msq[:, :],
                   waits=[t_e, t_s], inc=True)
        t_l = P.op("scalar", "activation", scr[:, :], var[:, :],
                   func=AF.Ln, waits=[t_v], inc=True)
        t_r = P.op("scalar", "activation", rstd[:, t0:t0 + TC], scr[:, :],
                   func=AF.Exp, scale=-0.5, inc=True)
        t_mr = P.op("vector", "tensor_mul", mr[:, t0:t0 + TC], mean[:, :],
                    rstd[:, t0:t0 + TC], waits=[t_r], inc=True)
        stat_toks.append((t_r, t_mr))
        prev_read = [t_m, t_e]

    apply_toks = {}
    scr2 = P.sbuf(stack, tag + "_scr2", [128, TC], F32)
    for dc in range(NDC):
        for tc in range(NTC):
            t0 = tc * TC
            xtile = XB[:, dc * T + t0: dc * T + t0 + TC]
            P.op("vector", "tensor_mul", scr2[:, :], xtile,
                 rstd[:, t0:t0 + TC], waits=[stat_toks[tc][0]])
            tok = P.op("vector", "tensor_sub", xtile, scr2[:, :],
                       mr[:, t0:t0 + TC], waits=[stat_toks[tc][1]], inc=True)
            apply_toks[(dc, tc)] = tok
    return apply_toks, rstd, mr



def _bf16np():
    import ml_dtypes
    return ml_dtypes.bfloat16


def emit_scalar_replicate(P, stack, tag, src_ap_11, waits, scale=1.0):
    ones1, t_o = P.sbuf_const("ones1x128_f32",
                              np.ones((1, 128), dtype=np.float32))
    ps = P.psum(stack, tag + "_ps", [128, 2])
    t1 = P.op("tensor", "matmul", ps[:, 0:1], ones1[:, :], src_ap_11,
              start=True, stop=True, waits=list(waits) + [t_o], inc=True)
    S = P.sbuf(stack, tag + "_S", [128, 1], F32)
    if scale != 1.0:
        t2 = P.op("vector", "tensor_scalar_mul", S[:, :], ps[:, 0:1],
                  float(scale), waits=[t1], inc=True)
    else:
        t2 = P.op("vector", "tensor_copy", S[:, :], ps[:, 0:1],
                  waits=[t1], inc=True)
    return S, t2


class StreamedMatmul:

    def __init__(self, P, stack, tag, Wp, n_kc, n_mc, PSbanks, pool):
        self.P, self.tag, self.Wp = P, tag, Wp
        self.n_kc, self.n_mc = n_kc, n_mc
        self.PS = PSbanks
        self.nmb = len(PSbanks)
        assert len(pool) >= n_kc
        self.nbuf = len(pool)
        self.wsb = pool
        self.dma_guard = {}
        self.ps_guard = {}

    def run(self, rhs_tile, rhs_tok, evac, tcw=TC, prep_tok=None):
        P = self.P
        out_toks = []
        n_mb = (self.n_mc + self.nmb - 1) // self.nmb
        step = 0
        for mb in range(n_mb):
            mcs = [mb * self.nmb + j for j in range(self.nmb)
                   if mb * self.nmb + j < self.n_mc]
            for kc in range(self.n_kc):
                i = step % self.nbuf
                war = self.dma_guard.get(step - self.nbuf)
                wts = [war] if war else []
                if prep_tok is not None:
                    wts.append(prep_tok)
                tin = P.dma("sync", self.wsb[i][:, 0:len(mcs) * 128],
                            self.Wp[kc * 128:(kc + 1) * 128,
                                    mcs[0] * 128:(mcs[-1] + 1) * 128],
                            waits=wts)
                last_mm = None
                for j, mc in enumerate(mcs):
                    wts2 = [tin, rhs_tok(kc)]
                    if kc == 0:
                        g = self.ps_guard.get(j)
                        if g:
                            wts2.append(g)
                    last_mm = P.op(
                        "tensor", "matmul", self.PS[j][:, 0:tcw],
                        self.wsb[i][:, j * 128:(j + 1) * 128],
                        rhs_tile(kc),
                        start=(kc == 0), stop=(kc == self.n_kc - 1),
                        waits=wts2, inc=True)
                self.dma_guard[step] = last_mm
                step += 1
            for j, mc in enumerate(mcs):
                tok = evac(mc, self.PS[j][:, 0:tcw],
                           self.dma_guard[step - 1])
                self.ps_guard[j] = tok
                out_toks.append(tok)
        return out_toks



def emit_l2_ffn(P, bs, x_in, x_out, w):
    ones_bf = P.sbuf_const("ones128_bf",
                           np.ones((128, 128), dtype=np.float32)
                           .astype(_bf16np()))

    XB, xb_toks = emit_load_xb(P, bs, x_in, name="l2f_xb")
    lnstack = ExitStack()
    emit_layernorm_inplace(P, lnstack, XB, xb_toks, ones_bf,
                           eps=1e-5, tag="l2f_ln")
    P.barrier()
    lnstack.close()

    st = ExitStack()
    sc = P.sbuf(st, "l2f_sc", [1, 4], F32)
    s_toks = [P.dma("sync", sc[0:1, i:i + 1], w["s%d" % i][0:1, 0:1])
              for i in range(3)]
    P.op("vector", "tensor_mul", sc[0:1, 3:4], sc[0:1, 0:1], sc[0:1, 1:2],
         waits=s_toks)
    t_sc = P.op("vector", "tensor_mul", sc[0:1, 3:4], sc[0:1, 3:4],
                sc[0:1, 2:3], inc=True)
    S_pp, t_S = emit_scalar_replicate(P, st, "l2f_S", sc[0:1, 3:4],
                                      [t_sc], scale=8.0)

    W0p = P.dram("l2f_W0p", [1024, 4096], BF16)
    W1p = P.dram("l2f_W1p", [4096, 4096], BF16)
    W2p = P.dram("l2f_W2p", [4096, 1024], BF16)
    prepstack = ExitStack()
    ku8 = [P.sbuf(prepstack, "l2f_ku8_%d" % i, [128, 4096], U8)
           for i in range(2)]
    kbf = [P.sbuf(prepstack, "l2f_kbf_%d" % i, [128, 4096], BF16)
           for i in range(3)]
    prep0 = []
    prev_out = None
    for kc in range(8):
        wts = [prev_out] if prev_out else []
        tk_t = P.dma("sync", ku8[0][:, :], w["k0"][kc * 128:(kc + 1) * 128, :],
                     waits=wts)
        tk_b = P.dma("sync", ku8[1][:, :],
                     w["k0"][1024 + kc * 128:1024 + (kc + 1) * 128, :],
                     waits=wts)
        P.op("vector", "tensor_copy", kbf[0][:, :], ku8[0][:, :],
             waits=[tk_t])
        P.op("vector", "tensor_copy", kbf[1][:, :], ku8[1][:, :],
             waits=[tk_b])
        tsub = P.op("vector", "tensor_sub", kbf[2][:, :],
                    kbf[0][:, :], kbf[1][:, :], inc=True)
        prev_out = P.dma("gpsimd", W0p.ap()[kc * 128:(kc + 1) * 128, :],
                         kbf[2][:, :], waits=[tsub])
        prep0.append(prev_out)
    prep_w = {0: [], 1: []}
    for j, (src, dst, cols, n_kc) in enumerate(
            [(w["k1"], W1p, 4096, 32), (w["k2"], W2p, 1024, 32)]):
        for kc in range(n_kc):
            i = kc % 2
            war = prep_w[j][kc - 2] if kc >= 2 else None
            tin = P.dma("sync", ku8[i][:, 0:cols],
                        src[kc * 128:(kc + 1) * 128, :],
                        waits=[war] if war else [])
            tcast = P.op("vector", "tensor_scalar", kbf[i][:, 0:cols],
                         ku8[i][:, 0:cols], 2.0, -1.0,
                         op0=ALU.mult, op1=ALU.add, waits=[tin], inc=True)
            prep_w[j].append(P.dma(
                "gpsimd", dst.ap()[kc * 128:(kc + 1) * 128, :],
                kbf[i][:, 0:cols], waits=[tcast]))
    t_prep0 = prep0[-1]
    t_prep1 = prep_w[0][-1]
    t_prep2 = prep_w[1][-1]
    prepstack.close()

    P.barrier()
    H1 = P.sbuf(st, "l2f_h1", [128, 32 * TC], BF16)
    H2 = P.sbuf(st, "l2f_h2", [128, 32 * TC], BF16)
    PS = [P.psum(st, "l2f_ps%d" % i, [128, TC]) for i in range(4)]
    Xre = [P.sbuf(st, "l2f_xre%d" % i, [128, TC], F32) for i in range(2)]
    Xo = [P.sbuf(st, "l2f_xo%d" % i, [128, TC], F32) for i in range(2)]
    pool = [P.sbuf(st, "l2f_wp%d" % i, [128, TC], BF16) for i in range(32)]

    sm1 = StreamedMatmul(P, st, "l2f_sm1", W0p.ap(), 8, 32, PS, pool)
    sm2 = StreamedMatmul(P, st, "l2f_sm2", W1p.ap(), 32, 32, PS, pool)
    sm3 = StreamedMatmul(P, st, "l2f_sm3", W2p.ap(), 32, 8, PS, pool)

    xo_guard = {}
    for tc in range(NTC):
        t0 = tc * TC
        h1_toks = {}

        def evac1(mc, ps, mmt, tc=tc, h1_toks=h1_toks):
            tok = P.op("scalar", "activation",
                       H1[:, mc * TC:(mc + 1) * TC], ps, func=AF.Relu,
                       waits=[mmt], inc=True)
            h1_toks[mc] = tok
            return tok
        sm1.run(lambda kc, t0=t0: XB[:, kc * T + t0: kc * T + t0 + TC],
                lambda kc: None, evac1, prep_tok=t_prep0)

        h2_toks = {}

        def evac2(mc, ps, mmt, tc=tc, h2_toks=h2_toks):
            tok = P.op("scalar", "activation",
                       H2[:, mc * TC:(mc + 1) * TC], ps, func=AF.Relu,
                       waits=[mmt], inc=True)
            h2_toks[mc] = tok
            return tok
        sm2.run(lambda kc: H1[:, kc * TC:(kc + 1) * TC],
                lambda kc: h1_toks[kc], evac2, prep_tok=t_prep1)

        def evac3(dc, ps, mmt, tc=tc, t0=t0):
            i = dc % 2
            war = xo_guard.get((tc, dc - 2))
            trel = P.dma("sync", Xre[i][:, :],
                         x_in[dc * 128:(dc + 1) * 128, t0:t0 + TC],
                         waits=[war] if war else [])
            tstt = P.op("vector", "scalar_tensor_tensor", Xo[i][:, :],
                        ps, S_pp[:, :], Xre[i][:, :],
                        op0=ALU.mult, op1=ALU.add,
                        waits=[mmt, trel, t_S], inc=True)
            tout = P.dma("gpsimd",
                         x_out[dc * 128:(dc + 1) * 128, t0:t0 + TC],
                         Xo[i][:, :], waits=[tstt])
            xo_guard[(tc, dc)] = tout
            return tstt
        sm3.run(lambda kc: H2[:, kc * TC:(kc + 1) * TC],
                lambda kc: h2_toks[kc], evac3, prep_tok=t_prep2)

    P.barrier()
    st.close()


def build_l2_ffn_launch():
    P = Prog("l2ffn")
    x_in = P.dram("x_in", [D, T], F32, kind="ExternalInput").ap()
    x_out = P.dram("x_out", [D, T], F32, kind="ExternalOutput").ap()
    w = {
        "k0": P.dram("wk0", [2048, 4096], U8, kind="ExternalInput").ap(),
        "k1": P.dram("wk1", [4096, 4096], U8, kind="ExternalInput").ap(),
        "k2": P.dram("wk2", [4096, 1024], U8, kind="ExternalInput").ap(),
        "s0": P.dram("ws0", [1, 1], F32, kind="ExternalInput").ap(),
        "s1": P.dram("ws1", [1, 1], F32, kind="ExternalInput").ap(),
        "s2": P.dram("ws2", [1, 1], F32, kind="ExternalInput").ap(),
    }
    bs = ExitStack()
    emit_l2_ffn(P, bs, x_in, x_out, w)
    bs.close()
    nc = P.assemble()
    return P, nc



def emit_transpose(P, psum_ap, src_ap, n_in_part, bf16=True):
    nm = "ident_bf" if bf16 else "ident_f32"
    arr = np.eye(128, dtype=np.float32)
    ident, t_i = P.sbuf_const(nm, arr.astype(_bf16np()) if bf16 else arr)
    return P.op("tensor", "transpose", psum_ap, src_ap,
                ident[0:n_in_part, 0:n_in_part], waits=[t_i])



def emit_l3_ffn(P, bs, x_in, x_out, w):
    ones_bf = P.sbuf_const("ones128_bf",
                           np.ones((128, 128), dtype=np.float32)
                           .astype(_bf16np()))
    ones1f, t_o1 = P.sbuf_const("ones1x128_f32",
                                np.ones((1, 128), dtype=np.float32))

    XB, xb_toks = emit_load_xb(P, bs, x_in, name="l3f_xb")
    st = ExitStack()
    gst = ExitStack()
    lnstack = ExitStack()
    _, rstd, mr = emit_layernorm_inplace(P, lnstack, XB, xb_toks, ones_bf,
                                         eps=1e-5, tag="l3f_ln",
                                         stats_stack=gst)
    P.barrier()
    lnstack.close()
    csb = P.sbuf(gst, "l3f_csb", [8, D], F32)
    rsb = P.sbuf(gst, "l3f_rsb", [8, D], F32)
    P.dma("sync", csb[:, :], w["centroids"][:, :])
    P.dma("sync", rsb[:, :], w["w_route"][:, :])
    csq = P.sbuf(gst, "l3f_csq", [8, D], F32)
    cn1 = P.sbuf(gst, "l3f_cn1", [8, 1], F32)
    P.op("scalar", "activation", csq[:, :], csb[:, :], func=AF.Square)
    P.op("vector", "reduce_sum", cn1[:, :], csq[:, :], axis=AX.X)
    P.op("scalar", "activation", cn1[:, :], cn1[:, :], func=AF.Sqrt)
    P.op("vector", "tensor_scalar_max", cn1[:, :], cn1[:, :], 1e-12)
    P.op("vector", "reciprocal", cn1[:, :], cn1[:, :])
    P.op("vector", "tensor_scalar", csb[:, :], csb[:, :], cn1[:, :], None,
         op0=ALU.mult)
    trp = P.psum(gst, "l3f_trp", [128, 16])
    cnT = P.sbuf(gst, "l3f_cnT", [128, NDC * 8], F32)
    wrT = P.sbuf(gst, "l3f_wrT", [128, NDC * 8], F32)
    for dc in range(NDC):
        emit_transpose(P, trp[:, 0:8], csb[:, dc * 128:(dc + 1) * 128], 8,
                       bf16=False)
        P.op("vector", "tensor_copy", cnT[:, dc * 8:(dc + 1) * 8],
             trp[:, 0:8])
        emit_transpose(P, trp[:, 8:16], rsb[:, dc * 128:(dc + 1) * 128], 8,
                       bf16=False)
        P.op("vector", "tensor_copy", wrT[:, dc * 8:(dc + 1) * 8],
             trp[:, 8:16])


    onesf, t_of = P.sbuf_const("ones128_f32", np.ones((128, 128),
                                                      dtype=np.float32))
    ss_ps = P.psum(gst, "l3f_ssps", [128, TC])
    rcp = P.sbuf(gst, "l3f_rcp", [128, T], F32)
    scr = P.sbuf(gst, "l3f_scr", [128, TC], F32)
    xnf = [P.sbuf(gst, "l3f_xnf%d" % i, [128, TC], F32) for i in range(NDC)]
    xsqf = [P.sbuf(gst, "l3f_xsqf%d" % i, [128, TC], F32)
            for i in range(NDC)]
    cos_ps = P.psum(gst, "l3f_cosps", [8, TC])
    rt_ps = P.psum(gst, "l3f_rtps", [8, TC])
    lg = P.sbuf(gst, "l3f_lg", [8, T], F32)
    t1 = P.sbuf(gst, "l3f_t1", [8, TC], F32)
    for tc in range(NTC):
        t0 = tc * TC
        for dc in range(NDC):
            P.dma("sync", xnf[dc][:, :],
                  x_in[dc * 128:(dc + 1) * 128, t0:t0 + TC])
            P.op("vector", "tensor_mul", xnf[dc][:, :], xnf[dc][:, :],
                 rstd[:, t0:t0 + TC])
            P.op("vector", "tensor_sub", xnf[dc][:, :], xnf[dc][:, :],
                 mr[:, t0:t0 + TC])
            P.op("scalar", "activation", xsqf[dc][:, :], xnf[dc][:, :],
                 func=AF.Square)
            P.op("tensor", "matmul", ss_ps[:, :], onesf[:, :],
                 xsqf[dc][:, :], start=(dc == 0), stop=(dc == NDC - 1),
                 waits=[t_of])
            P.op("tensor", "matmul", rt_ps[:, :],
                 wrT[:, dc * 8:(dc + 1) * 8], xnf[dc][:, :],
                 start=(dc == 0), stop=(dc == NDC - 1))
            P.op("tensor", "matmul", cos_ps[:, :],
                 cnT[:, dc * 8:(dc + 1) * 8], xnf[dc][:, :],
                 start=(dc == 0), stop=(dc == NDC - 1))
        P.op("scalar", "activation", scr[:, :], ss_ps[:, :], func=AF.Sqrt)
        P.op("vector", "tensor_scalar_max", scr[:, :], scr[:, :], 1e-12)
        P.op("vector", "reciprocal", rcp[:, t0:t0 + TC], scr[:, :])
        P.op("vector", "tensor_mul", t1[:, :], cos_ps[:, :],
             rcp[0:8, t0:t0 + TC])
        P.op("vector", "tensor_add", lg[:, t0:t0 + TC], t1[:, :],
             rt_ps[:, :])

    lgt_ps = P.psum(gst, "l3f_lgtps", [128, 8])
    LGt = P.sbuf(gst, "l3f_LGt", [128, 8], F32)
    m1n = P.sbuf(gst, "l3f_m1n", [128, 1], F32)
    m1 = P.sbuf(gst, "l3f_m1", [128, 1], F32)
    ismax = P.sbuf(gst, "l3f_ismax", [128, 8], F32)
    nmax = P.sbuf(gst, "l3f_nmax", [128, 1], F32)
    msk = P.sbuf(gst, "l3f_msk", [128, 8], F32)
    m2 = P.sbuf(gst, "l3f_m2", [128, 1], F32)
    thr = P.sbuf(gst, "l3f_thr", [128, 1], F32)
    ge = P.sbuf(gst, "l3f_ge", [128, 8], F32)
    pex = P.sbuf(gst, "l3f_pex", [128, 8], F32)
    den = P.sbuf(gst, "l3f_den", [128, 1], F32)
    gt = P.sbuf(gst, "l3f_gt", [128, 8], F32)
    g_ps = P.psum(gst, "l3f_gps", [128, 128])
    gates = P.sbuf(gst, "l3f_gates", [8, T], F32)
    for tt in range(T // 128):
        emit_transpose(P, lgt_ps[:, :], lg[:, tt * 128:(tt + 1) * 128], 8,
                       bf16=False)
        P.op("vector", "tensor_copy", LGt[:, :], lgt_ps[:, :])
        P.op("vector", "tensor_reduce", m1n[:, :], LGt[:, :], axis=AX.X,
             op=ALU.max, negate=True)
        P.op("vector", "tensor_scalar_mul", m1[:, :], m1n[:, :], -1.0)
        P.op("vector", "tensor_scalar", ismax[:, :], LGt[:, :], m1[:, :],
             None, op0=ALU.is_ge)
        P.op("vector", "reduce_sum", nmax[:, :], ismax[:, :], axis=AX.X)
        P.op("vector", "scalar_tensor_tensor", msk[:, :], ismax[:, :],
             -1e30, LGt[:, :], op0=ALU.mult, op1=ALU.add)
        P.op("vector", "tensor_reduce", m2[:, :], msk[:, :], axis=AX.X,
             op=ALU.max)
        P.op("vector", "tensor_scalar", nmax[:, :], nmax[:, :], 2.0, None,
             op0=ALU.is_ge)
        P.op("vector", "tensor_sub", thr[:, :], m1[:, :], m2[:, :])
        P.op("vector", "tensor_mul", thr[:, :], thr[:, :], nmax[:, :])
        P.op("vector", "tensor_add", thr[:, :], thr[:, :], m2[:, :])
        P.op("vector", "tensor_scalar", ge[:, :], LGt[:, :], thr[:, :],
             None, op0=ALU.is_ge)
        P.op("scalar", "activation", pex[:, :], LGt[:, :], func=AF.Exp,
             bias=m1n[:, :])
        P.op("vector", "tensor_mul", pex[:, :], pex[:, :], ge[:, :])
        P.op("vector", "reduce_sum", den[:, :], pex[:, :], axis=AX.X)
        P.op("vector", "reciprocal", den[:, :], den[:, :])
        P.op("vector", "tensor_scalar", gt[:, :], pex[:, :], den[:, :],
             None, op0=ALU.mult)
        emit_transpose(P, g_ps[0:8, 0:128], gt[:, :], 128, bf16=False)
        P.op("vector", "tensor_copy", gates[:, tt * 128:(tt + 1) * 128],
             g_ps[0:8, 0:128])
    gates_hbm = P.dram("l3f_gates_hbm", [8, T], F32)
    P.dma("sync", gates_hbm.ap()[:, :], gates[:, :])
    P.barrier()
    gst.close()

    acc_hbm = P.dram("l3f_acc", [D, T], F32)
    W1sb = P.sbuf(st, "l3f_w1sb", [128, 8 * 512], BF16)
    W2sb = P.sbuf(st, "l3f_w2sb", [128, 4 * 1024], BF16)
    b1sb = P.sbuf(st, "l3f_b1sb", [128, 4], F32)
    b2sb = P.sbuf(st, "l3f_b2sb", [128, 8], F32)
    brow = P.sbuf(st, "l3f_brow", [1, 1024], F32)
    b_ps = P.psum(st, "l3f_bps", [128, 8])
    grow = P.sbuf(st, "l3f_grow", [1, TC], F32)
    gr_ps = P.psum(st, "l3f_grps", [128, TC])
    gr = P.sbuf(st, "l3f_gr", [128, TC], F32)
    PS = [P.psum(st, "l3f_ps%d" % i, [128, TC]) for i in range(4)]
    Hsb = P.sbuf(st, "l3f_h", [128, 4 * TC], BF16)
    ctr = P.sbuf(st, "l3f_ctr", [128, TC], F32)
    for e in range(8):
        for kc in range(NDC):
            P.dma("gpsimd", W1sb[:, kc * 512:(kc + 1) * 512],
                  w["w1"][e * 1024 + kc * 128: e * 1024 + (kc + 1) * 128, :])
        for k2 in range(4):
            P.dma("gpsimd", W2sb[:, k2 * 1024:(k2 + 1) * 1024],
                  w["w2"][e * 512 + k2 * 128: e * 512 + (k2 + 1) * 128, :])
        P.dma("sync", brow[:, 0:512], w["b1"][e:e + 1, :])
        for j in range(4):
            emit_transpose(P, b_ps[:, j:j + 1],
                           brow[:, j * 128:(j + 1) * 128], 1, bf16=False)
            P.op("vector", "tensor_copy", b1sb[:, j:j + 1], b_ps[:, j:j + 1])
        P.dma("sync", brow[:, :], w["b2"][e:e + 1, :])
        for j in range(8):
            emit_transpose(P, b_ps[:, j:j + 1],
                           brow[:, j * 128:(j + 1) * 128], 1, bf16=False)
            P.op("vector", "tensor_copy", b2sb[:, j:j + 1], b_ps[:, j:j + 1])
        for tc in range(NTC):
            t0 = tc * TC
            for mc in range(4):
                for kc in range(NDC):
                    P.op("tensor", "matmul", PS[mc][:, :],
                         W1sb[:, kc * 512 + mc * 128: kc * 512 + mc * 128 + 128],
                         XB[:, kc * T + t0: kc * T + t0 + TC],
                         start=(kc == 0), stop=(kc == NDC - 1))
                P.op("scalar", "activation", Hsb[:, mc * TC:(mc + 1) * TC],
                     PS[mc][:, :], func=AF.Gelu, bias=b1sb[:, mc:mc + 1])
            P.dma("sync", grow[:, :], gates_hbm.ap()[e:e + 1, t0:t0 + TC])
            P.op("tensor", "matmul", gr_ps[:, :], ones1f[:, :], grow[:, :],
                 start=True, stop=True, waits=[t_o1])
            P.op("vector", "tensor_copy", gr[:, :], gr_ps[:, :])
            for half in range(2):
                for j in range(4):
                    dc = half * 4 + j
                    for k2 in range(4):
                        P.op("tensor", "matmul", PS[j][:, :],
                             W2sb[:, k2 * 1024 + dc * 128:
                                  k2 * 1024 + dc * 128 + 128],
                             Hsb[:, k2 * TC:(k2 + 1) * TC],
                             start=(k2 == 0), stop=(k2 == 3))
                for j in range(4):
                    dc = half * 4 + j
                    P.op("vector", "scalar_tensor_tensor", ctr[:, :],
                         PS[j][:, :], b2sb[:, dc:dc + 1], gr[:, :],
                         op0=ALU.add, op1=ALU.mult)
                    P.dma("gpsimd", acc_hbm.ap()[dc * 128:(dc + 1) * 128,
                                                 t0:t0 + TC],
                          ctr[:, :],
                          accum_op=(ALU.add if e > 0 else ALU.bypass))
    Xre = [P.sbuf(st, "l3f_xre%d" % i, [128, TC], F32) for i in range(2)]
    Acc = [P.sbuf(st, "l3f_acc%d" % i, [128, TC], F32) for i in range(2)]
    Xo = [P.sbuf(st, "l3f_xo%d" % i, [128, TC], F32) for i in range(2)]
    for tc in range(NTC):
        t0 = tc * TC
        for dc in range(NDC):
            i = dc % 2
            P.dma("sync", Xre[i][:, :], x_in[dc * 128:(dc + 1) * 128,
                                             t0:t0 + TC])
            P.dma("sync", Acc[i][:, :], acc_hbm.ap()[dc * 128:(dc + 1) * 128,
                                                     t0:t0 + TC])
            P.op("vector", "tensor_add", Xo[i][:, :], Xre[i][:, :],
                 Acc[i][:, :])
            P.dma("gpsimd", x_out[dc * 128:(dc + 1) * 128, t0:t0 + TC],
                  Xo[i][:, :])
    P.barrier()
    st.close()


def build_l3_ffn_launch():
    P = Prog("l3ffn")
    x_in = P.dram("x_in", [D, T], F32, kind="ExternalInput").ap()
    x_out = P.dram("x_out", [D, T], F32, kind="ExternalOutput").ap()
    w = {
        "w1": P.dram("ww1", [8 * 1024, 512], F32, kind="ExternalInput").ap(),
        "b1": P.dram("wb1", [8, 512], F32, kind="ExternalInput").ap(),
        "w2": P.dram("ww2", [8 * 512, 1024], F32, kind="ExternalInput").ap(),
        "b2": P.dram("wb2", [8, 1024], F32, kind="ExternalInput").ap(),
        "centroids": P.dram("wcen", [8, 1024], F32,
                            kind="ExternalInput").ap(),
        "w_route": P.dram("wrt", [8, 1024], F32, kind="ExternalInput").ap(),
    }
    bs = ExitStack()
    emit_l3_ffn(P, bs, x_in, x_out, w)
    bs.close()
    nc = P.assemble()
    return P, nc



def emit_l1_ffn(P, bs, x_in, x_out, w, counts_all, prev_mask,
                counts_out=None, mode="full"):
    ones_bf = P.sbuf_const("ones128_bf",
                           np.ones((128, 128), dtype=np.float32)
                           .astype(_bf16np()))
    onesf, t_of = P.sbuf_const("ones128_f32",
                               np.ones((128, 128), dtype=np.float32))
    ones1f, t_o1 = P.sbuf_const("ones1x128_f32",
                                np.ones((1, 128), dtype=np.float32))

    XB, xb_toks = emit_load_xb(P, bs, x_in, name="l1f_xb")
    st = ExitStack()
    gst = ExitStack()
    lnstack = ExitStack()
    _, rstd, mr = emit_layernorm_inplace(P, lnstack, XB, xb_toks, ones_bf,
                                         eps=1e-5, tag="l1f_ln",
                                         stats_stack=gst)
    P.barrier()
    lnstack.close()

    WgT = P.sbuf(gst, "l1f_wgT", [128, NDC * 8], F32)
    for kc in range(NDC):
        P.dma("sync", WgT[:, kc * 8:(kc + 1) * 8],
              w["Wg"][kc * 128:(kc + 1) * 128, :])
    lg_ps = P.psum(gst, "l1f_lgps", [8, TC])
    lg = P.sbuf(gst, "l1f_lg", [8, TC], F32)
    xnf = [P.sbuf(gst, "l1f_xnf%d" % i, [128, TC], F32) for i in range(NDC)]
    lt_ps = P.psum(gst, "l1f_ltps", [128, 8])
    LT = P.sbuf(gst, "l1f_LT", [128, 8], F32)
    m1n = P.sbuf(gst, "l1f_m1n", [128, 1], F32)
    pex = P.sbuf(gst, "l1f_pex", [128, 8], F32)
    den = P.sbuf(gst, "l1f_den", [128, 1], F32)
    v1 = P.sbuf(gst, "l1f_v1", [128, 1], F32)
    v2 = P.sbuf(gst, "l1f_v2", [128, 1], F32)
    oh1 = P.sbuf(gst, "l1f_oh1", [128, 8], F32)
    oh2 = P.sbuf(gst, "l1f_oh2", [128, 8], F32)
    mskd = P.sbuf(gst, "l1f_mskd", [128, 8], F32)
    wk = P.sbuf(gst, "l1f_wk", [128, 2], F32)
    tr_ps = P.psum(gst, "l1f_trps", [128, 128])
    oh1T = P.sbuf(gst, "l1f_oh1T", [8, T], BF16)
    oh2T = P.sbuf(gst, "l1f_oh2T", [8, T], BF16)
    w0T = P.sbuf(gst, "l1f_w0T", [1, T], BF16)
    w1T = P.sbuf(gst, "l1f_w1T", [1, T], BF16)
    for tc in range(NTC):
        t0 = tc * TC
        for dc in range(NDC):
            P.dma("sync", xnf[dc][:, :],
                  x_in[dc * 128:(dc + 1) * 128, t0:t0 + TC])
            P.op("vector", "tensor_mul", xnf[dc][:, :], xnf[dc][:, :],
                 rstd[:, t0:t0 + TC])
            P.op("vector", "tensor_sub", xnf[dc][:, :], xnf[dc][:, :],
                 mr[:, t0:t0 + TC])
            P.op("tensor", "matmul", lg_ps[:, :],
                 WgT[:, dc * 8:(dc + 1) * 8], xnf[dc][:, :],
                 start=(dc == 0), stop=(dc == NDC - 1))
        P.op("vector", "tensor_copy", lg[:, :], lg_ps[:, :])
        for tti in range(TC // 128):
            s0 = tti * 128
            g0 = t0 + s0
            emit_transpose(P, lt_ps[:, 0:8], lg[:, s0:s0 + 128], 8,
                           bf16=False)
            P.op("vector", "tensor_copy", LT[:, :], lt_ps[:, 0:8])
            P.op("vector", "tensor_reduce", m1n[:, :], LT[:, :], axis=AX.X,
                 op=ALU.max, negate=True)
            P.op("scalar", "activation", pex[:, :], LT[:, :], func=AF.Exp,
                 bias=m1n[:, :])
            P.op("vector", "reduce_sum", den[:, :], pex[:, :], axis=AX.X)
            P.op("vector", "reciprocal", den[:, :], den[:, :])
            P.op("vector", "tensor_scalar", pex[:, :], pex[:, :],
                 den[:, :], None, op0=ALU.mult)
            P.op("vector", "tensor_reduce", v1[:, :], pex[:, :], axis=AX.X,
                 op=ALU.max)
            P.op("vector", "tensor_scalar", oh1[:, :], pex[:, :], v1[:, :],
                 None, op0=ALU.is_ge)
            P.op("vector", "scalar_tensor_tensor", mskd[:, :], oh1[:, :],
                 -2.0, pex[:, :], op0=ALU.mult, op1=ALU.add)
            P.op("vector", "tensor_reduce", v2[:, :], mskd[:, :],
                 axis=AX.X, op=ALU.max)
            P.op("vector", "tensor_scalar", oh2[:, :], mskd[:, :],
                 v2[:, :], None, op0=ALU.is_ge)
            P.op("vector", "tensor_add", wk[:, 0:1], v1[:, :], v2[:, :])
            P.op("vector", "tensor_scalar_add", wk[:, 0:1], wk[:, 0:1],
                 1e-9)
            P.op("vector", "reciprocal", wk[:, 1:2], wk[:, 0:1])
            P.op("vector", "tensor_mul", wk[:, 0:1], v1[:, :], wk[:, 1:2])
            P.op("vector", "tensor_mul", wk[:, 1:2], v2[:, :], wk[:, 1:2])
            emit_transpose(P, tr_ps[0:8, 0:128], oh1[:, :], 128,
                           bf16=False)
            P.op("vector", "tensor_copy", oh1T[:, g0:g0 + 128],
                 tr_ps[0:8, 0:128])
            emit_transpose(P, tr_ps[0:8, 0:128], oh2[:, :], 128,
                           bf16=False)
            P.op("vector", "tensor_copy", oh2T[:, g0:g0 + 128],
                 tr_ps[0:8, 0:128])
            emit_transpose(P, tr_ps[0:1, 0:128], wk[:, 0:1], 128,
                           bf16=False)
            P.op("vector", "tensor_copy", w0T[0:1, g0:g0 + 128],
                 tr_ps[0:1, 0:128])
            emit_transpose(P, tr_ps[0:1, 0:128], wk[:, 1:2], 128,
                           bf16=False)
            P.op("vector", "tensor_copy", w1T[0:1, g0:g0 + 128],
                 tr_ps[0:1, 0:128])

    ones8T = P.sbuf_const("ones8xT_bf",
                          np.ones((8, T), dtype=np.float32)
                          .astype(_bf16np()))
    pos = P.sbuf(gst, "l1f_pos", [8, T], F32)
    kp0 = P.sbuf(gst, "l1f_kp0", [8, T], BF16)
    kp1 = P.sbuf(gst, "l1f_kp1", [8, T], BF16)
    casb = P.sbuf(gst, "l1f_casb", [8, 16], F32)
    pmsb = P.sbuf(gst, "l1f_pmsb", [8, 2], F32)
    off_ps = P.psum(gst, "l1f_offps", [16, 2])
    offs = P.sbuf(gst, "l1f_offs", [16, 2], F32)
    offs_hi = P.sbuf(gst, "l1f_offshi", [8, 2], F32)
    cap0 = P.sbuf(gst, "l1f_cap0", [8, 2], F32)
    cout = P.sbuf(gst, "l1f_cout", [8, 2], F32)
    if mode == "counts":
        P.op("vector", "tensor_tensor_scan", pos[:, :], ones8T[0][:, :],
             oh1T[:, :], 0.0, op0=ALU.mult, op1=ALU.add,
             waits=[ones8T[1]])
        P.op("vector", "tensor_copy", cout[:, 0:1], pos[:, T - 1:T])
        P.op("vector", "tensor_tensor_scan", pos[:, :], ones8T[0][:, :],
             oh2T[:, :], 0.0, op0=ALU.mult, op1=ALU.add)
        P.op("vector", "tensor_copy", cout[:, 1:2], pos[:, T - 1:T])
        P.dma("sync", counts_out, cout[:, :])
        P.barrier()
        gst.close()
        st.close()
        return
    P.dma("sync", casb[:, :], counts_all)
    P.dma("sync", pmsb[:, 0:1], prev_mask)
    P.op("vector", "memset", pmsb[:, 1:2], 1.0)
    P.op("tensor", "matmul", off_ps[:, 0:1], casb[:, :], pmsb[:, 0:1],
         start=True, stop=True)
    P.op("tensor", "matmul", off_ps[:, 1:2], casb[:, :], pmsb[:, 1:2],
         start=True, stop=True)
    P.op("vector", "tensor_copy", offs[:, :], off_ps[:, :])
    P.dma("sync", offs_hi[:, :], offs[8:16, :])
    P.op("vector", "tensor_scalar_min", cap0[:, 0:1], offs[0:8, 1:2],
         2048.0)
    P.op("vector", "tensor_add", cap0[:, 1:2], offs_hi[:, 0:1],
         cap0[:, 0:1])

    P.op("vector", "tensor_tensor_scan", pos[:, :], ones8T[0][:, :],
         oh1T[:, :], 0.0, op0=ALU.mult, op1=ALU.add, waits=[ones8T[1]])
    if counts_out is not None:
        P.op("vector", "tensor_copy", cout[:, 0:1], pos[:, T - 1:T])
    P.op("vector", "tensor_scalar", pos[:, :], pos[:, :], offs[0:8, 0:1],
         None, op0=ALU.add)
    P.op("vector", "tensor_scalar", pos[:, :], pos[:, :], 2048.0, None,
         op0=ALU.is_le)
    P.op("vector", "tensor_mul", kp0[:, :], pos[:, :], oh1T[:, :])
    P.op("vector", "tensor_tensor_scan", pos[:, :], ones8T[0][:, :],
         oh2T[:, :], 0.0, op0=ALU.mult, op1=ALU.add)
    if counts_out is not None:
        P.op("vector", "tensor_copy", cout[:, 1:2], pos[:, T - 1:T])
        P.dma("sync", counts_out, cout[:, :])
    P.op("vector", "tensor_scalar", pos[:, :], pos[:, :], cap0[:, 1:2],
         None, op0=ALU.add)
    P.op("vector", "tensor_scalar", pos[:, :], pos[:, :], 2048.0, None,
         op0=ALU.is_le)
    P.op("vector", "tensor_mul", kp1[:, :], pos[:, :], oh2T[:, :])

    ones1x8 = P.sbuf_const("ones1x8_bf", np.ones((1, 8), dtype=np.float32)
                           .astype(_bf16np()))
    wrep_ps = P.psum(gst, "l1f_wrepps", [8, TC])
    comb = P.sbuf(gst, "l1f_comb", [8, TC], F32)
    scr8 = P.sbuf(gst, "l1f_scr8", [8, TC], F32)
    comb_hbm = P.dram("l1f_comb_hbm", [8, T], F32)
    for tc in range(NTC):
        t0 = tc * TC
        P.op("tensor", "matmul", wrep_ps[:, :], ones1x8[0][:, :],
             w0T[0:1, t0:t0 + TC], start=True, stop=True,
             waits=[ones1x8[1]])
        P.op("vector", "tensor_mul", scr8[:, :], wrep_ps[:, :],
             kp0[:, t0:t0 + TC])
        P.op("tensor", "matmul", wrep_ps[:, :], ones1x8[0][:, :],
             w1T[0:1, t0:t0 + TC], start=True, stop=True)
        P.op("vector", "tensor_mul", comb[:, :], wrep_ps[:, :],
             kp1[:, t0:t0 + TC])
        P.op("vector", "tensor_add", comb[:, :], comb[:, :], scr8[:, :])
        P.dma("sync", comb_hbm.ap()[:, t0:t0 + TC], comb[:, :])
    P.barrier()
    gst.close()

    acc_hbm = P.dram("l1f_acc", [D, T], F32)
    W1p = P.dram("l1f_W1p", [D, 4096], BF16)
    W2sb = P.sbuf(st, "l1f_w2sb", [128, 32 * 1024], BF16)
    b1sb = P.sbuf(st, "l1f_b1sb", [128, 32], F32)
    b2sb = P.sbuf(st, "l1f_b2sb", [128, 8], F32)
    brow = P.sbuf(st, "l1f_brow", [1, 1024], F32)
    b_ps = P.psum(st, "l1f_bps", [128, 8])
    grow = P.sbuf(st, "l1f_grow", [1, TC], F32)
    gr_ps = P.psum(st, "l1f_grps", [128, TC])
    gr = P.sbuf(st, "l1f_gr", [128, TC], F32)
    PS = [P.psum(st, "l1f_ps%d" % i, [128, TC]) for i in range(4)]
    wpool = [P.sbuf(st, "l1f_wp%d" % i, [128, TC], BF16) for i in range(8)]
    Hsb = P.sbuf(st, "l1f_h", [128, 32 * TC], BF16)
    ctr = P.sbuf(st, "l1f_ctr", [128, TC], F32)
    for e in range(8):
        for kc in range(NDC):
            P.dma("gpsimd", W1p.ap()[kc * 128:(kc + 1) * 128, :],
                  w["W1"][e * 1024 + kc * 128:e * 1024 + (kc + 1) * 128, :])
        for k2 in range(32):
            P.dma("gpsimd", W2sb[:, k2 * 1024:(k2 + 1) * 1024],
                  w["W2"][e * 4096 + k2 * 128:e * 4096 + (k2 + 1) * 128, :])
        for q in range(4):
            P.dma("sync", brow[:, :], w["b1"][e:e + 1,
                                              q * 1024:(q + 1) * 1024])
            for j in range(8):
                emit_transpose(P, b_ps[:, j:j + 1],
                               brow[:, j * 128:(j + 1) * 128], 1,
                               bf16=False)
                P.op("vector", "tensor_copy", b1sb[:, q * 8 + j:q * 8 + j + 1],
                     b_ps[:, j:j + 1])
        P.dma("sync", brow[:, 0:1024], w["b2"][e:e + 1, :])
        for j in range(8):
            emit_transpose(P, b_ps[:, j:j + 1],
                           brow[:, j * 128:(j + 1) * 128], 1, bf16=False)
            P.op("vector", "tensor_copy", b2sb[:, j:j + 1], b_ps[:, j:j + 1])
        sm1 = StreamedMatmul(P, st, "l1f_sm1_%d" % e, W1p.ap(), NDC, 32,
                             PS, wpool)
        for tc in range(NTC):
            t0 = tc * TC
            h_toks = {}

            def evach(mc, ps, mmt, h_toks=h_toks):
                tok = P.op("scalar", "activation",
                           Hsb[:, mc * TC:(mc + 1) * TC], ps,
                           func=AF.Gelu_apprx_tanh, bias=b1sb[:, mc:mc + 1],
                           waits=[mmt])
                h_toks[mc] = tok
                return tok
            sm1.run(lambda kc, t0=t0: XB[:, kc * T + t0: kc * T + t0 + TC],
                    lambda kc: None, evach)
            P.dma("sync", grow[:, :], comb_hbm.ap()[e:e + 1, t0:t0 + TC])
            P.op("tensor", "matmul", gr_ps[:, :], ones1f[:, :], grow[:, :],
                 start=True, stop=True, waits=[t_o1])
            P.op("vector", "tensor_copy", gr[:, :], gr_ps[:, :])
            for half in range(2):
                for j in range(4):
                    dc = half * 4 + j
                    for k2 in range(32):
                        P.op("tensor", "matmul", PS[j][:, :],
                             W2sb[:, k2 * 1024 + dc * 128:
                                  k2 * 1024 + dc * 128 + 128],
                             Hsb[:, k2 * TC:(k2 + 1) * TC],
                             start=(k2 == 0), stop=(k2 == 31))
                for j in range(4):
                    dc = half * 4 + j
                    P.op("vector", "scalar_tensor_tensor", ctr[:, :],
                         PS[j][:, :], b2sb[:, dc:dc + 1], gr[:, :],
                         op0=ALU.add, op1=ALU.mult)
                    P.dma("gpsimd", acc_hbm.ap()[dc * 128:(dc + 1) * 128,
                                                 t0:t0 + TC],
                          ctr[:, :],
                          accum_op=(ALU.add if e > 0 else ALU.bypass))
    Xre = [P.sbuf(st, "l1f_xre%d" % i, [128, TC], F32) for i in range(2)]
    Acc = [P.sbuf(st, "l1f_acc%d" % i, [128, TC], F32) for i in range(2)]
    Xo = [P.sbuf(st, "l1f_xo%d" % i, [128, TC], F32) for i in range(2)]
    for tc in range(NTC):
        t0 = tc * TC
        for dc in range(NDC):
            i = dc % 2
            P.dma("sync", Xre[i][:, :], x_in[dc * 128:(dc + 1) * 128,
                                             t0:t0 + TC])
            P.dma("sync", Acc[i][:, :], acc_hbm.ap()[dc * 128:(dc + 1) * 128,
                                                     t0:t0 + TC])
            P.op("vector", "tensor_add", Xo[i][:, :], Xre[i][:, :],
                 Acc[i][:, :])
            P.dma("gpsimd", x_out[dc * 128:(dc + 1) * 128, t0:t0 + TC],
                  Xo[i][:, :])
    P.barrier()
    st.close()


def build_l1_ffn_launch():
    P = Prog("l1ffn")
    x_in = P.dram("x_in", [D, T], F32, kind="ExternalInput").ap()
    x_out = P.dram("x_out", [D, T], F32, kind="ExternalOutput").ap()
    counts_out = P.dram("counts_out", [8, 2], F32, kind="ExternalOutput").ap()
    w = {
        "Wg": P.dram("wWg", [D, 8], F32, kind="ExternalInput").ap(),
        "W1": P.dram("wW1", [8 * D, 4096], F32, kind="ExternalInput").ap(),
        "b1": P.dram("wb1", [8, 4096], F32, kind="ExternalInput").ap(),
        "W2": P.dram("wW2", [8 * 4096, D], F32, kind="ExternalInput").ap(),
        "b2": P.dram("wb2", [8, 1024], F32, kind="ExternalInput").ap(),
    }
    counts_all = P.dram("counts_all", [8, 16], F32,
                        kind="ExternalInput").ap()
    prev_mask = P.dram("prev_mask", [8, 1], F32, kind="ExternalInput").ap()
    bs = ExitStack()
    emit_l1_ffn(P, bs, x_in, x_out, w, counts_all, prev_mask,
                counts_out=counts_out)
    bs.close()
    nc = P.assemble()
    return P, nc



def make_attn_scratch(P, st, tag, dh):
    TCQ = 512
    return dict(
        spt=P.psum(st, tag + "_spt", [128, TCQ]),
        opt=P.psum(st, tag + "_opt", [dh, TCQ]),
        dpt=P.psum(st, tag + "_dpt", [1, TCQ]),
        rpt=P.psum(st, tag + "_rpt", [dh, TCQ]),
        ssb=P.sbuf(st, tag + "_ssb", [128, TCQ], F32),
        PT=[P.sbuf(st, tag + "_pt%d" % i, [128, TCQ], BF16)
            for i in range(T // 128)],
        dsb=P.sbuf(st, tag + "_dsb", [1, TCQ], F32),
        rsb=P.sbuf(st, tag + "_rsb", [dh, TCQ], F32),
    )


def emit_causal_attn_rt(P, scr, tag, rt, QRtile, KR, VT, n_h, dh, n_kv,
                        scale, mhat_rep, out_cb, masks, ones_bf,
                        qr_tok=None):
    TCQ = 512
    spt, opt, dpt, rpt = scr["spt"], scr["opt"], scr["dpt"], scr["rpt"]
    ssb, PT, dsb, rsb = scr["ssb"], scr["PT"], scr["dsb"], scr["rsb"]
    ones1f, t_o1 = P.sbuf_const("ones1x128_f32",
                                np.ones((1, 128), dtype=np.float32))
    hpt = 128 // dh
    grp = n_h // n_kv
    for hr in range(hpt):
        h = rt * hpt + hr
        kv = h // grp
        for tq in range(T // TCQ):
            q0 = tq * TCQ
            n_tk = (q0 + TCQ) // 128
            for tk in range(n_tk):
                k0 = tk * 128
                lhs = KR[hr][hr * dh:(hr + 1) * dh,
                             kv * T + k0: kv * T + k0 + 128]
                rhs = QRtile[hr * dh:(hr + 1) * dh, q0:q0 + TCQ]
                P.op("tensor", "matmul", spt[:, :], lhs, rhs,
                     start=True, stop=True,
                     waits=[qr_tok] if qr_tok else [])
                if k0 >= q0:
                    dlt = (k0 - q0) // 128
                    P.op("vector", "tensor_add", ssb[:, :], spt[:, :],
                         masks[0][dlt][:, :], waits=[masks[1][dlt]])
                    src = ssb[:, :]
                else:
                    src = spt[:, :]
                pt = PT[tk]
                P.op("scalar", "activation", pt[:, :], src, func=AF.Exp,
                     scale=float(scale), bias=mhat_rep[:, h:h + 1])
                P.op("tensor", "matmul", dpt[:, :], ones_bf[0][:, 0:1],
                     pt[:, :], start=(tk == 0), stop=(tk == n_tk - 1),
                     waits=[ones_bf[1]])
                P.op("tensor", "matmul", opt[:, :],
                     VT[:, tk * n_kv * dh + kv * dh:
                        tk * n_kv * dh + (kv + 1) * dh],
                     pt[:, :], start=(tk == 0), stop=(tk == n_tk - 1))
            P.op("vector", "reciprocal", dsb[:, :], dpt[:, :])
            P.op("tensor", "matmul", rpt[:, :], ones1f[0:1, 0:dh],
                 dsb[:, :], start=True, stop=True, waits=[t_o1])
            P.op("vector", "tensor_copy", rsb[:, :], rpt[:, :])
            out_cb(h, tq, opt, rsb)


def make_causal_masks(P):
    outs, toks = [], []
    for dlt in range(4):
        i = np.arange(128)[:, None] + 128 * dlt
        j = np.arange(512)[None, :]
        m = np.where(i <= j, 0.0, -1e30).astype(np.float32)
        h, t = P.sbuf_const("cmask%d" % dlt, m)
        outs.append(h)
        toks.append(t)
    return outs, toks


def emit_row_to_pp(P, stack, tag, row_hbm_ap, n):
    nch = n // 128
    row = P.sbuf(stack, tag + "_row", [1, min(n, 1024)], F32)
    tp = P.psum(stack, tag + "_tp", [128, min(nch, 8)])
    out = P.sbuf(stack, tag + "_pp", [128, nch], F32)
    for q in range((n + 1023) // 1024):
        cw = min(1024, n - q * 1024)
        P.dma("sync", row[:, 0:cw], row_hbm_ap[0:1, q * 1024:q * 1024 + cw])
        for j in range(cw // 128):
            emit_transpose(P, tp[:, j % 8:j % 8 + 1],
                           row[:, j * 128:(j + 1) * 128], 1, bf16=False)
            P.op("vector", "tensor_copy", out[:, q * 8 + j:q * 8 + j + 1],
                 tp[:, j % 8:j % 8 + 1])
    return out


def emit_rmsnorm_inplace(P, stack, XB, ones_bf, eps, gpp, tag,
                         stats_stack=None):
    sst = stats_stack if stats_stack is not None else stack
    rstd = P.sbuf(sst, tag + "_rstd", [128, NTC * TC], F32)
    sq_ps = P.psum(stack, tag + "_sq", [128, TC])
    xsq = [P.sbuf(stack, tag + "_xsq%d" % i, [128, TC], BF16)
           for i in range(NDC)]
    scr = P.sbuf(stack, tag + "_scr", [128, TC], F32)
    for tc in range(NTC):
        t0 = tc * TC
        for dc in range(NDC):
            xtile = XB[:, dc * T + t0: dc * T + t0 + TC]
            P.op("scalar", "activation", xsq[dc][:, :], xtile,
                 func=AF.Square)
            P.op("tensor", "matmul", sq_ps[:, :], ones_bf[0][:, :],
                 xsq[dc][:, :], start=(dc == 0), stop=(dc == NDC - 1),
                 waits=[ones_bf[1]])
        P.op("vector", "tensor_scalar", scr[:, :], sq_ps[:, :],
             1.0 / D, float(eps), op0=ALU.mult, op1=ALU.add)
        P.op("scalar", "activation", scr[:, :], scr[:, :], func=AF.Ln)
        P.op("scalar", "activation", rstd[:, t0:t0 + TC], scr[:, :],
             func=AF.Exp, scale=-0.5)
    scr2 = P.sbuf(stack, tag + "_scr2", [128, TC], F32)
    for dc in range(NDC):
        for tc in range(NTC):
            t0 = tc * TC
            xtile = XB[:, dc * T + t0: dc * T + t0 + TC]
            P.op("vector", "tensor_mul", scr2[:, :], xtile,
                 rstd[:, t0:t0 + TC])
            P.op("vector", "tensor_scalar", xtile, scr2[:, :],
                 gpp[:, dc:dc + 1], None, op0=ALU.mult)
    return rstd



def emit_l0_ffn(P, bs, x_in, x_out, w, counts_all, prev_mask,
                counts_out=None, mode="full", h_ext=None):
    ones_bf = P.sbuf_const("ones128_bf",
                           np.ones((128, 128), dtype=np.float32)
                           .astype(_bf16np()))
    ones1f, t_o1 = P.sbuf_const("ones1x128_f32",
                                np.ones((1, 128), dtype=np.float32))
    if h_ext is not None:
        h_ap = h_ext
    else:
        h_ap = P.dram("l0f_h", [D, T], F32).ap()
    if mode == "apply":
        emit_l0_ffn_phaseE(P, bs, x_in, x_out, w, counts_all, prev_mask,
                           counts_out, mode, h_ap)
        return
    masks = make_causal_masks(P)
    lnx_hbm = P.dram("l0f_lnx", [D, T], F32)
    qr_hbm = P.dram("l0f_qrh", [D, T], BF16)
    kr_hbm = P.dram("l0f_krh", [256, T], BF16)
    vt_hbm = P.dram("l0f_vth", [T, 256], BF16)

    sa = ExitStack()
    XB, xb_toks = emit_load_xb(P, sa, x_in, name="l0f_xb")
    lnstack = ExitStack()
    ln_toks, rstd0, mr0 = emit_layernorm_inplace(
        P, lnstack, XB, xb_toks, ones_bf, eps=1e-5, tag="l0f_ln")
    lf = [P.sbuf(lnstack, "l0f_lf%d" % i, [128, TC], F32) for i in range(2)]
    for tc in range(NTC):
        t0 = tc * TC
        for dc in range(NDC):
            i = dc % 2
            P.dma("sync", lf[i][:, :], x_in[dc * 128:(dc + 1) * 128,
                                            t0:t0 + TC])
            P.op("vector", "tensor_mul", lf[i][:, :], lf[i][:, :],
                 rstd0[:, t0:t0 + TC])
            P.op("vector", "tensor_sub", lf[i][:, :], lf[i][:, :],
                 mr0[:, t0:t0 + TC])
            P.dma("gpsimd", lnx_hbm.ap()[dc * 128:(dc + 1) * 128,
                                         t0:t0 + TC], lf[i][:, :])
    P.barrier()
    lnstack.close()
    g1pp = emit_row_to_pp(P, sa, "l0f_g1", w["g1"], D)
    r1stack = ExitStack()
    emit_rmsnorm_inplace(P, r1stack, XB, ones_bf, 1e-6, g1pp, "l0f_r1")
    P.barrier()
    r1stack.close()

    th = 10000.0 ** (-np.arange(0, 64, 2, dtype=np.float64) / 64)
    ang = np.outer(th, np.arange(T, dtype=np.float64))
    cs = np.concatenate([np.cos(ang), np.cos(ang)], 0).astype(np.float32)
    sn = np.concatenate([np.sin(ang), np.sin(ang)], 0).astype(np.float32)
    cosD = P.const_input("ropec_d", cs.astype(_bf16np()))
    sinD = P.const_input("ropes_d", sn.astype(_bf16np()))
    cosT = P.sbuf(sa, "l0f_cos", [64, T], BF16)
    sinT = P.sbuf(sa, "l0f_sin", [64, T], BF16)
    P.dma("sync", cosT[:, :], cosD[:, :])
    P.dma("sync", sinT[:, :], sinD[:, :])

    wpan = [P.sbuf(sa, "l0f_wpan%d" % i, [128, 1024], BF16)
            for i in range(2)]
    npan = [P.sbuf(sa, "l0f_npan%d" % i, [128, 1024], BF16)
            for i in range(2)]
    q_ps = P.psum(sa, "l0f_qps", [128, TC])
    n_ps = P.psum(sa, "l0f_nps", [128, TC])
    rsc = P.sbuf(sa, "l0f_rsc", [64, TC], F32)
    rsc2 = P.sbuf(sa, "l0f_rsc2", [64, TC], F32)
    qrt = P.sbuf(sa, "l0f_qrt", [128, TC], BF16)
    sqb = P.sbuf(sa, "l0f_sqb", [128, TC], BF16)
    nq_ps = P.psum(sa, "l0f_nqps", [2, TC])
    qn2 = P.sbuf(sa, "l0f_qn2", [2, 8], F32)
    kn2 = P.sbuf(sa, "l0f_kn2", [2, 2], F32)
    nred = P.sbuf(sa, "l0f_nred", [2, 1], F32)
    sel2 = P.sbuf_const("sel2h", np.kron(np.eye(2), np.ones((64, 1)))
                        .astype(np.float32).astype(_bf16np()))

    def rope_proj(dst_hbm, n_rows, w_ap, dstn2):
        for rt in range(n_rows // 128):
            wp = wpan[rt % 2]
            nh = npan[rt % 2]
            load_panel(P, wp, w_ap, rt * 128, 128, NDC)
            P.op("vector", "tensor_scalar_mul",
                 nh[:, :].rearrange("p (g x) -> p g x", x=64)[:, :, 0:32],
                 wp[:, :].rearrange("p (g x) -> p g x", x=64)[:, :, 32:64],
                 -1.0)
            P.op("vector", "tensor_copy",
                 nh[:, :].rearrange("p (g x) -> p g x", x=64)[:, :, 32:64],
                 wp[:, :].rearrange("p (g x) -> p g x", x=64)[:, :, 0:32])
            for tc in range(NTC):
                t0 = tc * TC
                for kc in range(NDC):
                    P.op("tensor", "matmul", q_ps[:, :],
                         wp[:, kc * 128:(kc + 1) * 128],
                         XB[:, kc * T + t0:kc * T + t0 + TC],
                         start=(kc == 0), stop=(kc == NDC - 1))
                    P.op("tensor", "matmul", n_ps[:, :],
                         nh[:, kc * 128:(kc + 1) * 128],
                         XB[:, kc * T + t0:kc * T + t0 + TC],
                         start=(kc == 0), stop=(kc == NDC - 1))
                for hf in range(2):
                    r0 = hf * 64
                    P.op("vector", "tensor_mul", rsc[:, :],
                         q_ps[r0:r0 + 64, 0:TC], cosT[:, t0:t0 + TC])
                    P.op("vector", "tensor_mul", rsc2[:, :],
                         n_ps[r0:r0 + 64, 0:TC], sinT[:, t0:t0 + TC])
                    P.op("vector", "tensor_add",
                         qrt[r0:r0 + 64, 0:TC], rsc[:, :], rsc2[:, :])
                P.dma("gpsimd", dst_hbm.ap()[rt * 128:(rt + 1) * 128,
                                             t0:t0 + TC], qrt[:, :])
                P.op("scalar", "activation", sqb[:, :], qrt[:, :],
                     func=AF.Square)
                P.op("tensor", "matmul", nq_ps[:, :], sel2[0][:, :],
                     sqb[:, :], start=True, stop=True, waits=[sel2[1]])
                P.op("vector", "tensor_reduce", nred[:, :], nq_ps[:, :],
                     axis=AX.X, op=ALU.max)
                if tc == 0:
                    P.op("vector", "tensor_copy", dstn2[:, rt:rt + 1],
                         nred[:, :])
                else:
                    P.op("vector", "tensor_max", dstn2[:, rt:rt + 1],
                         dstn2[:, rt:rt + 1], nred[:, :])
    rope_proj(qr_hbm, 1024, w["wq"], qn2)
    rope_proj(kr_hbm, 256, w["wk"], kn2)

    wvp = P.sbuf(sa, "l0f_wvp", [128, 8 * 256], BF16)
    for kc in range(NDC):
        P.dma("gpsimd", wvp[:, kc * 256:(kc + 1) * 256],
              w["wv"][kc * 128:(kc + 1) * 128, :])
    vt_ps = P.psum(sa, "l0f_vtps", [128, 256])
    vt_sb = P.sbuf(sa, "l0f_vtsb", [128, 256], BF16)
    for tt in range(T // 128):
        for kc in range(NDC):
            P.op("tensor", "matmul", vt_ps[:, :],
                 XB[:, kc * T + tt * 128: kc * T + (tt + 1) * 128],
                 wvp[:, kc * 256:(kc + 1) * 256],
                 start=(kc == 0), stop=(kc == NDC - 1))
        P.op("vector", "tensor_copy", vt_sb[:, :], vt_ps[:, :])
        P.dma("gpsimd", vt_hbm.ap()[tt * 128:(tt + 1) * 128, :],
              vt_sb[:, :])
    nq_hbm = P.dram("l0f_nqh", [2, 8], F32)
    nk_hbm = P.dram("l0f_nkh", [2, 2], F32)
    P.dma("sync", nq_hbm.ap()[:, :], qn2[:, :])
    P.dma("sync", nk_hbm.ap()[:, :], kn2[:, :])
    P.barrier()
    sa.close()

    sc_ = ExitStack()
    KRa = P.sbuf(sc_, "l0f_kra", [64, 4 * T], BF16)
    KRb = P.sbuf(sc_, "l0f_krb", [128, 4 * T], BF16)
    KR = [KRa, KRb]
    VT = P.sbuf(sc_, "l0f_vt", [128, 32 * 256], BF16)
    ATT = P.sbuf(sc_, "l0f_attb", [128, 8 * T], BF16)
    for kv in range(4):
        P.dma("sync", KRa[0:64, kv * T:(kv + 1) * T],
              kr_hbm.ap()[kv * 64:(kv + 1) * 64, :])
        P.dma("sync", KRb[64:128, kv * T:(kv + 1) * T],
              kr_hbm.ap()[kv * 64:(kv + 1) * 64, :])
    for tt in range(32):
        P.dma("sync", VT[:, tt * 256:(tt + 1) * 256],
              vt_hbm.ap()[tt * 128:(tt + 1) * 128, :])
    qrow = P.sbuf(sc_, "l0f_qrow", [1, 16], F32)
    krow = P.sbuf(sc_, "l0f_krow", [1, 16], F32)
    P.dma("sync", qrow[0:1, :].rearrange("a (q r) -> a q r", r=2),
          nq_hbm.ap().rearrange("r q -> q r")[None, :, :],
          allow_slow_non_contiguous=True)
    for h in range(16):
        kv = h // 4
        P.dma("sync", krow[0:1, h:h + 1],
              nk_hbm.ap()[kv % 2:kv % 2 + 1, kv // 2:kv // 2 + 1],
              allow_slow_non_contiguous=True)
    negm = P.sbuf(sc_, "l0f_negm", [1, 16], F32)
    P.op("vector", "tensor_mul", negm[:, :], qrow[:, :], krow[:, :])
    P.op("scalar", "activation", negm[:, :], negm[:, :], func=AF.Sqrt)
    P.op("vector", "tensor_scalar_mul", negm[:, :], negm[:, :],
         -1.02 / 8.0)
    mh_ps = P.psum(sc_, "l0f_mhps", [128, 16])
    P.op("tensor", "matmul", mh_ps[:, :], ones1f[0:1, :], negm[:, :],
         start=True, stop=True, waits=[t_o1])
    mhat_rep = P.sbuf(sc_, "l0f_mhat", [128, 16], F32)
    P.op("vector", "tensor_copy", mhat_rep[:, :], mh_ps[:, :])

    QRt = [P.sbuf(sc_, "l0f_qrt0", [128, T], BF16)]
    att_sc = P.sbuf(sc_, "l0f_attsc", [64, TC], F32)
    ascr = make_attn_scratch(P, sc_, "l0f_at", 64)
    qr_loads = {}
    for rt in range(8):
        qr_loads[rt] = P.dma("sync", QRt[0][:, :],
                             qr_hbm.ap()[rt * 128:(rt + 1) * 128, :])

        def out_cb(h, tq, opt, rsb, rt=rt):
            half = h % 2
            t0 = tq * TC
            P.op("vector", "tensor_mul", att_sc[:, :], opt[:, 0:TC],
                 rsb[:, 0:TC])
            P.op("vector", "tensor_copy",
                 ATT[half * 64:(half + 1) * 64,
                     rt * T + t0: rt * T + t0 + TC], att_sc[:, :])
        emit_causal_attn_rt(P, ascr, "l0f_at%d" % rt, rt, QRt[0], KR,
                            VT, 16, 64, 4, 0.125, mhat_rep, out_cb, masks,
                            ones_bf, qr_loads[rt])

    wop = [P.sbuf(sc_, "l0f_wop0", [128, 1024], BF16)]
    ho_ps = P.psum(sc_, "l0f_hops", [128, TC])
    lnx_sl = P.sbuf(sc_, "l0f_lnxsl", [128, TC], F32)
    hosb = P.sbuf(sc_, "l0f_hosb", [128, TC], F32)
    for tc in range(NTC):
        t0 = tc * TC
        for dc in range(NDC):
            wp = wop[0]
            load_panel(P, wp, w["wo"], dc * 128, 128, NDC)
            for kc in range(NDC):
                P.op("tensor", "matmul", ho_ps[:, :],
                     wp[:, kc * 128:(kc + 1) * 128],
                     ATT[:, kc * T + t0: kc * T + t0 + TC],
                     start=(kc == 0), stop=(kc == NDC - 1))
            P.dma("sync", lnx_sl[:, :],
                  lnx_hbm.ap()[dc * 128:(dc + 1) * 128, t0:t0 + TC])
            P.op("vector", "tensor_add", hosb[:, :], ho_ps[:, :],
                 lnx_sl[:, :])
            P.dma("gpsimd", h_ap[dc * 128:(dc + 1) * 128,
                                 t0:t0 + TC], hosb[:, :])
    P.barrier()
    sc_.close()
    emit_l0_ffn_phaseE(P, bs, x_in, x_out, w, counts_all, prev_mask,
                       counts_out, mode, h_ap)


def emit_l0_ffn_phaseE(P, bs, x_in, x_out, w, counts_all, prev_mask,
                       counts_out, mode, h_ap):
    ones_bf = P.sbuf_const("ones128_bf",
                           np.ones((128, 128), dtype=np.float32)
                           .astype(_bf16np()))
    ones1f, t_o1 = P.sbuf_const("ones1x128_f32",
                                np.ones((1, 128), dtype=np.float32))
    st = ExitStack()
    gst = ExitStack()
    XH, xh_toks = emit_load_xb(P, st, h_ap, name="l0f_xh2")
    g2pp = emit_row_to_pp(P, gst, "l0f_g2", w["g2"], D)
    r2stack = ExitStack()
    rstd2 = emit_rmsnorm_inplace(P, r2stack, XH, ones_bf, 1e-6, g2pp,
                                 "l0f_r2", stats_stack=gst)
    P.barrier()
    r2stack.close()

    rwT = P.sbuf(gst, "l0f_rwT", [128, NDC * 8], F32)
    for kc in range(NDC):
        P.dma("sync", rwT[:, kc * 8:(kc + 1) * 8],
              w["router_w"][kc * 128:(kc + 1) * 128, :])
    rb = P.sbuf(gst, "l0f_rb", [1, 8], F32)
    P.dma("sync", rb[:, :], w["router_b"][0:1, :])
    rb8 = P.sbuf(gst, "l0f_rb8", [8, 1], F32)
    rb_ps = P.psum(gst, "l0f_rbps", [8, 8])
    ones1x8 = P.sbuf_const("ones1x8_f32b", np.ones((1, 8), dtype=np.float32))
    emit_transpose(P, rb_ps[0:8, 0:1], rb[0:1, 0:8], 1, bf16=False)
    P.op("vector", "tensor_copy", rb8[:, :], rb_ps[0:8, 0:1])

    lg_ps = P.psum(gst, "l0f_lgps", [8, TC])
    lg = P.sbuf(gst, "l0f_lg", [8, TC], F32)
    xnf = [P.sbuf(gst, "l0f_xnf%d" % i, [128, TC], F32) for i in range(NDC)]
    lt_ps = P.psum(gst, "l0f_ltps", [128, 8])
    LT = P.sbuf(gst, "l0f_LT", [128, 8], F32)
    v1 = P.sbuf(gst, "l0f_v1", [128, 1], F32)
    v2 = P.sbuf(gst, "l0f_v2", [128, 1], F32)
    oh1 = P.sbuf(gst, "l0f_oh1", [128, 8], F32)
    oh2 = P.sbuf(gst, "l0f_oh2", [128, 8], F32)
    mskd = P.sbuf(gst, "l0f_mskd", [128, 8], F32)
    sk = P.sbuf(gst, "l0f_sk", [128, 2], F32)
    tr_ps = P.psum(gst, "l0f_trps", [128, 128])
    oh1T = P.sbuf(gst, "l0f_oh1T", [8, T], BF16)
    oh2T = P.sbuf(gst, "l0f_oh2T", [8, T], BF16)
    s0T = P.sbuf(gst, "l0f_s0T", [1, T], BF16)
    s1T = P.sbuf(gst, "l0f_s1T", [1, T], BF16)
    for tc in range(NTC):
        t0 = tc * TC
        for dc in range(NDC):
            P.dma("sync", xnf[dc][:, :],
                  h_ap[dc * 128:(dc + 1) * 128, t0:t0 + TC])
            P.op("vector", "tensor_mul", xnf[dc][:, :], xnf[dc][:, :],
                 rstd2[:, t0:t0 + TC])
            P.op("vector", "tensor_scalar", xnf[dc][:, :], xnf[dc][:, :],
                 g2pp[:, dc:dc + 1], None, op0=ALU.mult)
            P.op("tensor", "matmul", lg_ps[:, :],
                 rwT[:, dc * 8:(dc + 1) * 8], xnf[dc][:, :],
                 start=(dc == 0), stop=(dc == NDC - 1))
        P.op("vector", "tensor_scalar", lg[:, :], lg_ps[:, :], rb8[:, 0:1],
             None, op0=ALU.add)
        for tti in range(TC // 128):
            s0_ = tti * 128
            g0 = t0 + s0_
            emit_transpose(P, lt_ps[:, 0:8], lg[:, s0_:s0_ + 128], 8,
                           bf16=False)
            P.op("vector", "tensor_copy", LT[:, :], lt_ps[:, 0:8])
            P.op("vector", "tensor_reduce", v1[:, :], LT[:, :], axis=AX.X,
                 op=ALU.max)
            P.op("vector", "tensor_scalar", oh1[:, :], LT[:, :], v1[:, :],
                 None, op0=ALU.is_ge)
            P.op("vector", "scalar_tensor_tensor", mskd[:, :], oh1[:, :],
                 -1e30, LT[:, :], op0=ALU.mult, op1=ALU.add)
            P.op("vector", "tensor_reduce", v2[:, :], mskd[:, :],
                 axis=AX.X, op=ALU.max)
            P.op("vector", "tensor_scalar", oh2[:, :], mskd[:, :],
                 v2[:, :], None, op0=ALU.is_ge)
            P.op("vector", "tensor_sub", sk[:, 0:1], v1[:, :], v2[:, :])
            P.op("scalar", "activation", sk[:, 0:1], sk[:, 0:1],
                 func=AF.Sigmoid)
            P.op("vector", "scalar_tensor_tensor", sk[:, 1:2], sk[:, 0:1],
                 -1.0, sk[:, 0:1], op0=ALU.mult, op1=ALU.bypass)
            P.op("vector", "tensor_scalar", sk[:, 1:2], sk[:, 0:1], -1.0,
                 1.0, op0=ALU.mult, op1=ALU.add)
            emit_transpose(P, tr_ps[0:8, 0:128], oh1[:, :], 128,
                           bf16=False)
            P.op("vector", "tensor_copy", oh1T[:, g0:g0 + 128],
                 tr_ps[0:8, 0:128])
            emit_transpose(P, tr_ps[0:8, 0:128], oh2[:, :], 128,
                           bf16=False)
            P.op("vector", "tensor_copy", oh2T[:, g0:g0 + 128],
                 tr_ps[0:8, 0:128])
            emit_transpose(P, tr_ps[0:1, 0:128], sk[:, 0:1], 128,
                           bf16=False)
            P.op("vector", "tensor_copy", s0T[:, g0:g0 + 128],
                 tr_ps[0:1, 0:128])
            emit_transpose(P, tr_ps[0:1, 0:128], sk[:, 1:2], 128,
                           bf16=False)
            P.op("vector", "tensor_copy", s1T[:, g0:g0 + 128],
                 tr_ps[0:1, 0:128])

    ones8T = P.sbuf_const("ones8xT_bf", np.ones((8, T), dtype=np.float32)
                          .astype(_bf16np()))
    cstk = ExitStack()
    pos0 = P.sbuf(cstk, "l0f_pos0", [8, T], F32)
    pos1 = P.sbuf(cstk, "l0f_pos1", [8, T], F32)
    casb = P.sbuf(cstk, "l0f_casb", [8, 16], F32)
    pmsb = P.sbuf(cstk, "l0f_pmsb", [8, 1], F32)
    off_ps = P.psum(cstk, "l0f_offps", [16, 1])
    offs = P.sbuf(cstk, "l0f_offs", [16, 1], F32)
    offs_hi = P.sbuf(cstk, "l0f_offshi", [8, 1], F32)
    cout = P.sbuf(cstk, "l0f_cout", [8, 2], F32)
    P.op("vector", "tensor_tensor_scan", pos0[:, :], ones8T[0][:, :],
         oh1T[:, :], 0.0, op0=ALU.mult, op1=ALU.add, waits=[ones8T[1]])
    P.op("vector", "tensor_tensor_scan", pos1[:, :], ones8T[0][:, :],
         oh2T[:, :], 0.0, op0=ALU.mult, op1=ALU.add)
    if counts_out is not None:
        P.op("vector", "tensor_copy", cout[:, 0:1], pos0[:, T - 1:T])
        P.op("vector", "tensor_copy", cout[:, 1:2], pos1[:, T - 1:T])
        P.dma("sync", counts_out, cout[:, :])
    if mode == "counts":
        P.barrier()
        cstk.close()
        gst.close()
        st.close()
        return
    P.dma("sync", casb[:, :], counts_all)
    P.dma("sync", pmsb[:, :], prev_mask)
    P.op("tensor", "matmul", off_ps[:, :], casb[:, :], pmsb[:, :],
         start=True, stop=True)
    P.op("vector", "tensor_copy", offs[:, :], off_ps[:, :])
    P.dma("sync", offs_hi[:, :], offs[8:16, :])
    P.op("vector", "tensor_scalar", pos0[:, :], pos0[:, :], offs[0:8, 0:1],
         None, op0=ALU.add)
    P.op("vector", "tensor_add", pos1[:, :], pos1[:, :], pos0[:, :])
    P.op("vector", "tensor_scalar", pos1[:, :], pos1[:, :],
         offs_hi[:, 0:1], None, op0=ALU.add)
    P.op("vector", "tensor_scalar", pos0[:, :], pos0[:, :], 4096.0, None,
         op0=ALU.is_le)
    P.op("vector", "tensor_mul", oh1T[:, :], pos0[:, :], oh1T[:, :])
    P.op("vector", "tensor_scalar", pos1[:, :], pos1[:, :], 4096.0, None,
         op0=ALU.is_le)
    P.op("vector", "tensor_mul", oh2T[:, :], pos1[:, :], oh2T[:, :])
    P.barrier()
    cstk.close()

    evec = P.sbuf_const("evec8_bf", np.arange(8, dtype=np.float32)
                        .reshape(8, 1).astype(_bf16np()))
    evecf = P.sbuf_const("evec8_f32", np.arange(8, dtype=np.float32)
                         .reshape(8, 1))
    asg_ps = P.psum(gst, "l0f_asgps", [1, TC])
    mrow = P.sbuf(gst, "l0f_mrow", [1, TC], F32)
    a1row = P.sbuf(gst, "l0f_a1row", [1, TC], F32)
    wy = P.sbuf(gst, "l0f_wy", [1, TC], F32)
    tm0 = P.sbuf(gst, "l0f_tm0", [1, TC], F32)
    tm1 = P.sbuf(gst, "l0f_tm1", [1, TC], F32)
    g8_ps = P.psum(gst, "l0f_g8ps", [8, TC])
    we_hbm = P.dram("l0f_we", [8, T], F32)
    wx_hbm = P.dram("l0f_wx", [1, T], F32)
    wesb = P.sbuf(gst, "l0f_wesb", [8, TC], F32)
    for tc in range(NTC):
        t0 = tc * TC
        P.op("tensor", "matmul", asg_ps[:, :], evec[0][:, :],
             oh1T[:, t0:t0 + TC], start=True, stop=True, waits=[evec[1]])
        P.op("vector", "tensor_copy", mrow[:, :], asg_ps[:, :])
        P.op("tensor", "matmul", asg_ps[:, :], evec[0][:, :],
             oh2T[:, t0:t0 + TC], start=True, stop=True)
        P.op("vector", "tensor_copy", a1row[:, :], asg_ps[:, :])
        P.op("vector", "tensor_max", mrow[:, :], mrow[:, :], a1row[:, :])
        P.op("tensor", "matmul", asg_ps[:, :], ones8T[0][:, 0:1],
             oh1T[:, t0:t0 + TC], start=True, stop=True)
        P.op("vector", "tensor_mul", tm0[:, :], asg_ps[:, :],
             s0T[:, t0:t0 + TC])
        P.op("tensor", "matmul", asg_ps[:, :], ones8T[0][:, 0:1],
             oh2T[:, t0:t0 + TC], start=True, stop=True)
        P.op("vector", "tensor_mul", tm1[:, :], asg_ps[:, :],
             s1T[:, t0:t0 + TC])
        P.op("vector", "tensor_add", wy[:, :], tm0[:, :], tm1[:, :])
        P.op("tensor", "matmul", g8_ps[:, :], ones1x8[0][0:1, :],
             mrow[:, :], start=True, stop=True)
        P.op("vector", "tensor_scalar", wesb[:, :], g8_ps[:, :],
             evecf[0][:, :], None, op0=ALU.is_equal, waits=[evecf[1]])
        P.op("tensor", "matmul", g8_ps[:, :], ones1x8[0][0:1, :],
             wy[:, :], start=True, stop=True)
        P.op("vector", "tensor_mul", wesb[:, :], wesb[:, :], g8_ps[:, :])
        P.dma("sync", we_hbm.ap()[:, t0:t0 + TC], wesb[:, :])
        P.op("vector", "tensor_scalar", wy[:, :],
             wy[:, :], -1.0, 1.0, op0=ALU.mult, op1=ALU.add)
        P.dma("sync", wx_hbm.ap()[:, t0:t0 + TC], wy[:, :])
    P.barrier()
    gst.close()

    acc_hbm = P.dram("l0f_acc", [D, T], F32)
    w1sb = P.sbuf(st, "l0f_w1sb", [128, 8 * 2048], BF16)
    w2sb = P.sbuf(st, "l0f_w2sb", [128, 8 * 2048], BF16)
    w3sb = P.sbuf(st, "l0f_w3sb", [128, 16 * 1024], BF16)
    PS = [P.psum(st, "l0f_ps%d" % i, [128, TC]) for i in range(8)]
    H2 = P.sbuf(st, "l0f_h2sb", [128, 16 * TC], BF16)
    asb = P.sbuf(st, "l0f_asb", [128, TC], F32)
    usb = P.sbuf(st, "l0f_usb", [128, TC], F32)
    grow = P.sbuf(st, "l0f_grow", [1, TC], F32)
    gr = P.sbuf(st, "l0f_gr", [128, TC], F32)
    ctr = P.sbuf(st, "l0f_ctr", [128, TC], F32)
    for e in range(8):
        for kc in range(NDC):
            P.dma("gpsimd", w1sb[:, kc * 2048:(kc + 1) * 2048],
                  w["w1"][e * 1024 + kc * 128:e * 1024 + (kc + 1) * 128, :])
            P.dma("gpsimd", w2sb[:, kc * 2048:(kc + 1) * 2048],
                  w["w2"][e * 1024 + kc * 128:e * 1024 + (kc + 1) * 128, :])
        for k3 in range(16):
            P.dma("gpsimd", w3sb[:, k3 * 1024:(k3 + 1) * 1024],
                  w["w3"][e * 2048 + k3 * 128:e * 2048 + (k3 + 1) * 128, :])
        for tc in range(NTC):
            t0 = tc * TC
            for hb in range(4):
                for j in range(4):
                    mc = hb * 4 + j
                    for kc in range(NDC):
                        P.op("tensor", "matmul", PS[j][:, :],
                             w1sb[:, kc * 2048 + mc * 128:
                                  kc * 2048 + mc * 128 + 128],
                             XH[:, kc * T + t0:kc * T + t0 + TC],
                             start=(kc == 0), stop=(kc == NDC - 1))
                        P.op("tensor", "matmul", PS[4 + j][:, :],
                             w2sb[:, kc * 2048 + mc * 128:
                                  kc * 2048 + mc * 128 + 128],
                             XH[:, kc * T + t0:kc * T + t0 + TC],
                             start=(kc == 0), stop=(kc == NDC - 1))
                for j in range(4):
                    mc = hb * 4 + j
                    P.op("vector", "tensor_copy", asb[:, :], PS[j][:, :])
                    P.op("vector", "tensor_mul", usb[:, :], PS[4 + j][:, :],
                         asb[:, :])
                    P.op("scalar", "activation",
                         H2[:, mc * TC:(mc + 1) * TC], usb[:, :],
                         func=AF.Gelu)
            P.dma("sync", grow[:, :], we_hbm.ap()[e:e + 1, t0:t0 + TC])
            P.op("tensor", "matmul", PS[0][:, :], ones1f[:, :], grow[:, :],
                 start=True, stop=True, waits=[t_o1])
            P.op("vector", "tensor_copy", gr[:, :], PS[0][:, :])
            for half in range(2):
                for j in range(4):
                    dc = half * 4 + j
                    for k3 in range(16):
                        P.op("tensor", "matmul", PS[j][:, :],
                             w3sb[:, k3 * 1024 + dc * 128:
                                  k3 * 1024 + dc * 128 + 128],
                             H2[:, k3 * TC:(k3 + 1) * TC],
                             start=(k3 == 0), stop=(k3 == 15))
                for j in range(4):
                    dc = half * 4 + j
                    P.op("vector", "tensor_mul", ctr[:, :], PS[j][:, :],
                         gr[:, :])
                    P.dma("gpsimd", acc_hbm.ap()[dc * 128:(dc + 1) * 128,
                                                 t0:t0 + TC],
                          ctr[:, :],
                          accum_op=(ALU.add if e > 0 else ALU.bypass))
    wxr = P.sbuf(st, "l0f_wxr", [128, TC], F32)
    Xre, Acc, Hld, Xo = asb, usb, ctr, gr
    for tc in range(NTC):
        t0 = tc * TC
        P.dma("sync", grow[:, :], wx_hbm.ap()[0:1, t0:t0 + TC])
        P.op("tensor", "matmul", PS[1][:, :], ones1f[:, :], grow[:, :],
             start=True, stop=True)
        P.op("vector", "tensor_copy", wxr[:, :], PS[1][:, :])
        for dc in range(NDC):
            P.dma("sync", Xre[:, :], x_in[dc * 128:(dc + 1) * 128,
                                          t0:t0 + TC])
            P.dma("sync", Acc[:, :],
                  acc_hbm.ap()[dc * 128:(dc + 1) * 128, t0:t0 + TC])
            P.dma("sync", Hld[:, :],
                  h_ap[dc * 128:(dc + 1) * 128, t0:t0 + TC])
            P.op("vector", "tensor_mul", Xo[:, :], wxr[:, :],
                 XH[:, dc * T + t0: dc * T + t0 + TC])
            P.op("vector", "tensor_add", Xo[:, :], Xo[:, :], Acc[:, :])
            P.op("vector", "tensor_add", Xo[:, :], Xo[:, :], Hld[:, :])
            P.op("vector", "tensor_add", Xo[:, :], Xo[:, :], Xre[:, :])
            P.dma("gpsimd", x_out[dc * 128:(dc + 1) * 128, t0:t0 + TC],
                  Xo[:, :])
    P.barrier()
    st.close()


def build_l0_ffn_launch():
    P = Prog("l0ffn")
    x_in = P.dram("x_in", [D, T], F32, kind="ExternalInput").ap()
    x_out = P.dram("x_out", [D, T], F32, kind="ExternalOutput").ap()
    counts_out = P.dram("counts_out", [8, 2], F32, kind="ExternalOutput").ap()
    w = {
        "g1": P.dram("wg1", [1, D], F32, kind="ExternalInput").ap(),
        "g2": P.dram("wg2", [1, D], F32, kind="ExternalInput").ap(),
        "wq": P.dram("wwq", [D, 1024], F32, kind="ExternalInput").ap(),
        "wk": P.dram("wwk", [D, 256], F32, kind="ExternalInput").ap(),
        "wv": P.dram("wwv", [D, 256], F32, kind="ExternalInput").ap(),
        "wo": P.dram("wwo", [1024, D], F32, kind="ExternalInput").ap(),
        "router_w": P.dram("wrw", [D, 8], F32, kind="ExternalInput").ap(),
        "router_b": P.dram("wrb", [1, 8], F32, kind="ExternalInput").ap(),
        "w1": P.dram("ww1", [8 * D, 2048], F32, kind="ExternalInput").ap(),
        "w2": P.dram("ww2", [8 * D, 2048], F32, kind="ExternalInput").ap(),
        "w3": P.dram("ww3", [8 * 2048, D], F32, kind="ExternalInput").ap(),
    }
    counts_all = P.dram("counts_all", [8, 16], F32,
                        kind="ExternalInput").ap()
    prev_mask = P.dram("prev_mask", [8, 1], F32, kind="ExternalInput").ap()
    bs = ExitStack()
    emit_l0_ffn(P, bs, x_in, x_out, w, counts_all, prev_mask,
                counts_out=counts_out)
    bs.close()
    nc = P.assemble()
    return P, nc



def emit_l2_attn(P, bs, x_in, x_out, w):
    ones_bf = P.sbuf_const("ones128_bf",
                           np.ones((128, 128), dtype=np.float32)
                           .astype(_bf16np()))
    ones1f, t_o1 = P.sbuf_const("ones1x128_f32",
                                np.ones((1, 128), dtype=np.float32))
    masks = make_causal_masks(P)

    dim = 128
    inv_f = np.exp(-np.log(10000.0) * (np.arange(0, dim, 2) / dim))
    angT = np.outer(np.arange(T, dtype=np.float64), inv_f)
    pe = np.zeros((T, dim))
    pe[:, 0::2] = np.sin(angT)
    pe[:, 1::2] = np.cos(angT)
    peT = P.const_input("l2a_peT", pe.T.astype(np.float32)
                        .astype(_bf16np()))
    ang0 = np.zeros((1, 64))
    pe0 = np.zeros((dim, 1))
    pe0[0::2, 0] = np.sin(ang0)[0]
    pe0[1::2, 0] = np.cos(ang0)[0]
    pe0c = P.const_input("l2a_pe0", pe0.astype(np.float32))

    st = ExitStack()
    XB, xb_toks = emit_load_xb(P, st, x_in, name="l2a_xb")
    lnstack = ExitStack()
    emit_layernorm_inplace(P, lnstack, XB, xb_toks, ones_bf,
                           eps=1e-5, tag="l2a_ln")
    P.barrier()
    lnstack.close()

    woT_hbm = P.dram("l2a_woTh", [D, 1024], BF16)
    wqT_hbm = P.dram("l2a_wqTh", [D, 1024], BF16)
    prepst = ExitStack()
    wk_T = P.sbuf(st, "l2a_wkT", [128, NDC * 128], BF16)
    wv_T = P.sbuf(st, "l2a_wvT", [128, NDC * 128], BF16)
    wrow = P.sbuf(prepst, "l2a_wrow", [128, 1024], BF16)
    wcol = P.sbuf(prepst, "l2a_wcol", [128, 128], BF16)
    tr_ps = P.psum(prepst, "l2a_trps", [128, 128], BF16)
    for (w_ap, dsth, n_out) in ((w["Wq"], wqT_hbm, 1024),
                                (w["Wo"], woT_hbm, 1024)):
        for ro in range(n_out // 128):
            P.dma("gpsimd", wrow[:, :],
                  w_ap[ro * 128:(ro + 1) * 128, :])
            for ci in range(NDC):
                emit_transpose(P, tr_ps[:, :],
                               wrow[:, ci * 128:(ci + 1) * 128], 128)
                P.op("vector", "tensor_copy", wcol[:, :], tr_ps[:, :])
                P.dma("gpsimd",
                      dsth.ap()[ci * 128:(ci + 1) * 128,
                                ro * 128:(ro + 1) * 128], wcol[:, :])
    for (w_ap, dst, n_out) in ((w["Wk"], wk_T, 128), (w["Wv"], wv_T, 128)):
        for ro in range(n_out // 128):
            P.dma("gpsimd", wrow[:, :], w_ap[ro * 128:(ro + 1) * 128, :])
            for ci in range(NDC):
                emit_transpose(P, tr_ps[:, :],
                               wrow[:, ci * 128:(ci + 1) * 128], 128)
                P.op("vector", "tensor_copy",
                     dst[:, ci * n_out + ro * 128:
                         ci * n_out + (ro + 1) * 128], tr_ps[:, :])
    P.barrier()
    prepst.close()

    qr_hbm = P.dram("l2a_qrh", [1024, T], BF16)
    kr_hbm = P.dram("l2a_krh", [128, T], BF16)
    vt_hbm = P.dram("l2a_vth", [T, 128], BF16)
    att_hbm = P.dram("l2a_atth", [1024, T], BF16)
    wq_T = P.sbuf(st, "l2a_wqT", [128, NDC * 128], BF16)
    pesb = P.sbuf(st, "l2a_pesb", [128, T], BF16)
    P.dma("gpsimd", pesb[:, :], peT[:, :])
    pe0sb = P.sbuf(st, "l2a_pe0sb", [128, 1], F32)
    P.dma("sync", pe0sb[:, :], pe0c[:, :])
    q_ps = P.psum(st, "l2a_qps", [128, TC])
    qsc = P.sbuf(st, "l2a_qsc", [128, TC], F32)
    qbf = P.sbuf(st, "l2a_qbf", [128, TC], BF16)
    sqb = P.sbuf(st, "l2a_sqb", [128, TC], BF16)
    nq_ps = P.psum(st, "l2a_nqps", [1, TC])
    qn2 = P.sbuf(st, "l2a_qn2", [1, 16], F32)
    nred = P.sbuf(st, "l2a_nred", [1, 1], F32)

    def qk_norm_acc(col, tc):
        P.op("scalar", "activation", sqb[:, :], qbf[:, :], func=AF.Square)
        P.op("tensor", "matmul", nq_ps[:, :], ones_bf[0][:, 0:1],
             sqb[:, :], start=True, stop=True, waits=[ones_bf[1]])
        P.op("vector", "tensor_reduce", nred[:, :], nq_ps[:, :],
             axis=AX.X, op=ALU.max)
        if tc == 0:
            P.op("vector", "tensor_copy", qn2[:, col:col + 1], nred[:, :])
        else:
            P.op("vector", "tensor_max", qn2[:, col:col + 1],
                 qn2[:, col:col + 1], nred[:, :])
    for rt in range(8):
        for kc in range(NDC):
            P.dma("sync", wq_T[:, kc * 128:(kc + 1) * 128],
                  wqT_hbm.ap()[kc * 128:(kc + 1) * 128,
                               rt * 128:(rt + 1) * 128])
        for tc in range(NTC):
            t0 = tc * TC
            for kc in range(NDC):
                P.op("tensor", "matmul", q_ps[:, :],
                     wq_T[:, kc * 128:(kc + 1) * 128],
                     XB[:, kc * T + t0:kc * T + t0 + TC],
                     start=(kc == 0), stop=(kc == NDC - 1))
            P.op("vector", "tensor_add", qbf[:, :], q_ps[:, :],
                 pesb[:, t0:t0 + TC])
            P.dma("gpsimd", qr_hbm.ap()[rt * 128:(rt + 1) * 128,
                                        t0:t0 + TC], qbf[:, :])
            qk_norm_acc(rt, tc)
    for tc in range(NTC):
        t0 = tc * TC
        for kc in range(NDC):
            P.op("tensor", "matmul", q_ps[:, :],
                 wk_T[:, kc * 128:(kc + 1) * 128],
                 XB[:, kc * T + t0:kc * T + t0 + TC],
                 start=(kc == 0), stop=(kc == NDC - 1))
        P.op("vector", "tensor_scalar", qbf[:, :], q_ps[:, :],
             pe0sb[:, :], None, op0=ALU.add)
        P.dma("gpsimd", kr_hbm.ap()[:, t0:t0 + TC], qbf[:, :])
        qk_norm_acc(8, tc)
    for tt in range(T // 128):
        for kc in range(NDC):
            P.op("tensor", "matmul", q_ps[:, 0:128],
                 XB[:, kc * T + tt * 128: kc * T + (tt + 1) * 128],
                 wv_T[:, kc * 128:(kc + 1) * 128],
                 start=(kc == 0), stop=(kc == NDC - 1))
        P.op("vector", "tensor_copy", qbf[:, 0:128], q_ps[:, 0:128])
        P.dma("gpsimd", vt_hbm.ap()[tt * 128:(tt + 1) * 128, :],
              qbf[:, 0:128])
    negm = P.sbuf(st, "l2a_negm", [1, 8], F32)
    P.op("vector", "tensor_scalar", negm[:, :], qn2[:, 0:8],
         qn2[:, 8:9], None, op0=ALU.mult)
    P.op("scalar", "activation", negm[:, :], negm[:, :], func=AF.Sqrt)
    P.op("vector", "tensor_scalar_mul", negm[:, :], negm[:, :],
         -1.02 / math.sqrt(128.0))
    negm_hbm = P.dram("l2a_negmh", [1, 8], F32)
    P.dma("sync", negm_hbm.ap()[:, :], negm[:, :])
    P.barrier()
    st.close()

    st2 = ExitStack()
    KRk = P.sbuf(st2, "l2a_kr", [128, T], BF16)
    VT = P.sbuf(st2, "l2a_vt", [128, 32 * 128], BF16)
    P.dma("sync", KRk[:, :], kr_hbm.ap()[:, :])
    for tt in range(32):
        P.dma("sync", VT[:, tt * 128:(tt + 1) * 128],
              vt_hbm.ap()[tt * 128:(tt + 1) * 128, :])
    negm2 = P.sbuf(st2, "l2a_negm2", [1, 8], F32)
    P.dma("sync", negm2[:, :], negm_hbm.ap()[:, :])
    mh_ps = P.psum(st2, "l2a_mhps", [128, 8])
    P.op("tensor", "matmul", mh_ps[:, :], ones1f[0:1, :], negm2[:, :],
         start=True, stop=True, waits=[t_o1])
    mhat_rep = P.sbuf(st2, "l2a_mhat", [128, 8], F32)
    P.op("vector", "tensor_copy", mhat_rep[:, :], mh_ps[:, :])
    QRh = P.sbuf(st2, "l2a_qrh_sb", [128, T], BF16)
    att_sc = P.sbuf(st2, "l2a_attsc", [128, TC], F32)
    attb = P.sbuf(st2, "l2a_attb1", [128, TC], BF16)
    ascr = make_attn_scratch(P, st2, "l2a_at", 128)

    def out_cb(h, tq, opt, rsb):
        t0 = tq * TC
        P.op("vector", "tensor_mul", att_sc[:, :], opt[:, 0:TC],
             rsb[:, 0:TC])
        P.op("vector", "tensor_copy", attb[:, :], att_sc[:, :])
        P.dma("gpsimd", att_hbm.ap()[h * 128:(h + 1) * 128, t0:t0 + TC],
              attb[:, :])
    for h in range(8):
        qtok = P.dma("sync", QRh[:, :],
                     qr_hbm.ap()[h * 128:(h + 1) * 128, :])
        emit_causal_attn_rt(P, ascr, "l2a_h%d" % h, h, QRh, [KRk, None],
                            VT, 8, 128, 1, 1.0 / math.sqrt(128.0),
                            mhat_rep, out_cb, masks, ones_bf, qtok)
    P.barrier()
    st2.close()

    st3 = ExitStack()
    ho_ps = P.psum(st3, "l2a_hops", [128, TC])
    xre = P.sbuf(st3, "l2a_xre", [128, TC], F32)
    xo = P.sbuf(st3, "l2a_xo", [128, TC], F32)
    wosl = [P.sbuf(st3, "l2a_wosl%d" % i, [128, 1024], BF16)
            for i in range(2)]
    atsl = [P.sbuf(st3, "l2a_atsl%d" % i, [128, TC], BF16)
            for i in range(8)]
    for tc in range(NTC):
        t0 = tc * TC
        for kc in range(NDC):
            P.dma("sync", atsl[kc][:, :],
                  att_hbm.ap()[kc * 128:(kc + 1) * 128, t0:t0 + TC])
        for dc in range(NDC):
            wp = wosl[dc % 2]
            P.dma("sync",
                  wp[:, :].rearrange("p (kc m) -> p kc m", m=128),
                  woT_hbm.ap()[:, dc * 128:(dc + 1) * 128]
                  .rearrange("(kc p) m -> p kc m", p=128))
            for kc in range(NDC):
                P.op("tensor", "matmul", ho_ps[:, :],
                     wp[:, kc * 128:(kc + 1) * 128],
                     atsl[kc][:, :],
                     start=(kc == 0), stop=(kc == NDC - 1))
            P.dma("sync", xre[:, :], x_in[dc * 128:(dc + 1) * 128,
                                          t0:t0 + TC])
            P.op("vector", "tensor_add", xo[:, :], ho_ps[:, :], xre[:, :])
            P.dma("gpsimd", x_out[dc * 128:(dc + 1) * 128, t0:t0 + TC],
                  xo[:, :])
    P.barrier()
    st3.close()


def build_l2_attn_launch():
    P = Prog("l2attn")
    x_in = P.dram("x_in", [D, T], F32, kind="ExternalInput").ap()
    x_out = P.dram("x_out", [D, T], F32, kind="ExternalOutput").ap()
    w = {
        "Wq": P.dram("wWq", [1024, D], F32, kind="ExternalInput").ap(),
        "Wk": P.dram("wWk", [128, D], F32, kind="ExternalInput").ap(),
        "Wv": P.dram("wWv", [128, D], F32, kind="ExternalInput").ap(),
        "Wo": P.dram("wWo", [D, 1024], F32, kind="ExternalInput").ap(),
    }
    bs = ExitStack()
    emit_l2_attn(P, bs, x_in, x_out, w)
    bs.close()
    nc = P.assemble()
    return P, nc



def emit_l3_attn(P, bs, x_in, x_out, w):
    ones_bf = P.sbuf_const("ones128_bf",
                           np.ones((128, 128), dtype=np.float32)
                           .astype(_bf16np()))
    ones1f, t_o1 = P.sbuf_const("ones1x128_f32",
                                np.ones((1, 128), dtype=np.float32))
    CH = 128
    NCH = T // CH
    qf_hbm = P.dram("l3a_qf", [D, T], BF16)
    kf_hbm = P.dram("l3a_kf", [D, T], BF16)
    vf_hbm = P.dram("l3a_vf", [D, T], BF16)
    of_hbm = P.dram("l3a_of", [D, T], BF16)

    st = ExitStack()
    XB, xb_toks = emit_load_xb(P, st, x_in, name="l3a_xb")
    lnstack = ExitStack()
    emit_layernorm_inplace(P, lnstack, XB, xb_toks, ones_bf,
                           eps=1e-5, tag="l3a_ln")
    P.barrier()
    lnstack.close()
    bpp = emit_row_to_pp(P, st, "l3a_bq", w["b_qkv"], 3072)
    wpan = [P.sbuf(st, "l3a_wpan%d" % i, [128, 1024], BF16)
            for i in range(2)]
    q_ps = P.psum(st, "l3a_qps", [128, TC])
    qbf = P.sbuf(st, "l3a_qbf", [128, TC], BF16)
    for rt in range(24):
        dst = (qf_hbm, kf_hbm, vf_hbm)[rt // 8]
        ro = rt % 8
        wp = wpan[rt % 2]
        load_panel(P, wp, w["w_qkv"], rt * 128, 128, NDC)
        for tc in range(NTC):
            t0 = tc * TC
            for kc in range(NDC):
                P.op("tensor", "matmul", q_ps[:, :],
                     wp[:, kc * 128:(kc + 1) * 128],
                     XB[:, kc * T + t0:kc * T + t0 + TC],
                     start=(kc == 0), stop=(kc == NDC - 1))
            P.op("vector", "tensor_scalar", qbf[:, :], q_ps[:, :],
                 bpp[:, rt:rt + 1], None, op0=ALU.add)
            P.dma("gpsimd", dst.ap()[ro * 128:(ro + 1) * 128, t0:t0 + TC],
                  qbf[:, :])
    P.barrier()
    st.close()

    st2 = ExitStack()
    onesf2 = P.sbuf_const("ones128_f32", np.ones((128, 128),
                                                  dtype=np.float32))
    ones64T = P.sbuf_const("ones64T_bf",
                           np.ones((64, T), dtype=np.float32)
                           .astype(_bf16np()))
    trilm = P.sbuf_const("tril01",
                         (np.arange(128)[:, None] <=
                          np.arange(128)[None, :]).astype(np.float32))
    projb = P.sbuf(st2, "l3a_projb", [128, 64], BF16)
    qh = P.sbuf(st2, "l3a_qh", [64, T], BF16)
    kh = P.sbuf(st2, "l3a_kh", [64, T], BF16)
    vh = P.sbuf(st2, "l3a_vh", [64, T], BF16)
    pr_ps = P.psum(st2, "l3a_prps", [128, TC])
    qp = P.sbuf(st2, "l3a_qp", [64, T], BF16)
    kpf = P.sbuf(st2, "l3a_kpf", [64, T], BF16)
    kpt = P.sbuf(st2, "l3a_kpt", [128, NCH * 64], BF16)
    vtk = P.sbuf(st2, "l3a_vtk", [128, NCH * 64], BF16)
    tp_ps = P.psum(st2, "l3a_tpps", [128, 64], BF16)
    kcs = P.sbuf(st2, "l3a_kcs", [64, T], F32)
    qkc = P.sbuf(st2, "l3a_qkc", [64, T], F32)
    nrm_ps = P.psum(st2, "l3a_nrmps", [1, TC])
    nrm = P.sbuf(st2, "l3a_nrm", [1, T], F32)
    nrp_ps = P.psum(st2, "l3a_nrpps", [64, TC])
    nrp = P.sbuf(st2, "l3a_nrp", [64, T], F32)
    s_ps = P.psum(st2, "l3a_sps", [64, 64])
    S_sb = P.sbuf(st2, "l3a_S", [64, 64], F32)
    S_bf = P.sbuf(st2, "l3a_Sbf", [64, 64], BF16)
    at_ps = P.psum(st2, "l3a_atps", [128, CH])
    asb = P.sbuf(st2, "l3a_asb", [128, CH], F32)
    abf = [P.sbuf(st2, "l3a_abf%d" % i, [128, CH], BF16) for i in range(2)]
    ctx_ps = P.psum(st2, "l3a_ctxps", [64, CH])
    of_sc = P.sbuf(st2, "l3a_ofsc", [64, CH], F32)
    of_h = P.sbuf(st2, "l3a_ofh", [64, T], BF16)
    for h in range(16):
        hb = (h % 2) * 64
        P.dma("gpsimd", projb[0:64, :], w["proj"][h * 64:(h + 1) * 64, :])
        P.dma("gpsimd", projb[64:128, :], w["proj"][h * 64:(h + 1) * 64, :])
        pj = projb[0:64, :] if hb == 0 else projb[64:128, :]
        pj0 = projb[0:64, :]
        P.dma("sync", qh[:, :], qf_hbm.ap()[h * 64:(h + 1) * 64, :])
        P.dma("sync", kh[:, :], kf_hbm.ap()[h * 64:(h + 1) * 64, :])
        P.dma("sync", vh[:, :], vf_hbm.ap()[h * 64:(h + 1) * 64, :])
        for tc in range(NTC):
            t0 = tc * TC
            P.op("tensor", "matmul", pr_ps[0:64, :], pj0, qh[:, t0:t0 + TC],
                 start=True, stop=True)
            P.op("scalar", "activation", qp[:, t0:t0 + TC], pr_ps[0:64, :],
                 func=AF.Relu, scale=0.125)
            P.op("tensor", "matmul", pr_ps[0:64, :], pj0, kh[:, t0:t0 + TC],
                 start=True, stop=True)
            P.op("scalar", "activation", kpf[:, t0:t0 + TC], pr_ps[0:64, :],
                 func=AF.Relu, scale=0.125)
        for c in range(NCH):
            emit_transpose(P, tp_ps[:, :], kpf[:, c * CH:(c + 1) * CH], 64)
            P.op("vector", "tensor_copy", kpt[:, c * 64:(c + 1) * 64],
                 tp_ps[:, :])
            emit_transpose(P, tp_ps[:, :], vh[:, c * CH:(c + 1) * CH], 64)
            P.op("vector", "tensor_copy", vtk[:, c * 64:(c + 1) * 64],
                 tp_ps[:, :])
        P.op("vector", "tensor_tensor_scan", kcs[:, :], ones64T[0][:, :],
             kpf[:, :], 0.0, op0=ALU.mult, op1=ALU.add, waits=[ones64T[1]])
        P.op("vector", "tensor_mul", qkc[:, :], qp[:, :], kcs[:, :])
        for tc in range(NTC):
            t0 = tc * TC
            P.op("tensor", "matmul", nrm_ps[:, :], onesf2[0][0:64, 0:1],
                 qkc[:, t0:t0 + TC], start=True, stop=True,
                 waits=[onesf2[1]])
            P.op("vector", "tensor_scalar", nrm[:, t0:t0 + TC],
                 nrm_ps[:, :], 1e-6, None, op0=ALU.add)
            P.op("vector", "reciprocal", nrm[:, t0:t0 + TC],
                 nrm[:, t0:t0 + TC])
            P.op("tensor", "matmul", nrp_ps[:, :], ones1f[0:1, 0:64],
                 nrm[:, t0:t0 + TC], start=True, stop=True)
            P.op("vector", "tensor_copy", nrp[:, t0:t0 + TC], nrp_ps[:, :])
        P.op("vector", "memset", S_sb[:, :], 0.0)
        for c in range(NCH):
            c0 = c * CH
            P.op("vector", "tensor_copy", S_bf[:, :], S_sb[:, :])
            P.op("tensor", "matmul", at_ps[:, :], kpf[:, c0:c0 + CH],
                 qp[:, c0:c0 + CH], start=True, stop=True)
            P.op("vector", "tensor_mul", asb[:, :], at_ps[:, :],
                 trilm[0][:, :], waits=[trilm[1]])
            P.op("vector", "tensor_copy", abf[c % 2][:, :], asb[:, :])
            P.op("tensor", "matmul", ctx_ps[:, :],
                 vtk[:, c * 64:(c + 1) * 64], abf[c % 2][:, :],
                 start=True, stop=False)
            P.op("tensor", "matmul", ctx_ps[:, :], S_bf[:, :],
                 qp[:, c0:c0 + CH], start=False, stop=True)
            P.op("tensor", "matmul", s_ps[:, :], kpt[:, c * 64:(c + 1) * 64],
                 vtk[:, c * 64:(c + 1) * 64], start=True, stop=True)
            P.op("vector", "tensor_add", S_sb[:, :], S_sb[:, :],
                 s_ps[:, :])
            P.op("vector", "tensor_mul", of_sc[:, :], ctx_ps[:, :],
                 nrp[:, c0:c0 + CH])
            P.op("vector", "tensor_copy", of_h[:, c0:c0 + CH], of_sc[:, :])
        P.dma("gpsimd", of_hbm.ap()[h * 64:(h + 1) * 64, :], of_h[:, :])
    P.barrier()
    st2.close()

    st3 = ExitStack()
    bo = emit_row_to_pp(P, st3, "l3a_bo", w["b_out"], D)
    wpan3 = [P.sbuf(st3, "l3a_w3p%d" % i, [128, 1024], BF16)
             for i in range(2)]
    ofsl = [P.sbuf(st3, "l3a_ofsl%d" % i, [128, TC], BF16) for i in range(8)]
    ho_ps = P.psum(st3, "l3a_hops", [128, TC])
    xre = P.sbuf(st3, "l3a_xre", [128, TC], F32)
    xo = P.sbuf(st3, "l3a_xo", [128, TC], F32)
    for tc in range(NTC):
        t0 = tc * TC
        for kc in range(NDC):
            P.dma("sync", ofsl[kc][:, :],
                  of_hbm.ap()[kc * 128:(kc + 1) * 128, t0:t0 + TC])
        for dc in range(NDC):
            wp = wpan3[dc % 2]
            load_panel(P, wp, w["w_out"], dc * 128, 128, NDC)
            for kc in range(NDC):
                P.op("tensor", "matmul", ho_ps[:, :],
                     wp[:, kc * 128:(kc + 1) * 128],
                     ofsl[kc][:, :], start=(kc == 0), stop=(kc == NDC - 1))
            P.dma("sync", xre[:, :], x_in[dc * 128:(dc + 1) * 128,
                                          t0:t0 + TC])
            P.op("vector", "tensor_scalar", xo[:, :], ho_ps[:, :],
                 bo[:, dc:dc + 1], None, op0=ALU.add)
            P.op("vector", "tensor_add", xo[:, :], xo[:, :], xre[:, :])
            P.dma("gpsimd", x_out[dc * 128:(dc + 1) * 128, t0:t0 + TC],
                  xo[:, :])
    P.barrier()
    st3.close()


def build_l3_attn_launch():
    P = Prog("l3attn")
    x_in = P.dram("x_in", [D, T], F32, kind="ExternalInput").ap()
    x_out = P.dram("x_out", [D, T], F32, kind="ExternalOutput").ap()
    w = {
        "w_qkv": P.dram("wqkv", [D, 3072], F32, kind="ExternalInput").ap(),
        "b_qkv": P.dram("wbqkv", [1, 3072], F32, kind="ExternalInput").ap(),
        "w_out": P.dram("wout", [D, D], F32, kind="ExternalInput").ap(),
        "b_out": P.dram("wbout", [1, D], F32, kind="ExternalInput").ap(),
        "proj": P.dram("wproj", [16 * 64, 64], F32, kind="ExternalInput").ap(),
    }
    bs = ExitStack()
    emit_l3_attn(P, bs, x_in, x_out, w)
    bs.close()
    nc = P.assemble()
    return P, nc



def emit_l1_attn(P, bs, x_in, x_out, w):
    ones_bf = P.sbuf_const("ones128_bf",
                           np.ones((128, 128), dtype=np.float32)
                           .astype(_bf16np()))
    onesf2 = P.sbuf_const("ones128_f32", np.ones((128, 128),
                                                 dtype=np.float32))
    ones1f, t_o1 = P.sbuf_const("ones1x128_f32",
                                np.ones((1, 128), dtype=np.float32))
    idc = P.sbuf_const("ident128_f32", np.eye(128, dtype=np.float32))

    st = ExitStack()
    wpan = [P.sbuf(bs, "l1a_wpan%d" % i, [128, 1024], BF16)
            for i in range(2)]
    ql = P.sbuf(bs, "l1a_ql", [128, 8 * 64], BF16)
    kl = P.sbuf(bs, "l1a_kl", [128, 8 * 64], BF16)
    qn2 = P.sbuf(bs, "l1a_qn2", [2, 8], F32)
    kn2 = P.sbuf(bs, "l1a_kn2", [2, 8], F32)
    qrow = P.sbuf(bs, "l1a_qrow", [1, 16], F32)
    krow = P.sbuf(bs, "l1a_krow", [1, 16], F32)
    negm = P.sbuf(bs, "l1a_negm", [1, 16], F32)
    mhat = P.sbuf(bs, "l1a_mhat", [128, 16], F32)
    XB, xb_toks = emit_load_xb(P, st, x_in, name="l1a_xb")
    lnstack = ExitStack()
    emit_layernorm_inplace(P, lnstack, XB, xb_toks, ones_bf,
                           eps=1e-5, tag="l1a_ln")
    P.barrier()
    lnstack.close()

    qf_hbm = P.dram("l1a_qfh", [D, T], BF16)
    kf_hbm = P.dram("l1a_kfh", [D, T], BF16)
    vf_hbm = P.dram("l1a_vfh", [D, T], BF16)
    of_hbm = P.dram("l1a_of", [D, T], BF16)
    s3_hbm = P.dram("l1a_s3h", [64, T], F32)
    q_ps = P.psum(st, "l1a_qps", [128, TC])
    qtile = P.sbuf(st, "l1a_qtile", [128, TC], BF16)
    lred = P.sbuf(st, "l1a_lred", [128, 8], F32)
    lacc = P.sbuf(st, "l1a_lacc", [128, 64], F32)
    sqb = P.sbuf(st, "l1a_sqb", [128, TC], BF16)
    nq_ps = P.psum(st, "l1a_nqps", [2, TC])
    nred = P.sbuf(st, "l1a_nred", [2, 1], F32)
    sel2 = P.sbuf_const("sel2h", np.kron(np.eye(2), np.ones((64, 1)))
                        .astype(np.float32).astype(_bf16np()))
    for (w_ap, dsth, scl, ln_dst, do_n) in (
            (w["Wq"], qf_hbm, 0.125, ql, True),
            (w["Wk"], kf_hbm, 1.0, kl, False),
            (w["Wv"], vf_hbm, 1.0, None, False)):
        for rt in range(8):
            wp = wpan[rt % 2]
            load_panel(P, wp, w_ap, rt * 128, 128, NDC)
            for tc in range(NTC):
                t0 = tc * TC
                for kc in range(NDC):
                    P.op("tensor", "matmul", q_ps[:, :],
                         wp[:, kc * 128:(kc + 1) * 128],
                         XB[:, kc * T + t0:kc * T + t0 + TC],
                         start=(kc == 0), stop=(kc == NDC - 1))
                P.op("scalar", "activation", qtile[:, :], q_ps[:, :],
                     func=AF.Identity, scale=float(scl))
                P.dma("gpsimd", dsth.ap()[rt * 128:(rt + 1) * 128,
                                          t0:t0 + TC], qtile[:, :])
                if ln_dst is not None:
                    P.op("vector", "tensor_reduce", lred[:, :],
                         qtile[:, :].rearrange("p (l j) -> p l j", j=64),
                         axis=AX.X, op=ALU.add)
                    P.op("vector", "tensor_scalar_mul",
                         lacc[:, tc * 8:(tc + 1) * 8], lred[:, :],
                         1.0 / 64)
                if do_n:
                    P.op("scalar", "activation", sqb[:, :], qtile[:, :],
                         func=AF.Square)
                    P.op("tensor", "matmul", nq_ps[:, :], sel2[0][:, :],
                         sqb[:, :], start=True, stop=True,
                         waits=[sel2[1]])
                    P.op("vector", "tensor_reduce", nred[:, :],
                         nq_ps[:, :], axis=AX.X, op=ALU.max)
                    if tc == 0:
                        P.op("vector", "tensor_copy", qn2[:, rt:rt + 1],
                             nred[:, :])
                    else:
                        P.op("vector", "tensor_max", qn2[:, rt:rt + 1],
                             qn2[:, rt:rt + 1], nred[:, :])
            if ln_dst is not None:
                P.op("vector", "tensor_copy",
                     ln_dst[:, rt * 64:(rt + 1) * 64], lacc[:, :])
    for rt in range(8):
        P.op("scalar", "activation", sqb[:, 0:64],
             kl[:, rt * 64:(rt + 1) * 64], func=AF.Square)
        P.op("tensor", "matmul", nq_ps[:, 0:64], sel2[0][:, :],
             sqb[:, 0:64], start=True, stop=True)
        P.op("vector", "tensor_reduce", kn2[:, rt:rt + 1], nq_ps[:, 0:64],
             axis=AX.X, op=ALU.max)
    nq_hbm = P.dram("l1a_nqh", [2, 8], F32)
    nk_hbm = P.dram("l1a_nkh", [2, 8], F32)
    P.dma("sync", nq_hbm.ap()[:, :], qn2[:, :])
    P.dma("sync", nk_hbm.ap()[:, :], kn2[:, :])
    P.dma("sync", qrow[0:1, :].rearrange("a (q r) -> a q r", r=2),
          nq_hbm.ap().rearrange("r q -> q r")[None, :, :],
          allow_slow_non_contiguous=True)
    P.dma("sync", krow[0:1, :].rearrange("a (q r) -> a q r", r=2),
          nk_hbm.ap().rearrange("r q -> q r")[None, :, :],
          allow_slow_non_contiguous=True)
    P.op("vector", "tensor_mul", negm[:, :], qrow[:, :], krow[:, :])
    P.op("scalar", "activation", negm[:, :], negm[:, :], func=AF.Sqrt)
    P.op("vector", "tensor_scalar_mul", negm[:, :], negm[:, :], -1.02)
    mh_ps = P.psum(st, "l1a_mhps", [128, 16])
    P.op("tensor", "matmul", mh_ps[:, :], ones1f[0:1, :], negm[:, :],
         start=True, stop=True, waits=[t_o1])
    P.op("vector", "tensor_copy", mhat[:, :], mh_ps[:, :])
    P.barrier()
    st.close()

    st = ExitStack()
    qfh = P.sbuf(st, "l1a_qfh_sb", [64, T], BF16)
    kfh = P.sbuf(st, "l1a_kfh_sb", [64, T], BF16)
    vfh = P.sbuf(st, "l1a_vfh_sb", [64, T], BF16)
    s2p = P.psum(st, "l1a_s2p", [64, 64])
    s2f = P.sbuf(st, "l1a_s2f", [64, 64], F32)
    s2T = P.sbuf(st, "l1a_s2T", [64, 64], BF16)
    zf = P.sbuf(st, "l1a_zf", [64, 64], F32)
    zb = P.sbuf(st, "l1a_zb", [64, 64], BF16)
    zT = P.sbuf(st, "l1a_zT", [64, 64], BF16)
    xzT = P.sbuf(st, "l1a_xzT", [64, 64], BF16)
    wb = P.sbuf(st, "l1a_wb", [64, 64], BF16)
    scw = P.sbuf(st, "l1a_scw", [64, 64], F32)
    red1 = P.sbuf(st, "l1a_red1", [64, 1], F32)
    red2 = P.sbuf(st, "l1a_red2", [1, 64], F32)
    sc11 = P.sbuf(st, "l1a_sc11", [1, 2], F32)
    rep64 = P.sbuf(st, "l1a_rep64", [64, 1], F32)
    tp_ps = P.psum(st, "l1a_tpps", [128, 128])
    e3 = P.sbuf(st, "l1a_e3", [64, T], BF16)
    s3f = P.sbuf(st, "l1a_s3f", [64, TC], F32)
    den3 = P.sbuf(st, "l1a_den3", [64, 1], F32)
    et = P.sbuf(st, "l1a_et", [128, 32 * 64], BF16)
    vtk = P.sbuf(st, "l1a_vtk", [128, 32 * 64], BF16)
    svb = P.sbuf(st, "l1a_svb", [64, 64], BF16)
    zvb = P.sbuf(st, "l1a_zvb", [64, 64], BF16)
    sv_ps = s2p
    zv_ps = s2p
    tp_bs = P.psum(st, "l1a_tpbs", [128, 128], BF16)
    e1_ps = P.psum(st, "l1a_e1ps", [64, TC])
    e1 = P.sbuf(st, "l1a_e1", [64, TC], BF16)
    d1_ps = P.psum(st, "l1a_d1ps", [1, TC])
    d1 = P.sbuf(st, "l1a_d1", [1, TC], F32)
    r1_ps = e1_ps
    r1 = P.sbuf(st, "l1a_r1", [64, TC], F32)
    o_ps = P.psum(st, "l1a_ops", [64, TC])
    osb = P.sbuf(st, "l1a_osb", [64, TC], F32)
    ofh = P.sbuf(st, "l1a_ofh", [64, T], BF16)
    qlh = P.sbuf(st, "l1a_qlh", [64, 64], BF16)
    klh = P.sbuf(st, "l1a_klh", [64, 64], BF16)
    for h in range(16):
        rt, hb = divmod(h, 2)
        r0 = hb * 64
        P.op("vector", "tensor_copy", qlh[:, :],
             ql[r0:r0 + 64, rt * 64:(rt + 1) * 64])
        P.op("vector", "tensor_copy", klh[:, :],
             kl[r0:r0 + 64, rt * 64:(rt + 1) * 64])
        P.dma("sync", qfh[:, :], qf_hbm.ap()[h * 64:(h + 1) * 64, :])
        P.dma("sync", kfh[:, :], kf_hbm.ap()[h * 64:(h + 1) * 64, :])
        P.dma("sync", vfh[:, :], vf_hbm.ap()[h * 64:(h + 1) * 64, :])
        P.op("tensor", "matmul", s2p[:, :], qlh[:, :], klh[:, :],
             start=True, stop=True)
        P.op("vector", "tensor_reduce", red1[:, :], s2p[:, :], axis=AX.X,
             op=ALU.max, negate=True)
        P.op("scalar", "activation", s2f[:, :], s2p[:, :], func=AF.Exp,
             bias=red1[:, :])
        P.op("vector", "reduce_sum", red1[:, :], s2f[:, :], axis=AX.X)
        P.op("vector", "reciprocal", red1[:, :], red1[:, :])
        P.op("vector", "tensor_scalar", s2f[:, :], s2f[:, :], red1[:, :],
             None, op0=ALU.mult)
        P.op("scalar", "activation", scw[:, :], s2f[:, :], func=AF.Abs)
        P.op("vector", "reduce_sum", red1[:, :], scw[:, :], axis=AX.X)
        emit_transpose(P, tp_ps[0:1, 0:64], red1[:, 0:1], 64, bf16=False)
        P.op("vector", "tensor_reduce", sc11[:, 0:1], tp_ps[0:1, 0:64],
             axis=AX.X, op=ALU.max)
        P.op("tensor", "matmul", tp_ps[0:1, 0:64], onesf2[0][0:64, 0:1],
             scw[:, :], start=True, stop=True, waits=[onesf2[1]])
        P.op("vector", "tensor_reduce", sc11[:, 1:2], tp_ps[0:1, 0:64],
             axis=AX.X, op=ALU.max)
        P.op("vector", "tensor_mul", sc11[:, 0:1], sc11[:, 0:1],
             sc11[:, 1:2])
        P.op("vector", "reciprocal", sc11[:, 0:1], sc11[:, 0:1])
        P.op("tensor", "matmul", tp_ps[0:64, 0:1], ones1f[0:1, 0:64],
             sc11[:, 0:1], start=True, stop=True)
        P.op("vector", "tensor_copy", rep64[:, :], tp_ps[0:64, 0:1])
        P.op("vector", "tensor_copy", s2T[:, :], s2f[:, :])
        emit_transpose(P, tp_bs[0:64, 0:64], s2T[:, :], 64)
        P.op("vector", "tensor_scalar", zf[:, :], tp_bs[0:64, 0:64],
             rep64[:, :], None, op0=ALU.mult)
        P.op("vector", "tensor_copy", zb[:, :], zf[:, :])
        id7 = idc[0][0:64, 0:64]
        for it in range(6):
            P.op("tensor", "matmul", tp_ps[0:64, 0:64], s2T[:, :],
                 zb[:, :], start=True, stop=True)
            P.op("vector", "tensor_copy", scw[:, :], tp_ps[0:64, 0:64])
            P.op("vector", "tensor_copy", wb[:, :], scw[:, :])
            emit_transpose(P, tp_bs[64:128, 0:64], wb[:, :], 64)
            P.op("vector", "tensor_copy", xzT[:, :], tp_bs[64:128, 0:64])
            P.op("vector", "scalar_tensor_tensor", scw[:, :], idc[0][0:64, 0:64],
                 7.0, scw[:, :], op0=ALU.mult, op1=ALU.subtract,
                 waits=[idc[1]])
            P.op("vector", "tensor_copy", wb[:, :], scw[:, :])
            P.op("tensor", "matmul", tp_ps[0:64, 0:64], xzT[:, :],
                 wb[:, :], start=True, stop=True)
            P.op("vector", "scalar_tensor_tensor", scw[:, :],
                 idc[0][0:64, 0:64], 15.0, tp_ps[0:64, 0:64],
                 op0=ALU.mult, op1=ALU.subtract)
            P.op("vector", "tensor_copy", wb[:, :], scw[:, :])
            P.op("tensor", "matmul", tp_ps[0:64, 0:64], xzT[:, :],
                 wb[:, :], start=True, stop=True)
            P.op("vector", "scalar_tensor_tensor", scw[:, :],
                 idc[0][0:64, 0:64], 13.0, tp_ps[0:64, 0:64],
                 op0=ALU.mult, op1=ALU.subtract)
            P.op("vector", "tensor_copy", wb[:, :], scw[:, :])
            emit_transpose(P, tp_bs[64:128, 0:64], zb[:, :], 64)
            P.op("vector", "tensor_copy", zT[:, :], tp_bs[64:128, 0:64])
            P.op("tensor", "matmul", tp_ps[0:64, 0:64], zT[:, :], wb[:, :],
                 start=True, stop=True)
            P.op("vector", "tensor_scalar_mul", zf[:, :],
                 tp_ps[0:64, 0:64], 0.25)
            P.op("vector", "tensor_copy", zb[:, :], zf[:, :])
        for tc in range(NTC):
            t0 = tc * TC
            P.op("tensor", "matmul", e1_ps[:, :], qlh[:, :],
                 kfh[:, t0:t0 + TC], start=True, stop=True)
            P.op("vector", "tensor_copy", s3f[:, :], e1_ps[:, :])
            if tc == 0:
                P.op("vector", "tensor_reduce", den3[:, :], s3f[:, :],
                     axis=AX.X, op=ALU.max)
            else:
                P.op("vector", "tensor_reduce", red1[:, :], s3f[:, :],
                     axis=AX.X, op=ALU.max)
                P.op("vector", "tensor_max", den3[:, :], den3[:, :],
                     red1[:, :])
            P.dma("gpsimd", s3_hbm.ap()[0:64, t0:t0 + TC], s3f[:, :])
        P.op("vector", "tensor_scalar_mul", den3[:, :], den3[:, :], -1.0)
        for tc in range(NTC):
            t0 = tc * TC
            P.dma("sync", s3f[:, :], s3_hbm.ap()[0:64, t0:t0 + TC])
            P.op("scalar", "activation", e3[:, t0:t0 + TC], s3f[:, :],
                 func=AF.Exp, bias=den3[:, :],
                 accum_out=red1[:, :])
            if tc == 0:
                P.op("vector", "tensor_copy", den3[:, :].rearrange(
                    "p a -> p a") if False else red2[0:1, 0:1], red1[0:1, 0:1])
            if tc == 0:
                P.op("vector", "tensor_copy", scw[:, 0:1], red1[:, :])
            else:
                P.op("vector", "tensor_add", scw[:, 0:1], scw[:, 0:1],
                     red1[:, :])
        P.op("vector", "reciprocal", scw[:, 0:1], scw[:, 0:1])
        for c in range(32):
            emit_transpose(P, tp_bs[:, 0:64], e3[:, c * 128:(c + 1) * 128],
                           64)
            P.op("vector", "tensor_copy", et[:, c * 64:(c + 1) * 64],
                 tp_bs[:, 0:64])
            emit_transpose(P, tp_bs[:, 0:64],
                           vfh[:, c * 128:(c + 1) * 128], 64)
            P.op("vector", "tensor_copy", vtk[:, c * 64:(c + 1) * 64],
                 tp_bs[:, 0:64])
        for c in range(32):
            P.op("tensor", "matmul", sv_ps[:, :], et[:, c * 64:(c + 1) * 64],
                 vtk[:, c * 64:(c + 1) * 64], start=(c == 0), stop=(c == 31))
        P.op("vector", "tensor_scalar", scw[:, 1:2].rearrange("p a -> p a")
             if False else svb[:, :], sv_ps[:, :], scw[:, 0:1], None,
             op0=ALU.mult)
        emit_transpose(P, tp_bs[64:128, 0:64], zb[:, :], 64)
        P.op("vector", "tensor_copy", zT[:, :], tp_bs[64:128, 0:64])
        P.op("tensor", "matmul", zv_ps[:, :], zT[:, :], svb[:, :],
             start=True, stop=True)
        P.op("vector", "tensor_copy", zvb[:, :], zv_ps[:, :])
        for tc in range(NTC):
            t0 = tc * TC
            P.op("tensor", "matmul", e1_ps[:, :], klh[:, :],
                 qfh[:, t0:t0 + TC], start=True, stop=True)
            P.op("scalar", "activation", e1[:, :], e1_ps[:, :],
                 func=AF.Exp, bias=mhat[0:64, h:h + 1])
            P.op("tensor", "matmul", d1_ps[:, :], ones_bf[0][0:64, 0:1],
                 e1[:, :], start=True, stop=True, waits=[ones_bf[1]])
            P.op("vector", "reciprocal", d1[:, :], d1_ps[:, :])
            P.op("tensor", "matmul", r1_ps[:, :], ones1f[0:1, 0:64],
                 d1[:, :], start=True, stop=True)
            P.op("vector", "tensor_copy", r1[:, :], r1_ps[:, :])
            P.op("tensor", "matmul", o_ps[:, :], zvb[:, :], e1[:, :],
                 start=True, stop=True)
            P.op("vector", "tensor_mul", osb[:, :], o_ps[:, :], r1[:, :])
            P.op("vector", "tensor_copy", ofh[:, t0:t0 + TC], osb[:, :])
        P.dma("gpsimd", of_hbm.ap()[h * 64:(h + 1) * 64, :], ofh[:, :])
    P.barrier()

    st.close()
    st = ExitStack()
    bo = emit_row_to_pp(P, st, "l1a_bo", w["bo"], D)
    ofsl = [P.sbuf(st, "l1a_ofsl%d" % i, [128, TC], BF16) for i in range(8)]
    ho_ps = P.psum(st, "l1a_hops", [128, TC])
    xre = P.sbuf(st, "l1a_xre", [128, TC], F32)
    xo = P.sbuf(st, "l1a_xo", [128, TC], F32)
    for tc in range(NTC):
        t0 = tc * TC
        for kc in range(NDC):
            P.dma("sync", ofsl[kc][:, :],
                  of_hbm.ap()[kc * 128:(kc + 1) * 128, t0:t0 + TC])
        for dc in range(NDC):
            wp = wpan[dc % 2]
            load_panel(P, wp, w["Wo"], dc * 128, 128, NDC)
            for kc in range(NDC):
                P.op("tensor", "matmul", ho_ps[:, :],
                     wp[:, kc * 128:(kc + 1) * 128],
                     ofsl[kc][:, :], start=(kc == 0), stop=(kc == NDC - 1))
            P.dma("sync", xre[:, :], x_in[dc * 128:(dc + 1) * 128,
                                          t0:t0 + TC])
            P.op("vector", "tensor_scalar", xo[:, :], ho_ps[:, :],
                 bo[:, dc:dc + 1], None, op0=ALU.add)
            P.op("vector", "tensor_add", xo[:, :], xo[:, :], xre[:, :])
            P.dma("gpsimd", x_out[dc * 128:(dc + 1) * 128, t0:t0 + TC],
                  xo[:, :])
    P.barrier()
    st.close()


def build_l1_attn_launch():
    P = Prog("l1attn")
    x_in = P.dram("x_in", [D, T], F32, kind="ExternalInput").ap()
    x_out = P.dram("x_out", [D, T], F32, kind="ExternalOutput").ap()
    w = {
        "Wk": P.dram("wWk", [D, D], F32, kind="ExternalInput").ap(),
        "Wq": P.dram("wWq", [D, D], F32, kind="ExternalInput").ap(),
        "Wv": P.dram("wWv", [D, D], F32, kind="ExternalInput").ap(),
        "Wo": P.dram("wWo", [D, D], F32, kind="ExternalInput").ap(),
        "bo": P.dram("wbo", [1, D], F32, kind="ExternalInput").ap(),
    }
    bs = ExitStack()
    emit_l1_attn(P, bs, x_in, x_out, w)
    bs.close()
    nc = P.assemble()
    return P, nc



def emit_l0_attn(P, bs, x_in, x_out, w):
    ones_bf = P.sbuf_const("ones128_bf",
                           np.ones((128, 128), dtype=np.float32)
                           .astype(_bf16np()))
    DI = 2048
    NDI = DI // 128
    u_hbm = P.dram("l0a_u", [DI, T], F32)
    ubf_hbm = P.dram("l0a_ubf", [DI, T], BF16)
    res_hbm = P.dram("l0a_res", [DI, T], F32)
    dlt_hbm = P.dram("l0a_dlt", [DI, T], F32)
    du_hbm = P.dram("l0a_du", [DI, T], F32)
    y_hbm = P.dram("l0a_y", [DI, T], BF16)

    st = ExitStack()
    XB, xb_toks = emit_load_xb(P, st, x_in, name="l0a_xb")
    lnstack = ExitStack()
    emit_layernorm_inplace(P, lnstack, XB, xb_toks, ones_bf,
                           eps=1e-5, tag="l0a_ln")
    P.barrier()
    lnstack.close()
    gpp = emit_row_to_pp(P, st, "l0a_nw", w["norm_w"], D)
    r1stack = ExitStack()
    emit_rmsnorm_inplace(P, r1stack, XB, ones_bf, 1e-5, gpp, "l0a_r1")
    P.barrier()
    r1stack.close()

    wpan = [P.sbuf(st, "l0a_wpan%d" % i, [128, 1024], BF16)
            for i in range(2)]
    q_ps = P.psum(st, "l0a_qps", [128, TC])
    cw_pp = P.sbuf(st, "l0a_cw", [128, 4], F32)
    cb_pp = P.sbuf(st, "l0a_cb", [128, 1], F32)
    upad = P.sbuf(st, "l0a_upad", [128, T + 3], F32)
    ucv = P.sbuf(st, "l0a_ucv", [128, T], F32)
    ubf = P.sbuf(st, "l0a_ubf_sb", [128, T], BF16)
    rsb = P.sbuf(st, "l0a_rsb", [128, TC], F32)
    for rt in range(NDI):
        P.dma("sync", cw_pp[:, :], w["conv_w"][rt * 128:(rt + 1) * 128, :])
        P.dma("sync", cb_pp[:, :],
              w["conv_b"][rt * 128:(rt + 1) * 128, 0:1])
        P.op("vector", "memset", upad[:, 0:3], 0.0)
        wp = wpan[rt % 2]
        load_panel(P, wp, w["in_proj_w"], rt * 128, 128, NDC)
        for tc in range(NTC):
            t0 = tc * TC
            for kc in range(NDC):
                P.op("tensor", "matmul", q_ps[:, :],
                     wp[:, kc * 128:(kc + 1) * 128],
                     XB[:, kc * T + t0:kc * T + t0 + TC],
                     start=(kc == 0), stop=(kc == NDC - 1))
            P.op("scalar", "activation", upad[:, 3 + t0:3 + t0 + TC],
                 q_ps[:, :], func=AF.Identity)
        P.op("vector", "tensor_scalar", ucv[:, :], upad[:, 0:T],
             cw_pp[:, 0:1], None, op0=ALU.mult)
        for kk in range(1, 4):
            P.op("vector", "scalar_tensor_tensor", ucv[:, :],
                 upad[:, kk:kk + T], cw_pp[:, kk:kk + 1], ucv[:, :],
                 op0=ALU.mult, op1=ALU.add)
        P.op("scalar", "activation", ucv[:, :], ucv[:, :], func=AF.Silu,
             bias=cb_pp[:, :])
        P.dma("gpsimd", u_hbm.ap()[rt * 128:(rt + 1) * 128, :], ucv[:, :])
        P.op("vector", "tensor_copy", ubf[:, :], ucv[:, :])
        P.dma("gpsimd", ubf_hbm.ap()[rt * 128:(rt + 1) * 128, :],
              ubf[:, :])
    for rt in range(NDI):
        wp = wpan[rt % 2]
        load_panel(P, wp, w["in_proj_w"], 2048 + rt * 128, 128, NDC)
        for tc in range(NTC):
            t0 = tc * TC
            for kc in range(NDC):
                P.op("tensor", "matmul", q_ps[:, :],
                     wp[:, kc * 128:(kc + 1) * 128],
                     XB[:, kc * T + t0:kc * T + t0 + TC],
                     start=(kc == 0), stop=(kc == NDC - 1))
            P.op("vector", "tensor_copy", rsb[:, :], q_ps[:, :])
            P.dma("gpsimd", res_hbm.ap()[rt * 128:(rt + 1) * 128,
                                         t0:t0 + TC], rsb[:, :])
    P.barrier()
    st.close()

    st = ExitStack()
    dtbf = P.sbuf(bs, "l0a_dtbf", [64, T], BF16)
    Bbf = P.sbuf(bs, "l0a_Bbf", [16, T], BF16)
    Cbf = P.sbuf(bs, "l0a_Cbf", [16, T], BF16)
    A_sb = P.sbuf(bs, "l0a_A", [128, 16 * 16], F32)
    D_pp = P.sbuf(bs, "l0a_D", [128, 16], F32)
    xpw = [P.sbuf(st, "l0a_xpw%d" % i, [128, 96], BF16) for i in range(16)]
    for kc in range(16):
        P.dma("gpsimd", xpw[kc][:, :],
              w["x_proj_w"][kc * 128:(kc + 1) * 128, :])
    usl = [P.sbuf(st, "l0a_usl%d" % i, [128, TC], BF16) for i in range(16)]
    dt_ps = P.psum(st, "l0a_dtps", [64, TC])
    b_ps = P.psum(st, "l0a_bps", [16, TC])
    c_ps = P.psum(st, "l0a_cps", [16, TC])
    for tc in range(NTC):
        t0 = tc * TC
        for kc in range(16):
            P.dma("sync", usl[kc][:, :],
                  ubf_hbm.ap()[kc * 128:(kc + 1) * 128, t0:t0 + TC])
            P.op("tensor", "matmul", dt_ps[:, :], xpw[kc][:, 0:64],
                 usl[kc][:, :], start=(kc == 0), stop=(kc == 15))
            P.op("tensor", "matmul", b_ps[:, :], xpw[kc][:, 64:80],
                 usl[kc][:, :], start=(kc == 0), stop=(kc == 15))
            P.op("tensor", "matmul", c_ps[:, :], xpw[kc][:, 80:96],
                 usl[kc][:, :], start=(kc == 0), stop=(kc == 15))
        P.op("vector", "tensor_copy", dtbf[:, t0:t0 + TC], dt_ps[:, :])
        P.op("vector", "tensor_copy", Bbf[:, t0:t0 + TC], b_ps[:, :])
        P.op("vector", "tensor_copy", Cbf[:, t0:t0 + TC], c_ps[:, :])
    dpw = [P.sbuf(st, "l0a_dpw%d" % i, [64, 128], BF16) for i in range(2)]
    dpb = P.sbuf(st, "l0a_dpb", [128, 1], F32)
    d_ps = P.psum(st, "l0a_dps", [128, TC])
    sabs = P.sbuf(st, "l0a_sabs", [128, TC], F32)
    srel = P.sbuf(st, "l0a_srel", [128, TC], F32)
    sdel = P.sbuf(st, "l0a_sdel", [128, TC], F32)
    udu = P.sbuf(st, "l0a_udu", [128, TC], F32)
    for rt in range(NDI):
        i = rt % 2
        P.dma("gpsimd", dpw[i][:, :],
              w["dt_proj_w"][:, rt * 128:(rt + 1) * 128])
        P.dma("sync", dpb[:, :],
              w["dt_proj_b"][rt * 128:(rt + 1) * 128, 0:1])
        for tc in range(NTC):
            t0 = tc * TC
            P.op("tensor", "matmul", d_ps[:, :], dpw[i][:, :],
                 dtbf[:, t0:t0 + TC], start=True, stop=True)
            P.op("scalar", "activation", sabs[:, :], d_ps[:, :],
                 func=AF.Abs, bias=dpb[:, :])
            P.op("scalar", "activation", sabs[:, :], sabs[:, :],
                 func=AF.Exp, scale=-1.0)
            P.op("vector", "tensor_scalar_add", sabs[:, :], sabs[:, :],
                 1.0)
            P.op("scalar", "activation", sabs[:, :], sabs[:, :],
                 func=AF.Ln)
            P.op("vector", "tensor_scalar", srel[:, :], d_ps[:, :],
                 dpb[:, :], 0.0, op0=ALU.add, op1=ALU.max)
            P.op("vector", "tensor_add", sdel[:, :], sabs[:, :],
                 srel[:, :])
            P.dma("gpsimd", dlt_hbm.ap()[rt * 128:(rt + 1) * 128,
                                         t0:t0 + TC], sdel[:, :])
            P.dma("sync", udu[:, :],
                  u_hbm.ap()[rt * 128:(rt + 1) * 128, t0:t0 + TC])
            P.op("vector", "tensor_mul", udu[:, :], udu[:, :], sdel[:, :])
            P.dma("gpsimd", du_hbm.ap()[rt * 128:(rt + 1) * 128,
                                        t0:t0 + TC], udu[:, :])
    at = P.sbuf(st, "l0a_at", [128, 16], F32)
    for dblk in range(16):
        P.dma("sync", at[:, :],
              w["A_log"][dblk * 128:(dblk + 1) * 128, :])
        P.op("scalar", "activation", at[:, :], at[:, :], func=AF.Exp)
        P.op("vector", "tensor_scalar_mul",
             A_sb[:, dblk * 16:(dblk + 1) * 16], at[:, :], -1.0)
    for dblk in range(16):
        P.dma("sync", D_pp[:, dblk:dblk + 1],
              w["Dv"][dblk * 128:(dblk + 1) * 128, 0:1])
    P.barrier()
    st.close()

    st = ExitStack()
    sels = []
    for s in range(16):
        m = np.zeros((16, 128), dtype=np.float32)
        m[s, :] = 1.0
        sels.append(P.sbuf_const("sel_s%d" % s, m.astype(_bf16np())))
    state = P.sbuf(st, "l0a_state", [128, 256], F32)
    P.op("vector", "memset", state[:, :], 0.0)
    hbuf = P.sbuf(st, "l0a_hbuf", [128, 16 * TC], F32)
    dlt_t = [P.sbuf(st, "l0a_dlt%d" % i, [128, TC], F32) for i in range(2)]
    du_t = [P.sbuf(st, "l0a_du%d" % i, [128, TC], F32) for i in range(2)]
    B_ps = [P.psum(st, "l0a_Bps%d" % i, [128, TC]) for i in range(2)]
    C_ps = [P.psum(st, "l0a_Cps%d" % i, [128, TC]) for i in range(2)]
    dA = [P.sbuf(st, "l0a_dA%d" % i, [128, TC], F32) for i in range(2)]
    dBu = [P.sbuf(st, "l0a_dBu%d" % i, [128, TC], F32) for i in range(2)]
    hC = [P.sbuf(st, "l0a_hC%d" % i, [128, TC], F32) for i in range(2)]
    y_acc = P.sbuf(st, "l0a_yacc", [128, TC], F32)
    u_t = P.sbuf(st, "l0a_ut", [128, TC], F32)
    res_t = P.sbuf(st, "l0a_rest", [128, TC], F32)
    sil = P.sbuf(st, "l0a_sil", [128, TC], F32)
    ybf = P.sbuf(st, "l0a_ybf", [128, TC], BF16)
    for tc in range(NTC):
        t0 = tc * TC
        for dblk in range(16):
            i = dblk % 2
            P.dma("sync", dlt_t[i][:, :],
                  dlt_hbm.ap()[dblk * 128:(dblk + 1) * 128, t0:t0 + TC])
            P.dma("sync", du_t[i][:, :],
                  du_hbm.ap()[dblk * 128:(dblk + 1) * 128, t0:t0 + TC])
            for s in range(16):
                j = s % 2
                P.op("tensor", "matmul", B_ps[j][:, :], sels[s][0][:, :],
                     Bbf[:, t0:t0 + TC], start=True, stop=True,
                     waits=[sels[s][1]])
                P.op("tensor", "matmul", C_ps[j][:, :], sels[s][0][:, :],
                     Cbf[:, t0:t0 + TC], start=True, stop=True)
                P.op("scalar", "activation", dA[j][:, :], dlt_t[i][:, :],
                     func=AF.Exp,
                     scale=A_sb[:, dblk * 16 + s:dblk * 16 + s + 1])
                P.op("vector", "tensor_mul", dBu[j][:, :], du_t[i][:, :],
                     B_ps[j][:, :])
                P.op("vector", "tensor_tensor_scan",
                     hbuf[:, s * TC:(s + 1) * TC], dA[j][:, :],
                     dBu[j][:, :],
                     state[:, dblk * 16 + s:dblk * 16 + s + 1],
                     op0=ALU.mult, op1=ALU.add)
                P.op("vector", "tensor_mul", hC[j][:, :],
                     hbuf[:, s * TC:(s + 1) * TC], C_ps[j][:, :])
                if s == 0:
                    P.op("gpsimd", "tensor_copy", y_acc[:, :], hC[j][:, :])
                else:
                    P.op("gpsimd", "tensor_add", y_acc[:, :], y_acc[:, :],
                         hC[j][:, :])
            P.op("vector", "tensor_copy",
                 state[:, dblk * 16:(dblk + 1) * 16],
                 hbuf[:, :].rearrange("p (s t) -> p s t", t=TC)
                 [:, :, TC - 1:TC])
            P.dma("sync", u_t[:, :],
                  u_hbm.ap()[dblk * 128:(dblk + 1) * 128, t0:t0 + TC])
            P.dma("sync", res_t[:, :],
                  res_hbm.ap()[dblk * 128:(dblk + 1) * 128, t0:t0 + TC])
            P.op("vector", "tensor_scalar", u_t[:, :], u_t[:, :],
                 D_pp[:, dblk:dblk + 1], None, op0=ALU.mult)
            P.op("vector", "tensor_add", y_acc[:, :], y_acc[:, :],
                 u_t[:, :])
            P.op("scalar", "activation", sil[:, :], res_t[:, :],
                 func=AF.Silu)
            P.op("vector", "tensor_mul", y_acc[:, :], y_acc[:, :],
                 sil[:, :])
            P.op("vector", "tensor_copy", ybf[:, :], y_acc[:, :])
            P.dma("gpsimd", y_hbm.ap()[dblk * 128:(dblk + 1) * 128,
                                       t0:t0 + TC], ybf[:, :])
    P.barrier()
    st.close()

    st = ExitStack()
    wpan4 = [P.sbuf(st, "l0a_w4p%d" % i, [128, 16 * 128], BF16)
             for i in range(2)]
    ysl = [P.sbuf(st, "l0a_ysl%d" % i, [128, TC], BF16) for i in range(16)]
    o_ps = P.psum(st, "l0a_ops", [128, TC])
    xre = P.sbuf(st, "l0a_xre", [128, TC], F32)
    xo = P.sbuf(st, "l0a_xo", [128, TC], F32)
    for tc in range(NTC):
        t0 = tc * TC
        for kc in range(16):
            P.dma("sync", ysl[kc][:, :],
                  y_hbm.ap()[kc * 128:(kc + 1) * 128, t0:t0 + TC])
        for dc in range(NDC):
            wp = wpan4[dc % 2]
            load_panel(P, wp, w["out_proj_w"], dc * 128, 128, 16)
            for kc in range(16):
                P.op("tensor", "matmul", o_ps[:, :],
                     wp[:, kc * 128:(kc + 1) * 128],
                     ysl[kc][:, :], start=(kc == 0), stop=(kc == 15))
            P.dma("sync", xre[:, :], x_in[dc * 128:(dc + 1) * 128,
                                          t0:t0 + TC])
            P.op("vector", "tensor_add", xo[:, :], o_ps[:, :], xre[:, :])
            P.dma("gpsimd", x_out[dc * 128:(dc + 1) * 128, t0:t0 + TC],
                  xo[:, :])
    P.barrier()
    st.close()


def build_l0_attn_launch():
    P = Prog("l0attn")
    x_in = P.dram("x_in", [D, T], F32, kind="ExternalInput").ap()
    x_out = P.dram("x_out", [D, T], F32, kind="ExternalOutput").ap()
    w = {
        "norm_w": P.dram("wnw", [1, D], F32, kind="ExternalInput").ap(),
        "in_proj_w": P.dram("wip", [D, 4096], F32,
                            kind="ExternalInput").ap(),
        "conv_w": P.dram("wcw", [2048, 4], F32, kind="ExternalInput").ap(),
        "conv_b": P.dram("wcb", [2048, 1], F32, kind="ExternalInput").ap(),
        "x_proj_w": P.dram("wxp", [2048, 96], F32,
                           kind="ExternalInput").ap(),
        "dt_proj_w": P.dram("wdp", [64, 2048], F32,
                            kind="ExternalInput").ap(),
        "dt_proj_b": P.dram("wdpb", [2048, 1], F32,
                            kind="ExternalInput").ap(),
        "A_log": P.dram("wAl", [2048, 16], F32, kind="ExternalInput").ap(),
        "Dv": P.dram("wDv", [2048, 1], F32, kind="ExternalInput").ap(),
        "out_proj_w": P.dram("wop", [2048, D], F32,
                             kind="ExternalInput").ap(),
    }
    bs = ExitStack()
    emit_l0_attn(P, bs, x_in, x_out, w)
    bs.close()
    nc = P.assemble()
    return P, nc



def build_transpose_launch(direction):
    P = Prog("tr_" + direction)
    if direction == "in":
        src = P.dram("x_tok", [T, D], F32, kind="ExternalInput").ap()
        dst = P.dram("x_feat", [D, T], F32, kind="ExternalOutput").ap()
        n_p, n_f = T, D
    else:
        src = P.dram("x_feat", [D, T], F32, kind="ExternalInput").ap()
        dst = P.dram("x_tok", [T, D], F32, kind="ExternalOutput").ap()
        n_p, n_f = D, T
    st = ExitStack()
    tile = [P.sbuf(st, "tr_t%d" % i, [128, n_f], F32) for i in range(2)]
    tps = P.psum(st, "tr_ps", [128, 128])
    osb = [P.sbuf(st, "tr_o%d" % i, [128, 128], F32) for i in range(4)]
    for tt in range(n_p // 128):
        i = tt % 2
        P.dma("sync", tile[i][:, :], src[tt * 128:(tt + 1) * 128, :])
        for c in range(n_f // 128):
            j = c % 4
            emit_transpose(P, tps[:, :], tile[i][:, c * 128:(c + 1) * 128],
                           128, bf16=False)
            P.op("vector", "tensor_copy", osb[j][:, :], tps[:, :])
            P.dma("gpsimd", dst[c * 128:(c + 1) * 128,
                                tt * 128:(tt + 1) * 128], osb[j][:, :])
    P.barrier()
    st.close()
    nc = P.assemble()
    return P, nc


def build_l0_ffn_mode(mode):
    P = Prog("l0ffn_" + mode)
    x_in = P.dram("x_in", [D, T], F32, kind="ExternalInput").ap()
    w = {
        "g1": P.dram("wg1", [1, D], F32, kind="ExternalInput").ap(),
        "g2": P.dram("wg2", [1, D], F32, kind="ExternalInput").ap(),
        "wq": P.dram("wwq", [D, 1024], F32, kind="ExternalInput").ap(),
        "wk": P.dram("wwk", [D, 256], F32, kind="ExternalInput").ap(),
        "wv": P.dram("wwv", [D, 256], F32, kind="ExternalInput").ap(),
        "wo": P.dram("wwo", [1024, D], F32, kind="ExternalInput").ap(),
        "router_w": P.dram("wrw", [D, 8], F32, kind="ExternalInput").ap(),
        "router_b": P.dram("wrb", [1, 8], F32, kind="ExternalInput").ap(),
        "w1": P.dram("ww1", [8 * D, 2048], F32, kind="ExternalInput").ap(),
        "w2": P.dram("ww2", [8 * D, 2048], F32, kind="ExternalInput").ap(),
        "w3": P.dram("ww3", [8 * 2048, D], F32, kind="ExternalInput").ap(),
    }
    bs = ExitStack()
    if mode == "counts":
        h_ext = P.dram("h_ext", [D, T], F32, kind="ExternalOutput").ap()
        counts_out = P.dram("counts_out", [8, 2], F32,
                            kind="ExternalOutput").ap()
        emit_l0_ffn(P, bs, x_in, None, w, None, None,
                    counts_out=counts_out, mode="counts", h_ext=h_ext)
    else:
        h_ext = P.dram("h_ext", [D, T], F32, kind="ExternalInput").ap()
        x_out = P.dram("x_out", [D, T], F32, kind="ExternalOutput").ap()
        counts_all = P.dram("counts_all", [8, 16], F32,
                            kind="ExternalInput").ap()
        prev_mask = P.dram("prev_mask", [8, 1], F32,
                           kind="ExternalInput").ap()
        emit_l0_ffn(P, bs, x_in, x_out, w, counts_all, prev_mask,
                    counts_out=None, mode="apply", h_ext=h_ext)
    bs.close()
    nc = P.assemble()
    return P, nc


def build_l1_ffn_mode(mode):
    P = Prog("l1ffn_" + mode)
    x_in = P.dram("x_in", [D, T], F32, kind="ExternalInput").ap()
    w = {
        "Wg": P.dram("wWg", [D, 8], F32, kind="ExternalInput").ap(),
        "W1": P.dram("wW1", [8 * D, 4096], F32, kind="ExternalInput").ap(),
        "b1": P.dram("wb1", [8, 4096], F32, kind="ExternalInput").ap(),
        "W2": P.dram("wW2", [8 * 4096, D], F32, kind="ExternalInput").ap(),
        "b2": P.dram("wb2", [8, 1024], F32, kind="ExternalInput").ap(),
    }
    bs = ExitStack()
    if mode == "counts":
        counts_out = P.dram("counts_out", [8, 2], F32,
                            kind="ExternalOutput").ap()
        emit_l1_ffn(P, bs, x_in, None, w, None, None,
                    counts_out=counts_out, mode="counts")
    else:
        x_out = P.dram("x_out", [D, T], F32, kind="ExternalOutput").ap()
        counts_all = P.dram("counts_all", [8, 16], F32,
                            kind="ExternalInput").ap()
        prev_mask = P.dram("prev_mask", [8, 1], F32,
                           kind="ExternalInput").ap()
        emit_l1_ffn(P, bs, x_in, x_out, w, counts_all, prev_mask,
                    counts_out=None, mode="apply")
    bs.close()
    nc = P.assemble()
    return P, nc



_BUILD_CACHE = {}


def _get(name, builder):
    if name not in _BUILD_CACHE:
        _BUILD_CACHE[name] = builder()
    return _BUILD_CACHE[name]


def _run(P, nc, per_core, shared):
    in_maps = []
    for c in range(8):
        m = dict(shared)
        m.update(per_core[c])
        m.update(P.extra_inputs)
        in_maps.append(m)
    res = run_bass_kernel_spmd(nc, in_maps, core_ids=list(range(8)))
    return res.results


def _counts_all(couts):
    ca = np.zeros((8, 16), np.float32)
    for c in range(8):
        ca[c, 0:8] = couts[c][:, 0]
        ca[c, 8:16] = couts[c][:, 1]
    return ca


_PREV_MASKS = [np.asarray([[1.0] if i < c else [0.0] for i in range(8)],
                          np.float32) for c in range(8)]


def kernel(**inputs):
    ii = {k: np.ascontiguousarray(np.asarray(v)) for k, v in inputs.items()}
    x0 = ii["L0_attn_x"].astype(np.float32)

    P, nc = _get("tr_in", lambda: build_transpose_launch("in"))
    r = _run(P, nc, [{"x_tok": x0[c]} for c in range(8)], {})
    xT = [r[c]["x_feat"] for c in range(8)]

    P, nc = _get("l0a", build_l0_attn_launch)
    sh = {"wnw": ii["L0_attn_norm_w"].reshape(1, D),
          "wip": ii["L0_attn_in_proj_w"],
          "wcw": ii["L0_attn_conv_w"],
          "wcb": ii["L0_attn_conv_b"].reshape(-1, 1),
          "wxp": ii["L0_attn_x_proj_w"],
          "wdp": ii["L0_attn_dt_proj_w"],
          "wdpb": ii["L0_attn_dt_proj_b"].reshape(-1, 1),
          "wAl": ii["L0_attn_A_log"],
          "wDv": ii["L0_attn_D"].reshape(-1, 1),
          "wop": ii["L0_attn_out_proj_w"]}
    r = _run(P, nc, [{"x_in": xT[c]} for c in range(8)], sh)
    x1 = [r[c]["x_out"] for c in range(8)]

    sh = {"wg1": ii["L0_ffn_g1"].reshape(1, D),
          "wg2": ii["L0_ffn_g2"].reshape(1, D),
          "wwq": ii["L0_ffn_wq"], "wwk": ii["L0_ffn_wk"],
          "wwv": ii["L0_ffn_wv"], "wwo": ii["L0_ffn_wo"],
          "wrw": ii["L0_ffn_router_w"],
          "wrb": ii["L0_ffn_router_b"].reshape(1, 8),
          "ww1": ii["L0_ffn_w1"].reshape(8 * D, 2048),
          "ww2": ii["L0_ffn_w2"].reshape(8 * D, 2048),
          "ww3": ii["L0_ffn_w3"].reshape(8 * 2048, D)}
    P, nc = _get("l0f_c", lambda: build_l0_ffn_mode("counts"))
    r = _run(P, nc, [{"x_in": x1[c]} for c in range(8)], sh)
    ca = _counts_all([r[c]["counts_out"] for c in range(8)])
    hs = [r[c]["h_ext"] for c in range(8)]
    P, nc = _get("l0f_a", lambda: build_l0_ffn_mode("apply"))
    r = _run(P, nc, [{"x_in": x1[c], "h_ext": hs[c], "counts_all": ca,
                      "prev_mask": _PREV_MASKS[c]} for c in range(8)], sh)
    x2 = [r[c]["x_out"] for c in range(8)]

    P, nc = _get("l1a", build_l1_attn_launch)
    sh = {"wWk": ii["L1_attn_Wk"], "wWq": ii["L1_attn_Wq"],
          "wWv": ii["L1_attn_Wv"], "wWo": ii["L1_attn_Wo"],
          "wbo": ii["L1_attn_bo"].reshape(1, D)}
    r = _run(P, nc, [{"x_in": x2[c]} for c in range(8)], sh)
    x3 = [r[c]["x_out"] for c in range(8)]

    sh = {"wWg": ii["L1_ffn_Wg"],
          "wW1": ii["L1_ffn_W1"].reshape(8 * D, 4096),
          "wb1": ii["L1_ffn_b1"],
          "wW2": ii["L1_ffn_W2"].reshape(8 * 4096, D),
          "wb2": ii["L1_ffn_b2"]}
    P, nc = _get("l1f_c", lambda: build_l1_ffn_mode("counts"))
    r = _run(P, nc, [{"x_in": x3[c]} for c in range(8)], sh)
    ca = _counts_all([r[c]["counts_out"] for c in range(8)])
    P, nc = _get("l1f_a", lambda: build_l1_ffn_mode("apply"))
    r = _run(P, nc, [{"x_in": x3[c], "counts_all": ca,
                      "prev_mask": _PREV_MASKS[c]} for c in range(8)], sh)
    x4 = [r[c]["x_out"] for c in range(8)]

    P, nc = _get("l2a", build_l2_attn_launch)
    sh = {"wWq": ii["L2_attn_Wq"], "wWk": ii["L2_attn_Wk"],
          "wWv": ii["L2_attn_Wv"], "wWo": ii["L2_attn_Wo"]}
    r = _run(P, nc, [{"x_in": x4[c]} for c in range(8)], sh)
    x5 = [r[c]["x_out"] for c in range(8)]

    P, nc = _get("l2f", build_l2_ffn_launch)
    sh = {"wk0": ii["L2_ffn_k0"].view(np.uint8),
          "wk1": ii["L2_ffn_k1"].view(np.uint8),
          "wk2": ii["L2_ffn_k2"].view(np.uint8),
          "ws0": ii["L2_ffn_s0"].reshape(1, 1),
          "ws1": ii["L2_ffn_s1"].reshape(1, 1),
          "ws2": ii["L2_ffn_s2"].reshape(1, 1)}
    r = _run(P, nc, [{"x_in": x5[c]} for c in range(8)], sh)
    x6 = [r[c]["x_out"] for c in range(8)]

    P, nc = _get("l3a", build_l3_attn_launch)
    sh = {"wqkv": ii["L3_attn_w_qkv"],
          "wbqkv": ii["L3_attn_b_qkv"].reshape(1, 3072),
          "wout": ii["L3_attn_w_out"],
          "wbout": ii["L3_attn_b_out"].reshape(1, D),
          "wproj": ii["L3_attn_proj"].reshape(16 * 64, 64)}
    r = _run(P, nc, [{"x_in": x6[c]} for c in range(8)], sh)
    x7 = [r[c]["x_out"] for c in range(8)]

    P, nc = _get("l3f", build_l3_ffn_launch)
    sh = {"ww1": ii["L3_ffn_w1"].reshape(8 * D, 512),
          "wb1": ii["L3_ffn_b1"],
          "ww2": ii["L3_ffn_w2"].reshape(8 * 512, D),
          "wb2": ii["L3_ffn_b2"],
          "wcen": ii["L3_ffn_centroids"],
          "wrt": ii["L3_ffn_w_route"]}
    r = _run(P, nc, [{"x_in": x7[c]} for c in range(8)], sh)
    x8 = [r[c]["x_out"] for c in range(8)]

    P, nc = _get("tr_out", lambda: build_transpose_launch("out"))
    r = _run(P, nc, [{"x_feat": x8[c]} for c in range(8)], {})
    return np.stack([r[c]["x_tok"] for c in range(8)], 0)
